# version 25
# baseline (speedup 1.0000x reference)
import math
from contextlib import ExitStack

import numpy as np
import ml_dtypes

import concourse.bass as bass
import concourse.mybir as mybir
from concourse.bass_utils import run_bass_kernel_spmd

F32 = mybir.dt.float32
BF16 = mybir.dt.bfloat16
I32 = mybir.dt.int32
AF = mybir.ActivationFunctionType
ALU = mybir.AluOpType
AX = mybir.AxisListType

B = 4
T = 4096
TO = 2048
D = 1024
NM = D // 128
FF = 4096
NF = FF // 128
NE = 8
CHK = 512
NCHK = T // CHK
LN_EPS = 1e-5

BF = ml_dtypes.bfloat16

ENGINES = ("sync", "gp", "pe", "act", "dve")


class Prog:

    def __init__(self, nc, ctx):
        self.nc = nc
        self.q = {e: [] for e in ENGINES}
        self.sem = {e: ctx.enter_context(nc.semaphore(f"sem_{e}")) for e in ENGINES}
        self.cnt = {e: 0 for e in ENGINES}
        self.ssem = {}
        self.scnt = {}
        self._ctx = ctx
        self.waited = {e: {} for e in ENGINES}
        self.holder = {}

    def on(self, eng, fn):
        self.q[eng].append(fn)

    def tok(self, eng):
        return ("c", eng, self.cnt[eng])

    def dtok(self, stream):
        return ("d", stream, self.scnt.get(stream, 0))

    def _stream_sem(self, stream):
        if stream not in self.ssem:
            self.ssem[stream] = self._ctx.enter_context(
                self.nc.semaphore(f"ds_{stream}"))
            self.scnt[stream] = 0
        return self.ssem[stream]

    def wait(self, eng, token):
        if token is None:
            return
        kind, peng, val = token
        if kind == "c" and peng == eng:
            return
        key = (kind, peng)
        if self.waited[eng].get(key, 0) >= val:
            return
        self.waited[eng][key] = val
        sem = self.sem[peng] if kind == "c" else self._stream_sem(peng)
        self.on(eng, lambda e, sem=sem, val=val: e.wait_ge(sem, val))

    def waits(self, eng, tokens):
        for t in tokens:
            self.wait(eng, t)

    SERIAL = ("dve", "act", "gp")

    def step(self, eng, fn, inc=0):
        if eng in self.SERIAL:
            sem = self.sem[eng]
            prev = self.cnt[eng]
            if prev:
                self.on(eng, lambda e, sem=sem, prev=prev: e.wait_ge(sem, prev))
            self.cnt[eng] += 1
            val = self.cnt[eng]

            def wrapped(e, fn=fn, sem=sem):
                ins = fn(e)
                ins.then_inc(sem, 1)
            self.on(eng, wrapped)
            return ("c", eng, val)
        if inc:
            self.cnt[eng] += inc
            val = self.cnt[eng]
            sem = self.sem[eng]

            def wrapped(e, fn=fn, sem=sem, inc=inc):
                ins = fn(e)
                ins.then_inc(sem, inc)
            self.on(eng, wrapped)
            return ("c", eng, val)
        self.on(eng, fn)
        return None

    def raw(self, stream, eng, fn, inc):
        sem = self._stream_sem(stream)
        self.scnt[stream] += inc
        self.on(eng, lambda e, fn=fn, sem=sem: fn(e, sem))
        return ("d", stream, self.scnt[stream])

    def par_step(self, eng, fn_even, fn_odd, deps=()):
        self.waits(eng, deps)
        sem = self.sem[eng]
        prev = self.cnt[eng]
        if eng in self.SERIAL and prev:
            self.on(eng, lambda e, sem=sem, prev=prev: e.wait_ge(sem, prev))
        self.cnt[eng] += 1
        val = self.cnt[eng]
        holder = self.holder
        id_key = f"par_{eng}"

        def go(e, fn_even=fn_even, fn_odd=fn_odd, sem=sem, id_key=id_key):
            par = holder[id_key]
            with e.If_eq(par, 0):
                fn_even(e).then_inc(sem, 1)
            with e.Else():
                fn_odd(e).then_inc(sem, 1)
        self.on(eng, go)
        return ("c", eng, val)

    def dma(self, stream, eng, out, in_, deps=()):
        self.waits(eng, deps)
        sem = self._stream_sem(stream)
        self.scnt[stream] += 16
        val = self.scnt[stream]

        def wrapped(e, sem=sem, out=out, in_=in_):
            e.dma_start(out=out, in_=in_).then_inc(sem, 16)
        self.on(eng, wrapped)
        return ("d", stream, val)

    def barrier_all(self):
        toks = [self.tok(e) for e in ENGINES] + \
               [("d", s, self.scnt[s]) for s in self.scnt]
        for e in ENGINES:
            self.waits(e, toks)

    def finalize(self, block):
        def run(name):
            def body(e):
                for fn in self.q[name]:
                    fn(e)
            return body
        block.sync(run("sync"))
        block.gpsimd(run("gp"))
        block.tensor(run("pe"))
        block.scalar(run("act"))
        block.vector(run("dve"))



def emit_ln(P, x0T_ap, xn_dram, bufs, psF, cons, chunks=NCHK):
    xbuf, xsq, xnc, tmp = bufs["xbuf"], bufs["xsq"], bufs["xnc"], bufs["tmp"]
    mu, msq, var, rstd, mur = bufs["rows"]
    ones_c, ones_r = cons["ones_c"], cons["ones_r"]
    ps_stat, ps_stat2, ps_b1, ps_b2 = psF[0][0:1, :], psF[1][0:1, :], psF[2][:, :], psF[3][:, :]

    last_apply = [None, None]
    last_stats = None
    last_bc_cons = None
    last_sq_mm = [None, None]
    stores = []

    for k in range(chunks):
        p = k % 2
        ld = [P.dma(f"lnx{p}", "sync", xbuf[p][m],
                    x0T_ap[m * 128:(m + 1) * 128, k * CHK:(k + 1) * CHK],
                    deps=[last_apply[p]]) for m in range(NM)]
        P.waits("pe", [ld[-1], last_stats])
        for m in range(NM):
            P.step("pe", lambda e, m=m, p=p: e.matmul(
                ps_stat, ones_c[:, :], xbuf[p][m],
                start=(m == 0), stop=(m == NM - 1)),
                inc=(1 if m == NM - 1 else 0))
        t_sum = P.tok("pe")
        for m in range(NM):
            P.waits("act", [ld[-1], last_sq_mm[m % 2]])
            t_sq_m = P.step("act", lambda e, m=m, p=p: e.activation(
                out=xsq[m % 2], in_=xbuf[p][m], func=AF.Square), inc=1)
            P.wait("pe", t_sq_m)
            last_sq_mm[m % 2] = P.step("pe", lambda e, m=m: e.matmul(
                ps_stat2, ones_c[:, :], xsq[m % 2],
                start=(m == 0), stop=(m == NM - 1)), inc=1)
        t_sq = P.tok("pe")
        P.wait("dve", t_sum)
        P.step("dve", lambda e: e.tensor_scalar_mul(out=mu, in0=ps_stat, scalar1=1.0 / D))
        P.wait("dve", t_sq)
        P.step("dve", lambda e: e.tensor_scalar_mul(out=msq, in0=ps_stat2, scalar1=1.0 / D))
        P.step("dve", lambda e: e.tensor_mul(out=var, in0=mu, in1=mu))
        t_var = P.step("dve", lambda e: e.tensor_sub(out=var, in0=msq, in1=var), inc=1)
        last_stats = t_var
        P.wait("act", t_var)
        P.step("act", lambda e: e.activation(out=rstd, in_=var, func=AF.Ln,
                                             bias=cons["eps_b"][0:1, 0:1]))
        t_rstd = P.step("act", lambda e: e.activation(out=rstd, in_=rstd, func=AF.Exp,
                                                      scale=-0.5), inc=1)
        P.wait("dve", t_rstd)
        t_mur = P.step("dve", lambda e: e.tensor_mul(out=mur, in0=mu, in1=rstd), inc=1)
        P.waits("pe", [t_rstd, t_mur, last_bc_cons])
        P.step("pe", lambda e: e.matmul(ps_b1, ones_r[:, :], rstd, start=True, stop=True))
        t_bc = P.step("pe", lambda e: e.matmul(ps_b2, ones_r[:, :], mur, start=True,
                                               stop=True), inc=1)
        rsave, msave = bufs["rstd_all"], bufs["mur_all"]
        kk = k - (0 if k < NCHK // 2 else NCHK // 2)

        def mknop(e):
            return e.engine_nop()
        if k < NCHK // 2:
            P.par_step("dve",
                       lambda e, kk=kk: e.tensor_copy(
                           out=rsave[0:1, kk * CHK:(kk + 1) * CHK], in_=rstd),
                       mknop)
            P.par_step("dve",
                       lambda e, kk=kk: e.tensor_copy(
                           out=msave[0:1, kk * CHK:(kk + 1) * CHK], in_=mur),
                       mknop)
        else:
            P.par_step("dve", mknop,
                       lambda e, kk=kk: e.tensor_copy(
                           out=rsave[0:1, kk * CHK:(kk + 1) * CHK], in_=rstd))
            P.par_step("dve", mknop,
                       lambda e, kk=kk: e.tensor_copy(
                           out=msave[0:1, kk * CHK:(kk + 1) * CHK], in_=mur))
        P.wait("dve", t_bc)
        for m in range(NM):
            P.step("dve", lambda e, m=m, p=p: e.tensor_mul(
                out=tmp[m % 4], in0=xbuf[p][m], in1=ps_b1))
            P.step("dve", lambda e, m=m: e.tensor_sub(
                out=xnc[m], in0=tmp[m % 4], in1=ps_b2))
        t_apply = P.step("dve", lambda e: e.engine_nop(), inc=1)
        last_apply[p] = t_apply
        last_bc_cons = t_apply
        st = [P.dma("lnst", "sync", xn_dram[m * 128:(m + 1) * 128, k * CHK:(k + 1) * CHK],
                    xnc[m], deps=[t_apply]) for m in range(NM)]
        stores += st
        P.waits("dve", [st[-1]])
    return stores



def emit_read_own(P, src_dram, dst_tiles, deps=(), dtype_cast=False, width=TO):
    eng = "gp"
    P.waits(eng, deps)

    def go(e, sem):
        par = P.holder["par"]
        with e.If_eq(par, 0):
            for m in range(NM):
                e.dma_start(out=dst_tiles[m],
                            in_=src_dram[m * 128:(m + 1) * 128, 0:width]
                            ).then_inc(sem, 16)
        with e.Else():
            for m in range(NM):
                e.dma_start(out=dst_tiles[m],
                            in_=src_dram[m * 128:(m + 1) * 128, T - width:T]
                            ).then_inc(sem, 16)
    return P.raw("own", eng, go, inc=16 * NM)


def emit_load_parity(P, par_in_ap, par_sb, engines=("gp",)):
    t = P.dma("par", "gp", par_sb[0:1, 0:1], par_in_ap)

    def mk(eng):
        def alloc(e):
            par = e.alloc_register(f"par_reg_{eng}")
            P.holder[f"par_{eng}"] = par
            if eng == "gp":
                P.holder["par"] = par
            return e.reg_load(par, par_sb[0:1, 0:1])
        return alloc
    for eng in engines:
        P.wait(eng, t)
        P.step(eng, mk(eng))



def emit_router_f32(P, x0T_ap, wT_sb, rstd_all, mur_all, bufs, psF, cons,
                    deps=()):
    wr32, br_sb = cons["wr32_sb"], cons["br_sb"]
    ident = cons["ident_f32"]
    ones_r = cons["ones_r"]
    xf, xnf = bufs["xf"], bufs["xnf"]
    logit, m8, p1, p2, eq1, eq2, wt = (bufs["logit"], bufs["m8"], bufs["p1"],
                                       bufs["p2"], bufs["eq1"], bufs["eq2"],
                                       bufs["wt"])
    ps_r = psF[0][:, 0:NE]
    ps_wt = psF[1][0:NE, 0:128]
    ps_b1, ps_b2 = psF[2][:, :], psF[3][:, :]
    P.waits("pe", list(deps))
    t_xnf_done = None
    t_ev = None
    t_tr_ev = None
    t_bc_cons = None
    for kk in range(TO // CHK):
        def go(e, sem, kk=kk):
            par = P.holder["par"]
            with e.If_eq(par, 0):
                for m in range(NM):
                    e.dma_start(out=xf[m],
                                in_=x0T_ap[m * 128:(m + 1) * 128,
                                           kk * CHK:(kk + 1) * CHK]
                                ).then_inc(sem, 16)
            with e.Else():
                for m in range(NM):
                    e.dma_start(out=xf[m],
                                in_=x0T_ap[m * 128:(m + 1) * 128,
                                           TO + kk * CHK:TO + (kk + 1) * CHK]
                                ).then_inc(sem, 16)
        P.waits("gp", [t_xnf_done])
        t_ld = P.raw("rtx", "gp", go, inc=16 * NM)
        P.waits("pe", [t_bc_cons])
        P.step("pe", lambda e, kk=kk: e.matmul(
            ps_b1, ones_r[:, :], rstd_all[0:1, kk * CHK:(kk + 1) * CHK],
            start=True, stop=True))
        t_bc = P.step("pe", lambda e, kk=kk: e.matmul(
            ps_b2, ones_r[:, :], mur_all[0:1, kk * CHK:(kk + 1) * CHK],
            start=True, stop=True), inc=1)
        P.waits("dve", [t_bc, t_ld])
        for m in range(NM):
            P.step("dve", lambda e, m=m: e.tensor_mul(out=xnf[m], in0=xf[m],
                                                      in1=ps_b1))
            P.step("dve", lambda e, m=m: e.tensor_sub(out=xnf[m], in0=xnf[m],
                                                      in1=ps_b2))
        t_xnf_done = P.step("dve", lambda e: e.engine_nop(), inc=1)
        t_bc_cons = t_xnf_done
        for tt in range(CHK // 128):
            P.waits("pe", [t_xnf_done, t_ev])
            for m in range(NM):
                P.step("pe", lambda e, m=m, tt=tt: e.matmul(
                    ps_r, xnf[m][:, tt * 128:(tt + 1) * 128],
                    wr32[:, m * NE:(m + 1) * NE],
                    start=(m == 0), stop=(m == NM - 1)),
                    inc=(1 if m == NM - 1 else 0))
            t_mm = P.tok("pe")
            P.wait("dve", t_mm)
            P.step("dve", lambda e: e.tensor_tensor(out=logit, in0=ps_r,
                                                    in1=br_sb[:, :], op=ALU.add))
            P.step("dve", lambda e: e.max(out=m8, in_=logit))
            t_d = P.step("dve", lambda e: e.tensor_sub(out=p2, in0=m8[:, 1:2],
                                                       in1=m8[:, 0:1]), inc=1)
            P.wait("act", t_d)
            t_sg = P.step("act", lambda e: e.activation(out=p2, in_=p2,
                                                        func=AF.Sigmoid), inc=1)
            P.wait("dve", t_sg)
            P.step("dve", lambda e: e.tensor_scalar(out=p1, in0=p2, scalar1=-1.0,
                                                    scalar2=1.0, op0=ALU.mult,
                                                    op1=ALU.add))
            P.step("dve", lambda e: e.tensor_scalar(out=eq1, in0=logit,
                                                    scalar1=m8[:, 0:1], scalar2=None,
                                                    op0=ALU.is_equal))
            P.step("dve", lambda e: e.tensor_scalar(out=eq2, in0=logit,
                                                    scalar1=m8[:, 1:2], scalar2=None,
                                                    op0=ALU.is_equal))
            P.step("dve", lambda e: e.tensor_scalar(out=eq2, in0=eq2,
                                                    scalar1=p2[:, 0:1], scalar2=None,
                                                    op0=ALU.mult))
            t_w = P.step("dve", lambda e: e.scalar_tensor_tensor(
                out=wt, in0=eq1, scalar=p1[:, 0:1], in1=eq2,
                op0=ALU.mult, op1=ALU.add), inc=1)
            P.waits("pe", [t_w, t_tr_ev])
            t_tr = P.step("pe", lambda e: e.transpose(out=ps_wt, in_=wt,
                                                      identity=ident[:, :]), inc=1)
            P.wait("dve", t_tr)
            g_tt = kk * 4 + tt
            t_tr_ev = P.step("dve", lambda e, g_tt=g_tt: e.tensor_copy(
                out=wT_sb[:, g_tt * 128:(g_tt + 1) * 128], in_=ps_wt), inc=1)
            t_ev = t_tr_ev
    return t_tr_ev



def emit_moe(P, xn_own, wT_sb, y_acc, W1L_ap, W2L_ap, bufs, psF, cons, deps=(),
             idle_cb=None):
    w1r = bufs["w1r"]
    w2b = bufs["w2b"]
    hT = bufs["hT"]
    wb_sb = bufs["wb_sb"]
    ytmp = bufs["ytmp"]
    W2c_d = bufs["W2c_d"]
    b1_sb, b2r_sb, sel_sb = cons["b1_sb"], cons["b2r_sb"], cons["sel_sb"]
    MCH = 1024
    NC = TO // MCH
    NFH = NF // 2

    P.waits("dve", list(deps))
    t0 = P.step("dve", lambda e: e.memset(bufs["y_acc_flat"], 0.0), inc=1)

    t_w2_cons = [None, None]
    t_w2c = [None]
    t_h_ev = {}
    t_ps_h_free = [None, None]
    t_ps_y_free = [None, None]
    t_wb_free = None
    t_yadd_last = t0
    ps_wb = [psF[0][:, :], psF[1][:, :]]
    ps_h = [psF[2][:, :], psF[3][:, :]]
    ps_y = [psF[4][:, :], psF[5][:, :]]

    def w1_load(e_i, fh, dep):
        t = None
        for f16 in range(NFH):
            t = P.dma("w1r", "gp", w1r[:, f16 * 1024:(f16 + 1) * 1024],
                      W1L_ap[e_i, fh * NFH + f16, :, :],
                      deps=[dep] if f16 == 0 else [])
        return t

    t_w1_ld = bufs.get("w1_pre_tok")
    if t_w1_ld is None:
        t_w1_ld = w1_load(0, 0, None)
    it = 0
    for e_i in range(NE):
        for fh in range(2):
            t_w1mm_last = None
            for c in range(NC):
                P.waits("pe", [t_wb_free])
                for s in range(2):
                    P.step("pe", lambda e, e_i=e_i, c=c, s=s: e.matmul(
                        ps_wb[s], sel_sb[:, e_i * 128:(e_i + 1) * 128],
                        wT_sb[:, c * MCH + s * 512: c * MCH + (s + 1) * 512],
                        start=True, stop=True), inc=(1 if s == 1 else 0))
                t_wbmm = P.tok("pe")
                P.waits("act", [t_wbmm, t_yadd_last])
                for s in range(2):
                    P.step("act", lambda e, s=s: e.copy(
                        out=wb_sb[:, s * 512:(s + 1) * 512], in_=ps_wb[s]),
                        inc=(1 if s == 1 else 0))
                t_wb_ev = P.tok("act")
                t_wb_free = t_wb_ev

                for f16 in range(NFH):
                    f = fh * NFH + f16
                    for s in range(2):
                        P.waits("pe", [t_w1_ld, t_ps_h_free[s]])
                        for m in range(NM):
                            P.step("pe", lambda e, f16=f16, m=m, c=c, s=s:
                                   e.matmul(
                                ps_h[s], w1r[:, f16 * 1024 + m * 128:
                                             f16 * 1024 + (m + 1) * 128],
                                xn_own[m][:, c * MCH + s * 512:
                                          c * MCH + (s + 1) * 512],
                                start=(m == 0), stop=(m == NM - 1)),
                                inc=(1 if m == NM - 1 else 0))
                        t_hmm = P.tok("pe")
                        P.wait("act", t_hmm)
                        tk = P.step("act", lambda e, f16=f16, f=f, s=s, e_i=e_i:
                                    e.activation(
                            out=hT[f16][:, s * 512:(s + 1) * 512], in_=ps_h[s],
                            func=AF.Relu,
                            bias=b1_sb[:, e_i * NF + f: e_i * NF + f + 1]),
                            inc=1)
                        t_ps_h_free[s] = tk
                        t_h_ev[(f16, s)] = tk
                t_h_all = P.tok("act")
                t_w1mm_last = P.tok("pe")
                if c == NC - 1:
                    nxt = e_i * 2 + fh + 1
                    if nxt < 16:
                        t_w1_ld = w1_load(nxt // 2, nxt % 2, t_w1mm_last)
                for mt in range(NM):
                    pb = mt % 2
                    if c == 0:
                        t_ld = P.dma(f"w2_{pb}", "gp", w2b[pb],
                                     W2L_ap[e_i, mt, :,
                                            fh * 2048:(fh + 1) * 2048],
                                     deps=[t_w2_cons[pb], t_w2c[0]])
                        P.dma("w2c", "sync",
                              W2c_d[e_i, mt, :, fh * 2048:(fh + 1) * 2048],
                              w2b[pb], deps=[t_ld])
                        t_w2c[0] = P.dtok("w2c")
                    else:
                        t_ld = P.dma(f"w2_{pb}", "sync", w2b[pb],
                                     W2c_d[e_i, mt, :,
                                           fh * 2048:(fh + 1) * 2048],
                                     deps=[t_w2_cons[pb], t_w2c[0]])
                    for s in range(2):
                        P.waits("pe", [t_ld, t_h_all, t_ps_y_free[s]])
                        for f16 in range(NFH):
                            P.step("pe", lambda e, pb=pb, f16=f16, s=s: e.matmul(
                                ps_y[s], w2b[pb][:, f16 * 128:(f16 + 1) * 128],
                                hT[f16][:, s * 512:(s + 1) * 512],
                                start=(f16 == 0), stop=(f16 == NFH - 1)),
                                inc=(1 if f16 == NFH - 1 else 0))
                        t_ymm = P.tok("pe")
                        if s == 1:
                            t_w2_cons[pb] = t_ymm
                        P.waits("dve", [t_ymm, t_wb_ev])
                        if fh == 0:
                            P.step("dve", lambda e, s=s, e_i=e_i, mt=mt:
                                   e.scalar_tensor_tensor(
                                out=ytmp, in0=ps_y[s],
                                scalar=b2r_sb[:, e_i * NM + mt:
                                              e_i * NM + mt + 1],
                                in1=wb_sb[:, s * 512:(s + 1) * 512],
                                op0=ALU.add, op1=ALU.mult))
                        else:
                            P.step("dve", lambda e, s=s: e.scalar_tensor_tensor(
                                out=ytmp, in0=ps_y[s], scalar=0.0,
                                in1=wb_sb[:, s * 512:(s + 1) * 512],
                                op0=ALU.add, op1=ALU.mult))
                        tk = P.step("dve", lambda e, mt=mt, c=c, s=s:
                                    e.tensor_tensor(
                            out=y_acc[mt][:, c * MCH + s * 512:
                                          c * MCH + (s + 1) * 512],
                            in0=y_acc[mt][:, c * MCH + s * 512:
                                          c * MCH + (s + 1) * 512],
                            in1=ytmp, op=ALU.add), inc=1)
                        t_ps_y_free[s] = tk
                        t_yadd_last = tk
                if idle_cb is not None:
                    idle_cb(it)
                it += 1
    return t_yadd_last


def build_launch_A():
    nc = bass.Bass("TRN2", target_bir_lowering=False, debug=False)
    x0T = nc.dram_tensor("x0T", [D, T], F32, kind="ExternalInput").ap()
    Wr = nc.dram_tensor("Wr", [128, NM * NE], F32, kind="ExternalInput").ap()
    br_bc = nc.dram_tensor("br_bc", [128, NE], F32, kind="ExternalInput").ap()
    W1L = nc.dram_tensor("W1L", [NE, NF, 128, 1024], F32, kind="ExternalInput").ap()
    W2L = nc.dram_tensor("W2L", [NE, NM, 128, 4096], F32, kind="ExternalInput").ap()
    W2c_d = nc.dram_tensor("W2c_d", [NE, NM, 128, 4096], BF16).ap()
    b1r = nc.dram_tensor("b1r", [128, NE * NF], F32, kind="ExternalInput").ap()
    b2r = nc.dram_tensor("b2r", [128, NE * NM], F32, kind="ExternalInput").ap()
    selc = nc.dram_tensor("selc", [NE, NE * 128], F32, kind="ExternalInput").ap()
    ident_in = nc.dram_tensor("ident_in", [128, 128], F32, kind="ExternalInput").ap()
    par_in = nc.dram_tensor("par_in", [1, 1], I32, kind="ExternalInput").ap()

    xn_dram = nc.dram_tensor("xn_dram", [D, T], BF16, kind="ExternalOutput").ap()
    moe_out = nc.dram_tensor("moe_out", [D, TO], F32, kind="ExternalOutput").ap()

    ctx = ExitStack()
    with ctx:
        E = ctx.enter_context
        P = Prog(nc, ctx)
        poolA = E(nc.sbuf_tensor("poolA", [128, 16384], F32))
        poolB = E(nc.sbuf_tensor("poolB", [128, 32768], BF16))
        poolC = E(nc.sbuf_tensor("poolC", [128, 16384], BF16))
        poolS = E(nc.sbuf_tensor("poolS", [128, 2048], F32))
        poolW = E(nc.sbuf_tensor("poolW", [128, 10240], BF16))
        ones_c = E(nc.sbuf_tensor("ones_c", [128, 1], F32))
        ones_r = E(nc.sbuf_tensor("ones_r", [1, 128], F32))
        ident_f32 = E(nc.sbuf_tensor("ident_f32", [128, 128], F32))
        wr32_sb = E(nc.sbuf_tensor("wr32_sb", [128, NM * NE], F32))
        br_sb = E(nc.sbuf_tensor("br_sb", [128, NE], F32))
        b1_sb = E(nc.sbuf_tensor("b1_sb", [128, NE * NF], F32))
        b2r_sb = E(nc.sbuf_tensor("b2r_sb", [128, NE * NM], F32))
        sel_sb = E(nc.sbuf_tensor("sel_sb", [NE, NE * 128], BF16))
        par_sb = E(nc.sbuf_tensor("par_sb", [1, 1], I32))
        eps_b = E(nc.sbuf_tensor("eps_b", [1, 1], F32))
        wT_sb = E(nc.sbuf_tensor("wT_sb", [NE, TO], BF16))
        psF = [E(nc.psum_tensor(f"psF{i}", [128, 512], F32)) for i in range(6)]
        psB = [E(nc.psum_tensor(f"psB{i}", [128, 1024], BF16)) for i in range(2)]

        blk = E(nc.Block())

        cons = dict(ones_c=ones_c, ones_r=ones_r, ident_f32=ident_f32,
                    wr32_sb=wr32_sb, br_sb=br_sb, b1_sb=b1_sb, b2r_sb=b2r_sb,
                    sel_sb=sel_sb, psF=psF, psB=psB, eps_b=eps_b)

        P.step("dve", lambda e: e.memset(ones_c[:, :], 1.0))
        P.step("dve", lambda e: e.memset(eps_b[:, :], LN_EPS))
        t_init = P.step("dve", lambda e: e.memset(ones_r[:, :], 1.0), inc=1)
        P.wait("pe", t_init)
        P.wait("act", t_init)
        emit_load_parity(P, par_in, par_sb, engines=("gp", "dve"))
        tc = [
            P.dma("cns", "sync", wr32_sb[:, :], Wr),
            P.dma("cns", "sync", br_sb[:, :], br_bc),
            P.dma("cns", "sync", b1_sb[:, :], b1r),
            P.dma("cns", "sync", b2r_sb[:, :], b2r),
            P.dma("cnsg", "gp", sel_sb[:, :], selc),
            P.dma("cns", "sync", ident_f32[:, :], ident_in),
        ]

        ln_bufs = dict(
            xbuf=[[poolA[:, (p * NM + m) * CHK:(p * NM + m + 1) * CHK]
                   for m in range(NM)] for p in range(2)],
            xsq=[poolS[:, 0:512], poolS[:, 512:1024]],
            rows=[poolA[0:1, 9216 + i * 512: 9216 + (i + 1) * 512] for i in range(5)],
            xnc=[poolB[:, m * CHK:(m + 1) * CHK] for m in range(NM)],
            tmp=[poolA[:, 8192 + (i % 2) * CHK: 8192 + (i % 2 + 1) * CHK]
                 for i in range(4)],
            rstd_all=poolA[0:1, 11776:13824],
            mur_all=poolA[0:1, 13824:15872],
        )
        ln_stores = emit_ln(P, x0T, xn_dram, ln_bufs, psF, cons)

        P.barrier_all()

        xn_own = [poolC[:, m * TO:(m + 1) * TO] for m in range(NM)]
        t_own = emit_read_own(P, xn_dram, xn_own, deps=[])
        for e in ("pe", "dve", "act"):
            P.wait(e, t_own)

        rt_bufs = dict(
            logit=poolS[:, 0:8], m8=poolS[:, 8:16], p1=poolS[:, 16:17],
            p2=poolS[:, 17:18], eq1=poolS[:, 24:32], eq2=poolS[:, 32:40],
            wt=poolS[:, 40:48],
            xf=[poolA[:, m * CHK:(m + 1) * CHK] for m in range(NM)],
            xnf=[poolA[:, 4096 + m * CHK: 4096 + (m + 1) * CHK] for m in range(NM)],
        )
        P.waits("pe", tc)
        t_rt = emit_router_f32(P, x0T, wT_sb, ln_bufs["rstd_all"],
                               ln_bufs["mur_all"], rt_bufs, psF, cons)

        P.barrier_all()

        y_acc = [poolA[:, m * TO:(m + 1) * TO] for m in range(NM)]
        moe_bufs = dict(
            w1r=poolB[:, 0:16384],
            w2b=[poolW[:, p * 2048:(p + 1) * 2048] for p in range(2)],
            hT=[poolB[:, 16384 + f * 1024: 16384 + (f + 1) * 1024]
                for f in range(16)],
            wb_sb=poolS[:, 0:1024],
            ytmp=poolS[:, 1024:1536],
            y_acc_flat=poolA[:, :],
            W2c_d=W2c_d,
        )
        t_moe = emit_moe(P, xn_own, wT_sb, y_acc, W1L, W2L, moe_bufs, psF, cons)

        st = [P.dma("ost", "sync", moe_out[m * 128:(m + 1) * 128, :], y_acc[m],
                    deps=[t_moe]) for m in range(NM)]
        P.wait("sync", st[-1])
        P.wait("sync", ln_stores[-1])
        P.finalize(blk)
    return nc



def prep_A_inputs(inp):
    x0 = np.asarray(inp["L0_attn_x"])
    Wr = np.asarray(inp["L0_ffn_W_router"])
    br = np.asarray(inp["L0_ffn_b_router"])
    W1 = np.asarray(inp["L0_ffn_W1"])
    b1 = np.asarray(inp["L0_ffn_b1"])
    W2 = np.asarray(inp["L0_ffn_W2"])
    b2 = np.asarray(inp["L0_ffn_b2"])

    WrL = np.ascontiguousarray(
        Wr.reshape(NM, 128, NE).transpose(1, 0, 2).reshape(128, NM * NE))
    br_bc = np.ascontiguousarray(np.broadcast_to(br[None, :], (128, NE))).astype(np.float32)
    W1L = np.ascontiguousarray(
        W1.reshape(NE, NM, 128, NF, 128).transpose(0, 3, 2, 1, 4)
        .reshape(NE, NF, 128, NM * 128))
    W2L = np.ascontiguousarray(
        W2.reshape(NE, NF, 128, NM, 128).transpose(0, 3, 2, 1, 4)
        .reshape(NE, NM, 128, NF * 128))
    b1r = np.ascontiguousarray(b1.reshape(NE, NF, 128).transpose(2, 0, 1)
                               .reshape(128, NE * NF))
    b2r = np.ascontiguousarray(b2.reshape(NE, NM, 128).transpose(2, 0, 1)
                               .reshape(128, NE * NM))
    sel = np.zeros((NE, NE * 128), np.float32)
    for e in range(NE):
        sel[e, e * 128:(e + 1) * 128] = 1.0
    ident = np.eye(128, dtype=np.float32)

    in_maps = []
    for c in range(8):
        b = c // 2
        x0T = np.ascontiguousarray(x0[b].T)
        in_maps.append({
            "x0T": x0T, "Wr": WrL, "br_bc": br_bc, "W1L": W1L, "W2L": W2L,
            "b1r": b1r, "b2r": b2r, "selc": sel, "ident_in": ident,
            "par_in": np.array([[c % 2]], np.int32),
        })
    return in_maps


def kernel(**inputs):
    return kernel_fused(**inputs)


def kernel_unfused(**inputs):
    inp = {k: np.asarray(v) for k, v in inputs.items()}

    maps_A = prep_A_inputs(inp)
    res_A = run_bass_kernel_spmd(build_launch_A(), maps_A, core_ids=list(range(8)))
    xn = [res_A.results[c]["xn_dram"] for c in range(8)]
    moe = [res_A.results[c]["moe_out"] for c in range(8)]

    maps_B = prep_B_inputs(inp)
    for c in range(8):
        maps_B[c]["xnin"] = xn[c]
    res_B = run_bass_kernel_spmd(build_launch_B(), maps_B, core_ids=list(range(8)))

    base_C = prep_C_inputs(inp)
    maps_C = []
    for c in range(8):
        b, par = c // 2, c % 2
        m = dict(base_C)
        m["yT_in"] = np.ascontiguousarray(np.concatenate(
            [np.asarray(res_B.results[2 * b]["y_dram"]),
             np.asarray(res_B.results[2 * b + 1]["y_dram"])], axis=0))
        m["xn0_in"] = xn[c]
        m["moe_in"] = moe[c]
        m["x0T"] = maps_A[c]["x0T"]
        m["par_in"] = np.array([[par]], np.int32)
        maps_C.append(m)
    res_C = run_bass_kernel_spmd(build_launch_C(), maps_C, core_ids=list(range(8)))

    maps_D = prep_D_inputs(inp)
    xn1_full = {}
    for b in range(4):
        xn1_full[b] = np.ascontiguousarray(np.concatenate(
            [np.asarray(res_C.results[2 * b]["xn1_out"]),
             np.asarray(res_C.results[2 * b + 1]["xn1_out"])], axis=1))
    for c in range(8):
        maps_D[c]["xnin"] = xn1_full[c // 2]
    res_D = run_bass_kernel_spmd(build_launch_D(), maps_D, core_ids=list(range(8)))

    base_E = prep_E_inputs(inp)
    maps_E = []
    for c in range(8):
        b, par = c // 2, c % 2
        m = dict(base_E)
        y1cat = np.concatenate(
            [np.asarray(res_D.results[2 * b]["y_dram"]).reshape(512, T),
             np.asarray(res_D.results[2 * b + 1]["y_dram"]).reshape(512, T)],
            axis=0)
        m["y1_in"] = np.ascontiguousarray(y1cat.reshape(NM, 128, T))
        m["xn1_in"] = np.asarray(res_C.results[c]["xn1_out"])
        m["x1_in"] = np.asarray(res_C.results[c]["x1_out"])
        m["par_in"] = np.array([[par]], np.int32)
        maps_E.append(m)
    res_E = run_bass_kernel_spmd(build_launch_E(), maps_E, core_ids=list(range(8)))

    out = np.empty((B, T, D), np.float32)
    for c in range(8):
        b, par = c // 2, c % 2
        out[b, par * TO:(par + 1) * TO, :] = np.asarray(
            res_E.results[c]["out_T"]).T
    return out



NH0 = 4
LV = 12


def lmin_of_d(d):
    v = 128 * (d + 1)
    l = 0
    while (1 << l) < v:
        l += 1
    return min(l, 11)


def emit_l0_prep(P, xn, bufs, psF, cons, deps=()):
    wg_sb, wdl_sb = cons["wg_sb"], cons["wdl_sb"]
    bgs, Lps = cons["bgs_sb"], cons["lps_sb"]
    ident = cons["ident_f32"]
    rows = bufs["attn_rows"]
    lam_nat = bufs["lam_nat"]
    sp_scr = bufs["sp_scr"]
    spg_scr = bufs["spg_scr"]
    SPcol = bufs["SPcol"]
    negSPcol = bufs["negSPcol"]
    ps_dl = psF[0][0:48, :]
    ps_g = psF[1][0:4, :]
    ps_tr = psF[2]

    loglam = rows[0:48, :]
    SP = rows[64:68, :]
    sprow = rows[96:100, :]

    P.waits("pe", list(deps))
    t_dl_cons = None
    t_g_cons = None
    t_tr_cons = None
    t_ll = None
    for c in range(NCHK):
        cs = slice(c * CHK, (c + 1) * CHK)
        P.waits("pe", [t_dl_cons, t_ll])
        for m in range(NM):
            P.step("pe", lambda e, m=m, c=c: e.matmul(
                ps_dl, wdl_sb[:, m * 48:(m + 1) * 48],
                xn[m][:, c * CHK:(c + 1) * CHK],
                start=(m == 0), stop=(m == NM - 1)),
                inc=(1 if m == NM - 1 else 0))
        t_dlmm = P.tok("pe")
        P.wait("act", t_dlmm)
        P.step("act", lambda e: e.activation(out=sp_scr[0], in_=ps_dl, func=AF.Abs,
                                             bias=Lps[:, 0:1]))
        P.step("act", lambda e: e.activation(out=sp_scr[0], in_=sp_scr[0],
                                             func=AF.Exp, scale=-1.0))
        t_l = P.step("act", lambda e: e.activation(out=sp_scr[0], in_=sp_scr[0],
                                                   func=AF.Ln, bias=1.0))
        P.waits("dve", [t_dlmm, t_l, t_ll, t_tr_cons])
        P.step("dve", lambda e: e.tensor_scalar(out=sp_scr[1], in0=ps_dl,
                                                scalar1=Lps[:, 0:1], scalar2=0.0,
                                                op0=ALU.add, op1=ALU.max))
        P.step("dve", lambda e: e.tensor_tensor(out=sp_scr[1], in0=sp_scr[1],
                                                in1=sp_scr[0], op=ALU.add))
        t_lam = P.step("dve", lambda e: e.tensor_scalar_mul(
            out=sp_scr[1], in0=sp_scr[1], scalar1=1.0 / math.sqrt(128.0)), inc=1)
        t_dl_cons = t_lam
        P.wait("act", t_lam)
        t_ll = P.step("act", lambda e, cs=cs: e.activation(
            out=loglam[:, cs], in_=sp_scr[1], func=AF.Ln), inc=1)
        for tt in range(4):
            q = c * 4 + tt
            P.waits("pe", [t_lam, t_tr_cons])
            t_tr = P.step("pe", lambda e, tt=tt: e.transpose(
                out=ps_tr[:, 0:48], in_=sp_scr[1][0:48, tt * 128:(tt + 1) * 128],
                identity=ident[0:48, 0:48]), inc=1)
            P.wait("dve", t_tr)
            t_tr_cons = P.step("dve", lambda e, q=q: e.tensor_copy(
                out=lam_nat[:, q * 48:(q + 1) * 48], in_=ps_tr[:, 0:48]), inc=1)
        P.waits("pe", [t_g_cons])
        for m in range(NM):
            P.step("pe", lambda e, m=m, c=c: e.matmul(
                ps_g, wg_sb[:, m * 4:(m + 1) * 4],
                xn[m][:, c * CHK:(c + 1) * CHK],
                start=(m == 0), stop=(m == NM - 1)),
                inc=(1 if m == NM - 1 else 0))
        t_gmm = P.tok("pe")
        P.waits("act", [t_gmm, t_g_cons])
        P.step("act", lambda e: e.activation(out=spg_scr, in_=ps_g, func=AF.Abs,
                                             bias=bgs[:, 0:1]))
        P.step("act", lambda e: e.activation(out=spg_scr, in_=spg_scr,
                                             func=AF.Exp, scale=-1.0))
        t_gl = P.step("act", lambda e: e.activation(out=spg_scr, in_=spg_scr,
                                                    func=AF.Ln, bias=1.0))
        P.waits("dve", [t_gmm, t_gl])
        P.step("dve", lambda e, cs=cs: e.tensor_scalar(
            out=sprow[:, cs], in0=ps_g, scalar1=bgs[:, 0:1], scalar2=0.0,
            op0=ALU.add, op1=ALU.max))
        P.step("dve", lambda e, cs=cs: e.tensor_tensor(
            out=sprow[:, cs], in0=sprow[:, cs], in1=spg_scr, op=ALU.add))
        t_g_cons = P.step("dve", lambda e, cs=cs: e.tensor_scalar_mul(
            out=sprow[:, cs], in0=sprow[:, cs], scalar1=0.5), inc=1)
    t_scan = P.step("dve", lambda e: e.tensor_tensor_scan(
        out=SP, data0=sprow, data1=sprow, initial=0.0,
        op0=ALU.add, op1=ALU.add), inc=1)
    t_spc = None
    for k in range(32):
        P.waits("pe", [t_scan, t_spc])
        t_tr = P.step("pe", lambda e, k=k: e.transpose(
            out=ps_tr[:, 0:4], in_=SP[:, k * 128:(k + 1) * 128],
            identity=ident[64:68, 64:68]), inc=1)
        P.wait("dve", t_tr)
        t_spc = P.step("dve", lambda e, k=k: e.tensor_copy(
            out=SPcol[:, k * 4:(k + 1) * 4], in_=ps_tr[:, 0:4]), inc=1)
    t_neg = P.step("dve", lambda e: e.tensor_scalar_mul(
        out=negSPcol[:, :], in0=SPcol[:, :], scalar1=-1.0), inc=1)
    return t_neg


def emit_l0_head(P, h, xn, y_dram, bufs, psF, psB, cons):
    wq, wk, wv = bufs["wq"], bufs["wk"], bufs["wv"]
    QT, KT, Vnat = bufs["QT"], bufs["KT"], bufs["Vnat"]
    vtmp = bufs["vtmp"]
    rows = bufs["attn_rows"]
    lam_nat = bufs["lam_nat"]
    rowpool = bufs["rowpool"]
    Rsb = bufs["Rsb"]
    exinL = [bufs["exin"], bufs["exin_b"], bufs["exin_c"]]
    exptL = [bufs["expt"], bufs["expt_b"], bufs["expt_c"]]
    natx, nate, natl = bufs["natx"], bufs["nate"], bufs["natl"]
    dmx = bufs["dmx"]
    natA = bufs["natA"]
    natAT = bufs["natAT"]
    A_sb = bufs["A_sb"]
    ytile = bufs["ytile"]
    stT32, stT = bufs["stateT32"], bufs["stateT"]
    knat, vsc, qsc = bufs["knat"], bufs["vsc"], bufs["qsc"]
    gcol, mccol, dmcol = bufs["gcol"], bufs["mccol"], bufs["dmcol"]
    frow, spbnd = bufs["frow"], bufs["spbnd"]
    tmp11 = bufs["tmp11"]
    SPcol, negSPcol = bufs["SPcol"], bufs["negSPcol"]
    mnat, mst = cons["mnat_sb"], cons["mst_sb"]
    ones_r = cons["ones_r"]
    ones96 = bufs["ones96"]

    ps_sc3 = [psF[0][:, :], psF[1][:, :], psF[3][:, :]]
    ps_o = psF[2][:, :]
    ps_rb = psF[3][:, :]
    ps_dm = psF[4][:, 0:128]
    ps_st = psF[4][:, 384:512]
    ps_bt = psB[1][:, 0:512]

    loglam = rows[0:48, :]
    SP = rows[64:68, :]
    RSOFF = {0: 0, 1: 512, 2: 1024, 3: 1536, 5: 2048, 6: 2560}
    SPW = 3072
    RS = lambda i, w=512: rowpool[0:1, RSOFF[i]: RSOFF[i] + w]

    P.step("dve", lambda e: e.memset(ones96[:, :], 1.0))
    hd = bufs.get("head_dep")
    t_ld = [P.dma("wqkv", "gp", wq[:, :], cons["WqL_ap"][h], deps=[hd]),
            P.dma("wqkv", "gp", wk[:, :], cons["WkL_ap"][h]),
            P.dma("wqkv", "gp", wv[:, :], cons["WvL_ap"][h])]
    ev_bank = [None, None, None]
    t_vn_ev = None
    t_tr_prev = None
    for c in range(NCHK):
        for b, (wsb, dst) in enumerate([(wq, QT), (wk, KT)]):
            P.waits("pe", [t_ld[-1] if c == 0 else None, ev_bank[b]])
            for m in range(NM):
                P.step("pe", lambda e, wsb=wsb, m=m, c=c, b=b: e.matmul(
                    ps_sc3[b], wsb[:, m * 128:(m + 1) * 128],
                    xn[m][:, c * CHK:(c + 1) * CHK],
                    start=(m == 0), stop=(m == NM - 1)),
                    inc=(1 if m == NM - 1 else 0))
            t_mm = P.tok("pe")
            P.wait("act", t_mm)
            ev_bank[b] = P.step("act", lambda e, dst=dst, c=c, b=b: e.copy(
                out=dst[:, c * CHK:(c + 1) * CHK], in_=ps_sc3[b]), inc=1)
        P.waits("pe", [t_ld[-1] if c == 0 else None, ev_bank[2]])
        for m in range(NM):
            P.step("pe", lambda e, m=m, c=c: e.matmul(
                ps_sc3[2], wv[:, m * 128:(m + 1) * 128],
                xn[m][:, c * CHK:(c + 1) * CHK],
                start=(m == 0), stop=(m == NM - 1)),
                inc=(1 if m == NM - 1 else 0))
        t_vmm = P.tok("pe")
        P.waits("act", [t_vmm, t_tr_prev])
        t_vtmp = P.step("act", lambda e: e.copy(out=vtmp[:, :], in_=ps_sc3[2]),
                        inc=1)
        ev_bank[2] = t_vtmp
        P.waits("pe", [t_vtmp, t_vn_ev])
        for tt in range(4):
            P.step("pe", lambda e, tt=tt: e.transpose(
                out=ps_bt[:, tt * 128:(tt + 1) * 128],
                in_=vtmp[:, tt * 128:(tt + 1) * 128],
                identity=cons["ident_bf16"][:, :]), inc=(1 if tt == 3 else 0))
        t_tr_prev = P.tok("pe")
        P.wait("act", t_tr_prev)
        t_vn_ev = P.step("act", lambda e, c=c: e.copy(
            out=Vnat[:, c * 512:(c + 1) * 512], in_=ps_bt), inc=1)

    P.step("dve", lambda e: e.memset(exinL[0][:, :], 0.0))
    P.step("dve", lambda e: e.memset(exinL[1][:, :], 0.0))

    t_sc_free = [None, None, None]
    t_A_cons = [None, None, None]
    t_exp_done = [None, None, None]
    t_amult = [None, None, None]
    t_natAT_ev = None
    t_merge = None
    t_apply = bufs.get("apply_dep")
    t_st_mm = None
    t_kn_ev = None
    t_stT = None
    t_qsc = None
    t_y_ev = None
    t_fr = None
    t_d11 = None
    t_pe_prev = hd
    t_rowread = hd
    t_rb_ev = None
    t_vsc = None
    t_scn_cons = None
    for c in range(NCHK):
        nk = 4 * c + 4
        k0 = max(0, 4 * c - 8)
        cs = slice(c * CHK, (c + 1) * CHK)
        stg = []
        for slot, row_ap in [(0, loglam[12 * h + 8:12 * h + 9, cs]),
                             (1, loglam[12 * h + 9:12 * h + 10, cs]),
                             (2, loglam[12 * h + 10:12 * h + 11, cs]),
                             (3, loglam[12 * h + 11:12 * h + 12, cs])]:
            stg.append(P.dma("rstg", "sync", RS(slot), row_ap,
                             deps=[t_rowread, t_fr, t_d11]))
        w0 = max(0, (4 * c - 1) * 128)
        w1 = (4 * c + 4) * 128
        stg.append(P.dma("rstg", "sync", rowpool[0:1, SPW:SPW + (w1 - w0)],
                         SP[h:h + 1, w0:w1]))
        if c >= 4:
            stg.append(P.dma("rstg", "sync", spbnd[:, :],
                             SP[h:h + 1, 512 * (c - 1):512 * (c - 1) + 1],
                             deps=[t_d11]))
        spwin = lambda k, w0=w0: rowpool[0:1, SPW + k * 128 - w0:
                                         SPW + k * 128 - w0 + 128]
        spoff = SPW + (512 * c - w0)
        winSP = rowpool[0:1, spoff:spoff + 512]
        P.waits("dve", [stg[-1]])
        P.step("dve", lambda e: e.tensor_sub(out=RS(0), in0=RS(0), in1=RS(1)))
        P.step("dve", lambda e: e.tensor_copy(out=RS(5), in_=RS(1)))
        P.step("dve", lambda e: e.tensor_sub(out=RS(5), in0=RS(5), in1=RS(2)))
        P.step("dve", lambda e: e.tensor_copy(out=RS(6), in_=RS(2)))
        P.step("dve", lambda e: e.tensor_sub(out=RS(6), in0=RS(6), in1=RS(3)))
        P.step("dve", lambda e, winSP=winSP: e.tensor_sub(
            out=RS(1), in0=RS(1), in1=winSP))
        P.step("dve", lambda e, winSP=winSP: e.tensor_sub(
            out=RS(2), in0=RS(2), in1=winSP))
        t_drv = P.step("dve", lambda e, winSP=winSP: e.tensor_sub(
            out=RS(3), in0=RS(3), in1=winSP), inc=1)
        for i, slot in enumerate([1, 2, 3, 0, 5, 6]):
            bnk = i % 3
            P.waits("pe", [t_drv, t_sc_free[bnk],
                           ev_bank[bnk] if c == 0 else None])
            t_b = P.step("pe", lambda e, slot=slot, bnk=bnk: e.matmul(
                ps_sc3[bnk], ones96[:, :], RS(slot), start=True, stop=True),
                inc=1)
            P.wait("act", t_b)
            t_rb_ev = P.step("act", lambda e, i=i, bnk=bnk: e.copy(
                out=Rsb[:, i * 512:(i + 1) * 512], in_=ps_sc3[bnk]), inc=1)
            t_sc_free[bnk] = t_rb_ev

        def emit_state():
            nonlocal t_kn_ev, t_merge, t_apply, t_st_mm, t_stT, t_qsc, t_fr, \
                t_d11, t_b3_ready, t_natAT_ev, t_vsc, t_scn_cons, t_y_ev
            idxc = (4 * c) * 4 + h
            P.waits("pe", [t_rb_ev])
            t_mc = P.step("pe", lambda e, idxc=idxc: e.matmul(
                ps_rb[:, 0:1], ones_r[:, :], negSPcol[0:1, idxc:idxc + 1],
                start=True, stop=True), inc=1)
            P.wait("act", t_mc)
            t_mcc = P.step("act", lambda e: e.copy(out=mccol[:, :],
                                                   in_=ps_rb[:, 0:1]), inc=1)
            t_dmc = None
            if c >= 4:
                P.waits("dve", [stg[-1]])
                t_d11 = P.step("dve", lambda e, winSP=winSP: e.tensor_sub(
                    out=tmp11[:, :], in0=spbnd[:, :],
                    in1=winSP[0:1, 0:1]), inc=1)
                P.waits("pe", [t_d11, t_mcc])
                t_dmr = P.step("pe", lambda e: e.matmul(
                    ps_rb[:, 1:2], ones96[:, :], tmp11[:, :],
                    start=True, stop=True), inc=1)
                P.wait("act", t_dmr)
                t_dmc = P.step("act", lambda e: e.activation(
                    out=dmcol[:, :], in_=ps_rb[:, 1:2], func=AF.Exp), inc=1)
            newk = [4 * c - 12 + j for j in range(4)]
            for j, kj in enumerate(newk):
                P.step("act", lambda e, j=j, kj=kj: e.activation(
                    out=gcol[:, j:j + 1], in_=SPcol[:, kj * 4 + h:kj * 4 + h + 1],
                    func=AF.Exp, bias=mccol[:, 0:1]), inc=(1 if j == 3 else 0))
            t_gc = P.tok("act")
            P.waits("pe", [t_kn_ev, t_natAT_ev, t_vn_ev if c == 3 else None])
            for j, kj in enumerate(newk):
                P.step("pe", lambda e, j=j, kj=kj: e.transpose(
                    out=ps_bt[:, j * 128:(j + 1) * 128],
                    in_=KT[:, kj * 128:(kj + 1) * 128],
                    identity=cons["ident_bf16"][:, :]), inc=(1 if j == 3 else 0))
            t_ktr = P.tok("pe")
            P.wait("act", t_ktr)
            t_kn_ev = P.step("act", lambda e: e.copy(out=knat[:, :], in_=ps_bt),
                             inc=1)
            P.waits("dve", [t_gc, t_st_mm])
            for j, kj in enumerate(newk):
                P.step("dve", lambda e, j=j, kj=kj: e.tensor_scalar(
                    out=vsc[:, j * 128:(j + 1) * 128],
                    in0=Vnat[:, kj * 128:(kj + 1) * 128],
                    scalar1=gcol[:, j:j + 1], scalar2=None, op0=ALU.mult),
                    inc=(1 if j == 3 else 0))
            t_vsc = P.tok("dve")
            P.waits("pe", [t_kn_ev, t_vsc, t_scn_cons, t_merge])
            for j in range(4):
                P.step("pe", lambda e, j=j: e.matmul(
                    ps_st, knat[:, j * 128:(j + 1) * 128],
                    vsc[:, j * 128:(j + 1) * 128],
                    start=(j == 0), stop=(j == 3)), inc=(1 if j == 3 else 0))
            t_st_mm = P.tok("pe")
            if c == 3:
                P.waits("dve", [t_st_mm])
                t_merge = P.step("dve", lambda e: e.tensor_copy(
                    out=stT32[:, :], in_=ps_st), inc=1)
            else:
                P.waits("dve", [t_st_mm, t_dmc])
                t_merge = P.step("dve", lambda e: e.scalar_tensor_tensor(
                    out=stT32[:, :], in0=stT32[:, :], scalar=dmcol[:, 0:1],
                    in1=ps_st, op0=ALU.mult, op1=ALU.add), inc=1)
            P.waits("dve", [t_apply])
            t_stT = P.step("dve", lambda e: e.tensor_copy(out=stT[:, :],
                                                          in_=stT32[:, :]), inc=1)
            P.waits("act", [t_drv])
            t_fr = P.step("act", lambda e, winSP=winSP: e.activation(
                out=frow[:, :], in_=RS(3), func=AF.Exp,
                bias=winSP[0:1, 0:1]), inc=1)
            P.waits("pe", [t_fr, t_dmc if c >= 4 else t_mcc, t_sc_free[2]])
            t_fb = P.step("pe", lambda e: e.matmul(
                ps_rb, ones_r[:, :], frow[:, :], start=True, stop=True), inc=1)
            P.waits("dve", [t_fb, t_apply])
            t_qsc = P.step("dve", lambda e, cs=cs: e.tensor_tensor(
                out=qsc[:, :], in0=QT[:, cs], in1=ps_rb, op=ALU.mult), inc=1)
            P.waits("pe", [t_stT, t_qsc])
            t_apply = P.step("pe", lambda e: e.matmul(
                ps_o, stT[:, :], qsc[:, :], start=False, stop=False), inc=1)
            t_b3_ready = t_qsc


        t_b3_ready = t_rb_ev
        pend = []
        amq = []
        first_av = [True]
        done = [False]

        def flush_av():
            k2, slot2, tA2 = pend.pop(0)
            first = first_av[0]
            last = (len(pend) == 0 and len(amq) == 0 and done[0])
            first_av[0] = False
            P.waits("pe", [tA2, t_y_ev if first else None])
            t_av = P.step("pe", lambda e, k2=k2, slot2=slot2, first=first,
                          last=last:
                          e.matmul(ps_o, Vnat[:, k2 * 128:(k2 + 1) * 128],
                                   A_sb[slot2], start=first, stop=last),
                          inc=1)
            t_A_cons[slot2] = t_av
            return t_av

        def emit_amult(ent):
            k, slot, pp, b, t_s, t_e = ent
            P.waits("dve", [t_e, t_s, t_A_cons[slot]])
            t_A = P.step("dve", lambda e, slot=slot, pp=pp, b=b: e.tensor_tensor(
                out=A_sb[slot], in0=exptL[pp], in1=ps_sc3[b], op=ALU.mult), inc=1)
            t_amult[pp] = t_A
            t_sc_free[b] = t_A
            pend.append((k, slot, t_A))

        for idx, k in enumerate(range(k0, nk)):
            b = idx % 3
            pp = idx % 3
            slot = idx % 3
            if len(amq) >= 3:
                emit_amult(amq.pop(0))
            P.waits("pe", [t_sc_free[b], t_b3_ready if b == 2 else None])
            t_s = P.step("pe", lambda e, k=k, cs=cs, b=b: e.matmul(
                ps_sc3[b], KT[:, k * 128:(k + 1) * 128], QT[:, cs],
                start=True, stop=True), inc=1)
            if len(pend) > 2:
                flush_av()
            if c >= 3 and idx == 5:
                emit_state()
            zero_subs = []
            corr_subs = []
            plain_subs = []
            P.waits("dve", [t_exp_done[pp], t_rb_ev])
            for sub in range(4):
                d = 4 * c + sub - k
                ss = slice(sub * 128, (sub + 1) * 128)
                if d in (0, 1) or d < 0:
                    zero_subs.append(ss)
                    continue
                if d >= 9:
                    roff = 1024 + sub * 128
                elif d in (2, 3):
                    roff = sub * 128
                elif d in (4, 5, 6, 7):
                    roff = 512 + sub * 128
                else:
                    roff = 1024 + sub * 128
                if d in (2, 4, 8):
                    mcol = {2: 0, 4: 128, 8: 256}[d]
                    rdoff = {2: 3 * 512, 4: 4 * 512, 8: 5 * 512}[d] + sub * 128
                    P.step("dve", lambda e, mcol=mcol, rdoff=rdoff:
                           e.scalar_tensor_tensor(
                        out=dmx, in0=Rsb[:, rdoff:rdoff + 128], scalar=1.0,
                        in1=mst[:, mcol:mcol + 128], op0=ALU.mult, op1=ALU.mult))
                    P.step("dve", lambda e, ss=ss, pp=pp, roff=roff:
                           e.tensor_tensor(
                        out=exinL[pp][:, ss], in0=Rsb[:, roff:roff + 128],
                        in1=dmx, op=ALU.add))
                    corr_subs.append(ss)
                else:
                    plain_subs.append((ss, roff))
            t_asm = P.tok("dve")
            P.waits("act", [t_asm, t_amult[pp]])
            for ss, roff in plain_subs:
                P.step("act", lambda e, ss=ss, roff=roff, pp=pp, k=k:
                       e.activation(
                    out=exptL[pp][:, ss], in_=Rsb[:, roff:roff + 128],
                    func=AF.Exp, bias=SPcol[:, (k * 4 + h):(k * 4 + h + 1)]))
            for ss in corr_subs:
                P.step("act", lambda e, ss=ss, pp=pp, k=k: e.activation(
                    out=exptL[pp][:, ss], in_=exinL[pp][:, ss],
                    func=AF.Exp, bias=SPcol[:, (k * 4 + h):(k * 4 + h + 1)]))
            for ss in zero_subs:
                P.step("dve", lambda e, ss=ss, pp=pp: e.memset(
                    exptL[pp][:, ss], 0.0))
            t_e = P.tok("act")
            t_exp_done[pp] = t_e
            amq.append((k, slot, pp, b, t_s, t_e))
        while amq:
            while len(pend) > 2:
                flush_av()
            emit_amult(amq.pop(0))

        nat_pairs = []
        for sub in range(4):
            q = 4 * c + sub
            nat_pairs.append((q, sub, q - 1 >= 0))
        ps_scn2 = [psF[5][:, 0:256], psF[5][:, 256:512]]
        ps_sn = psF[4][:, 128:384]
        ps_nt2 = [psB[1][:, 512:768], psB[1][:, 768:1024]]
        t_natx2 = [None, None]
        t_sn_cons = None
        t_tr_prev_nat = None
        t_ev2 = [t_natAT_ev, t_natAT_ev]
        t_avn2 = [None, None]
        if first_av[0] and pend:
            flush_av()

        def nat_scores(j):
            q, sub, hasd1 = nat_pairs[j]
            r = j % 2
            P.waits("pe", [t_natx2[r], t_sn_cons,
                           t_merge if j == 0 else None,
                           stg[-1] if j == 0 else None])
            P.step("pe", lambda e, q=q, r=r: e.matmul(
                ps_scn2[r][:, 0:128], QT[:, q * 128:(q + 1) * 128],
                KT[:, q * 128:(q + 1) * 128], start=True, stop=True))
            if hasd1:
                P.step("pe", lambda e, q=q, r=r: e.matmul(
                    ps_scn2[r][:, 128:256], QT[:, q * 128:(q + 1) * 128],
                    KT[:, (q - 1) * 128:q * 128], start=True, stop=True))
            P.step("pe", lambda e, q=q, spwin=spwin: e.matmul(
                ps_sn[:, 0:128], ones96[:, :], spwin(q),
                start=True, stop=True), inc=(0 if hasd1 else 1))
            if hasd1:
                P.step("pe", lambda e, q=q, spwin=spwin: e.matmul(
                    ps_sn[:, 128:256], ones96[:, :], spwin(q - 1),
                    start=True, stop=True), inc=1)
            return P.tok("pe")

        def nat_dve(j, t_sb):
            nonlocal t_scn_cons, t_sn_cons
            q, sub, hasd1 = nat_pairs[j]
            r = j % 2
            w = 256 if hasd1 else 128
            P.waits("dve", [t_sb])
            t_nx = P.step("dve", lambda e, q=q, w=w: e.tensor_scalar(
                out=natx[:, 0:w], in0=ps_sn[:, 0:w],
                scalar1=negSPcol[:, (q * 4 + h):(q * 4 + h + 1)],
                scalar2=0.0, op0=ALU.add, op1=ALU.min), inc=1)
            t_sn_cons = t_nx
            P.wait("act", t_nx)
            t_ne = P.step("act", lambda e, w=w: e.activation(
                out=nate[:, 0:w], in_=natx[:, 0:w], func=AF.Exp), inc=1)
            first = True
            for l in range(0, 9):
                if not hasd1 and l == 8:
                    continue
                lam_col = lam_nat[:, q * 48 + h * 12 + l:
                                  q * 48 + h * 12 + l + 1]
                mk = mnat[:, l * 256: l * 256 + w]
                if first:
                    P.step("dve", lambda e, mk=mk, lam_col=lam_col, w=w:
                           e.tensor_scalar(out=natl[:, 0:w], in0=mk,
                                           scalar1=lam_col, scalar2=None,
                                           op0=ALU.mult))
                    first = False
                else:
                    P.step("dve", lambda e, mk=mk, lam_col=lam_col, w=w:
                           e.scalar_tensor_tensor(
                               out=natl[:, 0:w], in0=mk, scalar=lam_col,
                               in1=natl[:, 0:w], op0=ALU.mult, op1=ALU.add))
            P.waits("dve", [t_ne, t_tr_prev_nat])
            P.step("dve", lambda e, r=r, w=w: e.tensor_tensor(
                out=natx[:, 0:w], in0=nate[:, 0:w], in1=ps_scn2[r][:, 0:w],
                op=ALU.mult))
            t_An = P.step("dve", lambda e, w=w: e.tensor_tensor(
                out=natA[:, 0:w], in0=natx[:, 0:w], in1=natl[:, 0:w],
                op=ALU.mult), inc=1)
            t_natx2[r] = t_An
            t_scn_cons = t_An
            return t_An

        def nat_pe_tail(j, t_An):
            nonlocal t_tr_prev_nat
            q, sub, hasd1 = nat_pairs[j]
            r = j % 2
            nh = 2 if hasd1 else 1
            P.waits("pe", [t_An, t_ev2[r]])
            for i2 in range(nh):
                P.step("pe", lambda e, r=r, i2=i2: e.transpose(
                    out=ps_nt2[r][:, i2 * 128:(i2 + 1) * 128],
                    in_=natA[:, i2 * 128:(i2 + 1) * 128],
                    identity=cons["ident_bf16"][:, :]),
                    inc=(1 if i2 == nh - 1 else 0))
            t_tr = P.tok("pe")
            t_tr_prev_nat = t_tr
            P.waits("act", [t_tr, t_avn2[r]])
            t_ev = P.step("act", lambda e, r=r, nh=nh: e.copy(
                out=natAT[r][:, 0:nh * 128], in_=ps_nt2[r][:, 0:nh * 128]),
                inc=1)
            t_ev2[r] = t_ev
            P.waits("pe", [t_ev])
            for i2 in range(nh):
                kk = q if i2 == 0 else q - 1
                P.step("pe", lambda e, kk=kk, sub=sub, r=r, i2=i2: e.matmul(
                    ps_o[:, sub * 128:(sub + 1) * 128],
                    Vnat[:, kk * 128:(kk + 1) * 128],
                    natAT[r][:, i2 * 128:(i2 + 1) * 128],
                    start=False, stop=False), inc=(1 if i2 == nh - 1 else 0))
            t_avn = P.tok("pe")
            t_avn2[r] = t_avn

        prev = None
        for j in range(len(nat_pairs)):
            t_sb = nat_scores(j)
            t_rowread = t_sb
            if prev is not None:
                nat_pe_tail(prev[0], prev[1])
            t_An = nat_dve(j, t_sb)
            prev = (j, t_An)
        if prev is not None:
            nat_pe_tail(prev[0], prev[1])
        t_natAT_ev = t_ev2[(len(nat_pairs) - 1) % 2]
        done[0] = True
        while pend:
            flush_av()
        t_pe_prev = P.tok("pe")
        P.waits("act", [t_pe_prev, bufs.get("last_yst")])
        t_y = P.step("act", lambda e: e.copy(out=ytile[:, :], in_=ps_o), inc=1)
        t_y_ev = t_y
        bufs["last_yst"] = P.dma("yst", "sync",
                                 y_dram[h, :, c * CHK:(c + 1) * CHK], ytile[:, :],
                                 deps=[t_y])
    bufs["apply_dep"] = t_apply
    return P.tok("pe")


def build_launch_B():
    nc = bass.Bass("TRN2", target_bir_lowering=False, debug=False)
    xnin = nc.dram_tensor("xnin", [D, T], BF16, kind="ExternalInput").ap()
    WqL = nc.dram_tensor("WqL", [NH0, 128, 1024], F32, kind="ExternalInput").ap()
    WkL = nc.dram_tensor("WkL", [NH0, 128, 1024], F32, kind="ExternalInput").ap()
    WvL = nc.dram_tensor("WvL", [NH0, 128, 1024], F32, kind="ExternalInput").ap()
    WgT = nc.dram_tensor("WgT", [128, NM * 4], F32, kind="ExternalInput").ap()
    WdlT = nc.dram_tensor("WdlT", [128, NM * 48], F32, kind="ExternalInput").ap()
    bg_in = nc.dram_tensor("bg_in", [4, 1], F32, kind="ExternalInput").ap()
    lp_in = nc.dram_tensor("lp_in", [48, 1], F32, kind="ExternalInput").ap()
    mnat_in = nc.dram_tensor("mnat_in", [128, 2304], F32, kind="ExternalInput").ap()
    mst_in = nc.dram_tensor("mst_in", [128, 3 * 128], F32, kind="ExternalInput").ap()
    ident_in = nc.dram_tensor("ident_in", [128, 128], F32, kind="ExternalInput").ap()
    y_dram = nc.dram_tensor("y_dram", [NH0, 128, T], BF16, kind="ExternalOutput").ap()

    ctx = ExitStack()
    with ctx:
        E = ctx.enter_context
        P = Prog(nc, ctx)
        poolA = E(nc.sbuf_tensor("poolA", [128, 16384], F32))
        poolB = E(nc.sbuf_tensor("poolB", [128, 32768], BF16))
        poolC = E(nc.sbuf_tensor("poolC", [128, 12288], BF16))
        poolS = E(nc.sbuf_tensor("poolS", [128, 2048], F32))
        poolW = E(nc.sbuf_tensor("poolW", [128, 10240], BF16))
        ones_r = E(nc.sbuf_tensor("ones_r", [1, 128], F32))
        ident_f32 = E(nc.sbuf_tensor("ident_f32", [128, 128], F32))
        ident_bf16 = E(nc.sbuf_tensor("ident_bf16", [128, 128], BF16))
        wg_sb = E(nc.sbuf_tensor("wg_sb", [128, NM * 4], BF16))
        wdl_sb = E(nc.sbuf_tensor("wdl_sb", [128, NM * 48], BF16))
        bgs_sb = E(nc.sbuf_tensor("bgs_sb", [4, 1], F32))
        lps_sb = E(nc.sbuf_tensor("lps_sb", [48, 1], F32))
        mnat_sb = E(nc.sbuf_tensor("mnat_sb", [128, 2304], F32))
        mst_sb = E(nc.sbuf_tensor("mst_sb", [128, 3 * 128], F32))
        psF = [E(nc.psum_tensor(f"psF{i}", [128, 512], F32)) for i in range(6)]
        psB = [E(nc.psum_tensor(f"psB{i}", [128, 1024], BF16)) for i in range(2)]
        blk = E(nc.Block())

        cons = dict(ones_r=ones_r, ident_f32=ident_f32, ident_bf16=ident_bf16,
                    wg_sb=wg_sb, wdl_sb=wdl_sb, bgs_sb=bgs_sb, lps_sb=lps_sb,
                    mnat_sb=mnat_sb, mst_sb=mst_sb,
                    WqL_ap=[WqL[hh, :, :] for hh in range(NH0)],
                    WkL_ap=[WkL[hh, :, :] for hh in range(NH0)],
                    WvL_ap=[WvL[hh, :, :] for hh in range(NH0)])

        t_init = P.step("dve", lambda e: e.memset(ones_r[:, :], 1.0), inc=1)
        P.wait("pe", t_init)
        tc = [
            P.dma("cns", "sync", ident_f32[:, :], ident_in),
            P.dma("cns", "sync", bgs_sb[:, :], bg_in),
            P.dma("cns", "sync", lps_sb[:, :], lp_in),
            P.dma("cns", "sync", mnat_sb[:, :], mnat_in),
            P.dma("cns", "sync", mst_sb[:, :], mst_in),
            P.dma("cnsg", "gp", ident_bf16[:, :], ident_in),
            P.dma("cnsg", "gp", wg_sb[:, :], WgT),
            P.dma("cnsg", "gp", wdl_sb[:, :], WdlT),
        ]
        xn = [poolB[:, m * T:(m + 1) * T] for m in range(NM)]
        t_xn = [P.dma("xnf", "sync", xn[m], xnin[m * 128:(m + 1) * 128, :])
                for m in range(NM)]
        for e in ("pe", "act", "dve", "gp"):
            P.waits(e, [t_xn[-1], tc[-1], P.dtok("cns")])

        bufs = dict(
            attn_rows=poolA[:, 0:4096],
            lam_nat=poolA[:, 4096:5632],
            rowpool=poolA[64:65, 11648:15360],
            Rsb=poolA[:, 5632:8704],
            exin=poolA[:, 8704:9216],
            expt=poolA[:, 9216:9728],
            exin_b=poolA[:, 9728:10240],
            expt_b=poolA[:, 10240:10752],
            exin_c=poolA[:, 15360:15872],
            expt_c=poolA[:, 15872:16384],
            natx=poolA[:, 10752:11008],
            nate=poolA[:, 11008:11264],
            natl=poolA[:, 11264:11520],
            dmx=poolA[:, 11520:11648],
            sp_scr=[poolA[:, 11648:12160][0:48, :], poolA[:, 12160:12672][0:48, :]],
            spg_scr=poolA[96:100, 12672:13184],
            SPcol=poolS[:, 0:128],
            negSPcol=poolS[:, 128:256],
            QT=poolC[:, 0:4096],
            KT=poolC[:, 4096:8192],
            Vnat=poolC[:, 8192:12288],
            vtmp=poolW[:, 2048:2560],
            A_sb=[poolW[:, 0:512], poolW[:, 512:1024], poolW[:, 2048:2560]],
            ytile=poolW[:, 1024:1536],
            natA=poolW[:, 1536:1792],
            wq=poolW[:, 2560:3584],
            wk=poolW[:, 3584:4608],
            wv=poolW[:, 4608:5632],
            stateT32=poolS[:, 256:384],
            gcol=poolS[:, 384:388],
            mccol=poolS[:, 388:389],
            dmcol=poolS[:, 389:390],
            frow=poolS[0:1, 390:902],
            spbnd=poolS[64:65, 902:903],
            tmp11=poolS[64:65, 903:904],
            ones96=poolS[64:65, 904:1032],
            stateT=poolW[:, 5632:5760],
            knat=poolW[:, 5760:6272],
            vsc=poolW[:, 6272:6784],
            qsc=poolW[:, 6784:7296],
            natAT=[poolW[:, 7296:7552], poolW[:, 7552:7808]],
        )
        t_prep = emit_l0_prep(P, xn, bufs, psF, cons)
        P.barrier_all()
        import os
        nheads = int(os.environ.get("L0_NHEADS", str(NH0)))
        for h in range(nheads):
            tk = emit_l0_head(P, h, xn, y_dram, bufs, psF, psB, cons)
            bufs["head_dep"] = tk
        P.wait("sync", P.dtok("yst"))
        P.finalize(blk)
    return nc


def prep_B_inputs(inp):
    Wq = np.asarray(inp["L0_attn_Wq"])
    Wk = np.asarray(inp["L0_attn_Wk"])
    Wv = np.asarray(inp["L0_attn_Wv"])
    Wg = np.asarray(inp["L0_attn_Wg"])
    bg = np.asarray(inp["L0_attn_bg"])
    Wdl = np.asarray(inp["L0_attn_Wdl"])
    Lp = np.asarray(inp["L0_attn_Lparam"])
    ident = np.eye(128, dtype=np.float32)

    def wslice(W, hbase, hd):
        Ws = W.reshape(8 * hd // hd, hd, NM, 128)
        Ws = W.reshape(8, hd, NM, 128)
        out = Ws[hbase:hbase + NH0].transpose(0, 3, 2, 1).reshape(NH0, 128, NM * hd)
        return np.ascontiguousarray(out)

    pp = np.arange(128)[:, None]
    ff = np.arange(128)[None, :]

    def lvl_of(d):
        d = np.asarray(d)
        with np.errstate(divide="ignore"):
            l = np.where(d <= 0, 0,
                         np.floor(np.log2(np.maximum(d, 1))).astype(np.int64) + 1)
        return np.minimum(l, 11)

    mnat = np.zeros((128, 9 * 256), np.float32)
    d0 = pp - ff
    d1 = 128 + pp - ff
    for l in range(9):
        if l <= 7:
            mnat[:, l * 256:l * 256 + 128] = ((d0 >= 0) & (lvl_of(d0) == l))
        if l >= 1:
            mnat[:, l * 256 + 128:(l + 1) * 256] = (lvl_of(d1) == l)
    mst = np.zeros((128, 3 * 128), np.float32)
    for i in range(3):
        mst[:, i * 128:(i + 1) * 128] = (ff < pp)

    in_maps = []
    for c in range(8):
        par = c % 2
        hbase = par * NH0
        wg = np.ascontiguousarray(
            Wg[hbase:hbase + NH0].reshape(NH0, NM, 128).transpose(2, 1, 0)
            .reshape(128, NM * NH0))
        wdl = np.ascontiguousarray(
            Wdl.reshape(8, 12, NM, 128)[hbase:hbase + NH0]
            .transpose(3, 2, 0, 1).reshape(128, NM * 48))
        in_maps.append({
            "WqL": wslice(Wq, hbase, 128), "WkL": wslice(Wk, hbase, 128),
            "WvL": wslice(Wv, hbase, 128),
            "WgT": wg, "WdlT": wdl,
            "bg_in": np.ascontiguousarray(bg[hbase:hbase + NH0][:, None]),
            "lp_in": np.ascontiguousarray(
                Lp[hbase:hbase + NH0].reshape(48)[:, None]),
            "mnat_in": mnat, "mst_in": mst, "ident_in": ident,
        })
    return in_maps



def emit_ln_stats_apply(P, src_tiles, cs_w, bufs, psF, cons, out_write,
                        affine=None, eps=LN_EPS):
    ones_c, ones_r = cons["ones_c"], cons["ones_r"]
    xsq = bufs["xsq"]
    mu, msq, var, rstd, mur = bufs["rows"]
    ps_stat, ps_stat2 = psF[0][0:1, 0:cs_w], psF[1][0:1, 0:cs_w]
    ps_b1, ps_b2 = psF[2][:, 0:cs_w], psF[3][:, 0:cs_w]

    for m in range(NM):
        P.step("pe", lambda e, m=m: e.matmul(
            ps_stat, ones_c[:, :], src_tiles[m],
            start=(m == 0), stop=(m == NM - 1)), inc=(1 if m == NM - 1 else 0))
    t_sum = P.tok("pe")
    last_sq_mm = [None, None]
    for m in range(NM):
        P.waits("act", [last_sq_mm[m % 2]])
        t_sq = P.step("act", lambda e, m=m: e.activation(
            out=xsq[m % 2], in_=src_tiles[m], func=AF.Square), inc=1)
        P.wait("pe", t_sq)
        last_sq_mm[m % 2] = P.step("pe", lambda e, m=m: e.matmul(
            ps_stat2, ones_c[:, :], xsq[m % 2][:, 0:cs_w],
            start=(m == 0), stop=(m == NM - 1)), inc=1)
    t_sq2 = P.tok("pe")
    P.waits("dve", [t_sum, t_sq2])
    P.step("dve", lambda e: e.tensor_scalar_mul(out=mu[0:1, 0:cs_w], in0=ps_stat,
                                                scalar1=1.0 / D))
    P.step("dve", lambda e: e.tensor_scalar_mul(out=msq[0:1, 0:cs_w], in0=ps_stat2,
                                                scalar1=1.0 / D))
    P.step("dve", lambda e: e.tensor_mul(out=var[0:1, 0:cs_w], in0=mu[0:1, 0:cs_w],
                                         in1=mu[0:1, 0:cs_w]))
    t_var = P.step("dve", lambda e: e.tensor_sub(
        out=var[0:1, 0:cs_w], in0=msq[0:1, 0:cs_w], in1=var[0:1, 0:cs_w]), inc=1)
    P.wait("act", t_var)
    P.step("act", lambda e: e.activation(out=rstd[0:1, 0:cs_w],
                                         in_=var[0:1, 0:cs_w], func=AF.Ln,
                                         bias=cons["eps_b"][0:1, 0:1]))
    t_rstd = P.step("act", lambda e: e.activation(out=rstd[0:1, 0:cs_w],
                                                  in_=rstd[0:1, 0:cs_w],
                                                  func=AF.Exp, scale=-0.5), inc=1)
    P.wait("dve", t_rstd)
    t_mur = P.step("dve", lambda e: e.tensor_mul(
        out=mur[0:1, 0:cs_w], in0=mu[0:1, 0:cs_w], in1=rstd[0:1, 0:cs_w]), inc=1)
    P.waits("pe", [t_rstd, t_mur])
    P.step("pe", lambda e: e.matmul(ps_b1, ones_r[:, :], rstd[0:1, 0:cs_w],
                                    start=True, stop=True))
    t_bc = P.step("pe", lambda e: e.matmul(ps_b2, ones_r[:, :], mur[0:1, 0:cs_w],
                                           start=True, stop=True), inc=1)
    P.wait("dve", t_bc)
    toks = []
    for m in range(NM):
        toks.append(out_write(m, ps_b1, ps_b2))
    return toks[-1]


def build_launch_C():
    nc = bass.Bass("TRN2", target_bir_lowering=False, debug=False)
    yT_in = nc.dram_tensor("yT_in", [NM, 128, T], BF16, kind="ExternalInput").ap()
    xn0_in = nc.dram_tensor("xn0_in", [D, T], BF16, kind="ExternalInput").ap()
    moe_in = nc.dram_tensor("moe_in", [D, TO], F32, kind="ExternalInput").ap()
    x0T = nc.dram_tensor("x0T", [D, T], F32, kind="ExternalInput").ap()
    WoL = nc.dram_tensor("WoL", [NM, 128, NM * 128], F32, kind="ExternalInput").ap()
    lngb = nc.dram_tensor("lngb", [128, 2 * NM], F32, kind="ExternalInput").ap()
    par_in = nc.dram_tensor("par_in", [1, 1], I32, kind="ExternalInput").ap()
    ident_in = nc.dram_tensor("ident_in", [128, 128], F32, kind="ExternalInput").ap()

    x1_out = nc.dram_tensor("x1_out", [D, TO], F32, kind="ExternalOutput").ap()
    xn1_out = nc.dram_tensor("xn1_out", [D, TO], BF16, kind="ExternalOutput").ap()
    sdbg_out = nc.dram_tensor("sdbg_out", [D, TO], F32, kind="ExternalOutput").ap()
    s2dbg_out = nc.dram_tensor("s2dbg_out", [D, TO], BF16, kind="ExternalOutput").ap()

    ctx = ExitStack()
    with ctx:
        E = ctx.enter_context
        P = Prog(nc, ctx)
        poolA = E(nc.sbuf_tensor("poolA", [128, 16384], F32))
        poolB = E(nc.sbuf_tensor("poolB", [128, 32768], BF16))
        poolC = E(nc.sbuf_tensor("poolC", [128, 16384], BF16))
        poolS = E(nc.sbuf_tensor("poolS", [128, 5120], F32))
        ones_c = E(nc.sbuf_tensor("ones_c", [128, 1], F32))
        ones_r = E(nc.sbuf_tensor("ones_r", [1, 128], F32))
        eps_b = E(nc.sbuf_tensor("eps_b", [1, 1], F32))
        lngb_sb = E(nc.sbuf_tensor("lngb_sb", [128, 2 * NM], F32))
        par_sb = E(nc.sbuf_tensor("par_sb", [1, 1], I32))
        psF = [E(nc.psum_tensor(f"psF{i}", [128, 512], F32)) for i in range(6)]
        blk = E(nc.Block())

        cons = dict(ones_c=ones_c, ones_r=ones_r, eps_b=eps_b)

        P.step("dve", lambda e: e.memset(ones_c[:, :], 1.0))
        P.step("dve", lambda e: e.memset(eps_b[:, :], LN_EPS))
        t_init = P.step("dve", lambda e: e.memset(ones_r[:, :], 1.0), inc=1)
        P.wait("pe", t_init)
        P.wait("act", t_init)
        emit_load_parity(P, par_in, par_sb, engines=("gp",))
        tcn = P.dma("cns", "sync", lngb_sb[:, :], lngb)

        yT = [poolB[:, ht * TO:(ht + 1) * TO] for ht in range(NM)]

        def go_y(e, sem):
            par = P.holder["par"]
            with e.If_eq(par, 0):
                for ht in range(NM):
                    e.dma_start(out=yT[ht], in_=yT_in[ht, :, 0:TO]).then_inc(sem, 16)
            with e.Else():
                for ht in range(NM):
                    e.dma_start(out=yT[ht], in_=yT_in[ht, :, TO:T]).then_inc(sem, 16)
        t_y = P.raw("yld", "gp", go_y, inc=16 * NM)
        xn_own = [poolC[:, m * TO:(m + 1) * TO] for m in range(NM)]
        t_xn = emit_read_own(P, xn0_in, xn_own)

        s_pre = [poolA[:, m * TO:(m + 1) * TO] for m in range(NM)]
        s_sb = [poolB[:, 16384 + m * TO: 16384 + (m + 1) * TO] for m in range(NM)]
        ln_bufs = dict(
            xsq=[poolS[:, 0:512], poolS[:, 512:1024]],
            rows=[poolS[0:1, 1024 + i * 512: 1024 + (i + 1) * 512] for i in range(5)],
        )
        g_sb = poolS[:, 4608:5120]

        NCH = TO // CHK
        t_spre = None
        woT = [poolB[:, 24576 + ht * 1024: 24576 + (ht + 1) * 1024]
               for ht in range(NM)]
        t_wo = [P.dma("wold", "gp", woT[ht], WoL[ht, :, :]) for ht in range(NM)]

        t_soff = None
        for c in range(NCH):
            cs = slice(c * CHK, (c + 1) * CHK)
            for mt in range(NM):
                P.waits("pe", [t_y, t_wo[-1], t_spre])
                for ht in range(NM):
                    P.step("pe", lambda e, ht=ht, mt=mt, cs=cs: e.matmul(
                        psF[0][:, :], woT[ht][:, mt * 128:(mt + 1) * 128],
                        yT[ht][:, cs], start=(ht == 0), stop=(ht == NM - 1)),
                        inc=(1 if ht == NM - 1 else 0))
                t_mm = P.tok("pe")
                P.waits("dve", [t_mm, t_xn])
                t_spre = P.step("dve", lambda e, mt=mt, cs=cs: e.tensor_tensor(
                    out=s_pre[mt][:, cs], in0=psF[0][:, :], in1=xn_own[mt][:, cs],
                    op=ALU.add), inc=1)
        P.barrier_all()
        for m in range(NM):
            P.dma("sdbg", "sync", sdbg_out[m * 128:(m + 1) * 128, :], s_pre[m])
        P.barrier_all()

        for c in range(NCH):
            cs = slice(c * CHK, (c + 1) * CHK)

            def write_s(m, ps_b1, ps_b2, cs=cs):
                P.step("dve", lambda e, m=m: e.tensor_mul(
                    out=ln_bufs["xsq"][m % 2], in0=s_pre[m][:, cs], in1=ps_b1))
                P.step("dve", lambda e, m=m: e.tensor_sub(
                    out=ln_bufs["xsq"][m % 2], in0=ln_bufs["xsq"][m % 2], in1=ps_b2))
                return P.step("dve", lambda e, m=m: e.tensor_scalar(
                    out=s_sb[m][:, cs], in0=ln_bufs["xsq"][m % 2],
                    scalar1=lngb_sb[:, m:m + 1], scalar2=lngb_sb[:, NM + m:NM + m + 1],
                    op0=ALU.mult, op1=ALU.add), inc=1)
            P.wait("dve", tcn)
            src = [s_pre[m][:, cs] for m in range(NM)]
            t_ln2 = emit_ln_stats_apply(P, src, CHK, ln_bufs, psF, cons, write_s)
            P.barrier_all()

            x1c = [poolA[:, m * TO + c * CHK: m * TO + (c + 1) * CHK]
                   for m in range(NM)]
            t_g = None
            for m in range(NM):
                t_moe = P.dma("moeld", "sync", poolS[:, 4096:4608],
                              moe_in[m * 128:(m + 1) * 128, cs], deps=[t_g])

                def go_x0(e, sem, m=m, cs=cs, c=c):
                    par = P.holder["par"]
                    with e.If_eq(par, 0):
                        e.dma_start(out=poolS[:, 3584:4096],
                                    in_=x0T[m * 128:(m + 1) * 128, cs]).then_inc(sem, 16)
                    with e.Else():
                        e.dma_start(out=poolS[:, 3584:4096],
                                    in_=x0T[m * 128:(m + 1) * 128,
                                            TO + c * CHK:TO + (c + 1) * CHK]
                                    ).then_inc(sem, 16)
                t_x0 = P.raw("x0ld", "gp", go_x0, inc=16)
                P.waits("act", [t_moe])
                t_g = P.step("act", lambda e: e.activation(
                    out=g_sb, in_=poolS[:, 4096:4608], func=AF.Silu), inc=1)
                P.waits("dve", [t_g, t_ln2, t_x0])
                P.step("dve", lambda e, m=m, cs=cs: e.tensor_mul(
                    out=g_sb, in0=g_sb, in1=s_sb[m][:, cs]))
                t_x1 = P.step("dve", lambda e, m=m, x1c=x1c: e.tensor_tensor(
                    out=x1c[m], in0=poolS[:, 3584:4096], in1=g_sb, op=ALU.add),
                    inc=1)
                P.dma("x1st", "sync", x1_out[m * 128:(m + 1) * 128, cs], x1c[m],
                      deps=[t_x1])
                P.waits("gp", [t_x1])
                P.waits("sync", [t_x1])
            P.barrier_all()
            xn1c = [poolC[:, m * TO + c * CHK: m * TO + (c + 1) * CHK]
                    for m in range(NM)]

            def write_xn1(m, ps_b1, ps_b2, cs=cs, x1c=x1c, xn1c=xn1c):
                P.step("dve", lambda e, m=m, x1c=x1c: e.tensor_mul(
                    out=ln_bufs["xsq"][m % 2], in0=x1c[m], in1=ps_b1))
                t = P.step("dve", lambda e, m=m, xn1c=xn1c: e.tensor_sub(
                    out=xn1c[m], in0=ln_bufs["xsq"][m % 2], in1=ps_b2), inc=1)
                P.dma("xn1st", "sync", xn1_out[m * 128:(m + 1) * 128, cs],
                      xn1c[m], deps=[t])
                return t
            t_ln1 = emit_ln_stats_apply(P, x1c, CHK, ln_bufs, psF, cons, write_xn1)
            P.barrier_all()
        for m in range(NM):
            P.dma("s2dbg", "sync", s2dbg_out[m * 128:(m + 1) * 128, :], s_sb[m])
        P.wait("sync", P.dtok("s2dbg"))
        P.wait("sync", P.dtok("x1st"))
        P.wait("sync", P.dtok("xn1st"))
        P.finalize(blk)
    return nc


def prep_C_inputs(inp):
    Wo = np.asarray(inp["L0_attn_Wo"])
    lng = np.asarray(inp["L0_attn_ln_g"])
    lnb = np.asarray(inp["L0_attn_ln_b"])
    WoL = np.ascontiguousarray(
        Wo.reshape(NM, 128, NM, 128).transpose(3, 1, 2, 0).reshape(NM, 128, NM * 128))
    WoL2 = np.zeros((NM, 128, NM * 128), np.float32)
    WoT = Wo.T
    for hdt in range(NM):
        for mt in range(NM):
            WoL2[hdt, :, mt * 128:(mt + 1) * 128] = WoT[hdt * 128:(hdt + 1) * 128,
                                                        mt * 128:(mt + 1) * 128]
    lngb = np.zeros((128, 2 * NM), np.float32)
    for m in range(NM):
        lngb[:, m] = lng[m * 128:(m + 1) * 128]
        lngb[:, NM + m] = lnb[m * 128:(m + 1) * 128]
    ident = np.eye(128, dtype=np.float32)
    base = {"WoL": WoL2, "lngb": lngb, "ident_in": ident}
    return base



NH1 = 8


def emit_l1_pair(P, hp, xn, y_dram, bufs, psF, psB, cons):
    wq, wk, wv = bufs["wslots"][hp % 2]
    QT, KT, VT, Vn = bufs["QT"], bufs["KT"], bufs["VT"], bufs["Vn"]
    pc = bufs["pc"]
    vh = bufs["vh"]
    A_sb = bufs["A_sb"]
    of32 = bufs["of32"]
    zf = bufs["zf"]
    y1t = bufs["y1t"]
    zr_sb, tri = cons["zr_sb"], cons["tri_sb"]
    ident_bf16 = cons["ident_bf16"]
    ones_r = cons["ones_r"]

    ps_scL = [psF[0][:, :], psF[1][:, :], psF[3][:, :], psF[4][:, :]]
    ps_oz = psF[2][0:65, :]
    ps_o = psF[2][0:64, :]
    ps_z = psF[2][64:65, :]
    ps_rb = psF[5][0:64, :]
    ps_tr = psB[1][:, :]
    nA = len(A_sb)
    nB = len(ps_scL)
    Vn3 = Vn.rearrange("p (k c) -> p k c", c=65)
    ps_tr3 = ps_tr.rearrange("p (k c) -> p k c", c=64)

    hd = bufs.get("head_dep")
    tok_w = bufs.get("wpre_tok")
    if tok_w is None:
        P.dma("wqkv", "gp", wq[:, :], cons["Wq1L_ap"][hp], deps=[hd])
        P.dma("wqkv", "gp", wk[:, :], cons["Wk1L_ap"][hp])
        tok_w = P.dma("wqkv", "gp", wv[:, :], cons["Wv1L_ap"][hp])
    t_ld = [tok_w]
    evs = []
    fi = 0
    for c in range(NCHK):
        for which, (wsb, dst) in enumerate([(wq, QT), (wk, KT), (wv, VT)]):
            b2 = fi % 4
            P.waits("pe", [t_ld[-1] if fi == 0 else None,
                           evs[-4] if len(evs) >= 4 else None])
            for m in range(NM):
                P.step("pe", lambda e, wsb=wsb, m=m, c=c, b2=b2: e.matmul(
                    ps_scL[b2], wsb[:, m * 128:(m + 1) * 128],
                    xn[m][:, c * CHK:(c + 1) * CHK],
                    start=(m == 0), stop=(m == NM - 1)),
                    inc=(1 if m == NM - 1 else 0))
            t_mm = P.tok("pe")
            P.wait("act", t_mm)
            t_ev = P.step("act", lambda e, dst=dst, c=c, b2=b2: e.copy(
                out=dst[:, c * CHK:(c + 1) * CHK], in_=ps_scL[b2]), inc=1)
            evs.append(t_ev)
            fi += 1
    t_ev = evs[-1]
    P.waits("dve", [t_ev])
    t_vn_ms = P.step("dve", lambda e: e.memset(Vn[:, :], 1.0), inc=1)
    P.step("dve", lambda e: e.tensor_scalar_mul(out=vh[:, :], in0=VT[:, :],
                                                scalar1=0.5))
    t_pc = P.step("dve", lambda e: e.tensor_tensor_scan(
        out=pc[:, :], data0=vh[:, :], data1=vh[:, :], initial=0.0,
        op0=ALU.add, op1=ALU.add), inc=1)
    if hp < 3:
        nxt = bufs["wslots"][(hp + 1) % 2]
        P.dma("wqkv", "gp", nxt[0][:, :], cons["Wq1L_ap"][hp + 1], deps=[t_pc])
        P.dma("wqkv", "gp", nxt[1][:, :], cons["Wk1L_ap"][hp + 1])
        bufs["wpre_tok"] = P.dma("wqkv", "gp", nxt[2][:, :],
                                 cons["Wv1L_ap"][hp + 1])
    else:
        bufs["wpre_tok"] = None

    t_A_cons = [None] * nA
    t_sc_free = [None] * nB
    t_v_ev = bufs.get("ps_tr_dep")
    t_vn = [None, None]
    epi = [None]
    t_yst = [bufs.get("yst_dep")]

    def flush_epi():
        t_z0, hl0, c0 = epi[0]
        epi[0] = None
        P.waits("pe", [t_z0])
        t_rb = P.step("pe", lambda e: e.matmul(
            ps_rb, ones_r[0:1, 0:64], zf[:, :], start=True, stop=True), inc=1)
        P.waits("dve", [t_rb, t_yst[0]])
        t_y = P.step("dve", lambda e: e.tensor_tensor(
            out=y1t[:, :], in0=of32[:, :], in1=ps_rb, op=ALU.mult), inc=1)
        P.dma("yst", "sync",
              y_dram[hp * 2 + hl0, :, c0 * CHK:(c0 + 1) * CHK],
              y1t[:, :], deps=[t_y])
        t_yst[0] = P.dtok("yst")

    for hl in range(2):
        hb = hl * 64
        if epi[0] is not None:
            flush_epi()
        for grp in range(2):
            P.waits("pe", [t_ev, t_v_ev])
            for j in range(16):
                k = grp * 16 + j
                P.step("pe", lambda e, k=k, j=j, hb=hb: e.transpose(
                    out=ps_tr[:, j * 64:(j + 1) * 64],
                    in_=VT[hb:hb + 64, k * 128:(k + 1) * 128],
                    identity=ident_bf16[hb:hb + 64, hb:hb + 64]),
                    inc=(1 if j == 15 else 0))
            t_tr = P.tok("pe")
            P.waits("dve", [t_tr, t_vn_ms])
            t_v_ev = P.step("dve", lambda e, grp=grp: e.tensor_copy(
                out=Vn3[:, grp * 16:(grp + 1) * 16, 0:64],
                in_=ps_tr3[:, :, :]), inc=1)
        t_vn[hl] = t_v_ev
        t_z_prev = [None]
        for c in range(NCHK):
            nk = 4 * c + 4
            cs = slice(c * CHK, (c + 1) * CHK)
            pend = []
            first_av = [True]

            def flush_av(hl=hl):
                k2, slot2, tA2 = pend.pop(0)
                first = first_av[0]
                last = (len(pend) == 0)
                first_av[0] = False
                P.waits("pe", [tA2, t_z_prev[0] if first else None,
                               t_vn[hl] if first else None, t_pc if first else None])
                t_av = P.step("pe", lambda e, k2=k2, first=first, last=last:
                              e.matmul(ps_oz, Vn[:, k2 * 65:(k2 + 1) * 65],
                                       A_sb[slot2], start=first, stop=last),
                              inc=1)
                t_A_cons[slot2] = t_av
                return t_av

            for idx, k in enumerate(range(nk - 1, -1, -1)):
                b = idx % nB
                slot = idx % nA
                P.waits("pe", [t_sc_free[b]])
                t_s = P.step("pe", lambda e, k=k, cs=cs, hb=hb, b=b: e.matmul(
                    ps_scL[b], KT[hb:hb + 64, k * 128:(k + 1) * 128],
                    QT[hb:hb + 64, cs], start=True, stop=True), inc=1)
                if idx == 2 and epi[0] is not None:
                    flush_epi()
                P.waits("act", [t_s, t_A_cons[slot]])
                t_sq = P.step("act", lambda e, b=b, slot=slot: e.activation(
                    out=A_sb[slot], in_=ps_scL[b], func=AF.Square,
                    scale=0.125, bias=1.0), inc=1)
                t_sc_free[b] = t_sq
                t_A = t_sq
                for sub in range(4):
                    d = 4 * c + sub - k
                    if d == 0:
                        P.waits("dve", [t_sq])
                        t_A = P.step("dve", lambda e, slot=slot, sub=sub:
                                     e.tensor_tensor(
                            out=A_sb[slot][:, sub * 128:(sub + 1) * 128],
                            in0=A_sb[slot][:, sub * 128:(sub + 1) * 128],
                            in1=tri[:, :], op=ALU.mult), inc=1)
                    elif d < 0:
                        P.waits("dve", [t_sq])
                        t_A = P.step("dve", lambda e, slot=slot, sub=sub: e.memset(
                            A_sb[slot][:, sub * 128:(sub + 1) * 128], 0.0), inc=1)
                pend.append((k, slot, t_A))
                if idx >= 3:
                    flush_av()
            while pend:
                flush_av()
            t_pe_done = P.tok("pe")
            P.waits("dve", [t_pe_done])
            P.step("dve", lambda e, hb=hb, cs=cs: e.tensor_tensor(
                out=of32[:, :], in0=ps_o, in1=pc[hb:hb + 64, cs], op=ALU.add))
            P.step("dve", lambda e, cs=cs: e.tensor_tensor(
                out=zf[:, :], in0=ps_z, in1=zr_sb[0:1, cs], op=ALU.add))
            t_z = P.step("dve", lambda e: e.reciprocal(out=zf[:, :], in_=zf[:, :]),
                         inc=1)
            t_z_prev[0] = t_z
            epi[0] = (t_z, hl, c)
    flush_epi()
    bufs["ps_tr_dep"] = t_v_ev
    bufs["yst_dep"] = t_yst[0]
    return P.tok("pe")


def build_launch_D():
    nc = bass.Bass("TRN2", target_bir_lowering=False, debug=False)
    xnin = nc.dram_tensor("xnin", [D, T], BF16, kind="ExternalInput").ap()
    Wq1L = nc.dram_tensor("Wq1L", [4, 128, 1024], F32, kind="ExternalInput").ap()
    Wk1L = nc.dram_tensor("Wk1L", [4, 128, 1024], F32, kind="ExternalInput").ap()
    Wv1L = nc.dram_tensor("Wv1L", [4, 128, 1024], F32, kind="ExternalInput").ap()
    zr_in = nc.dram_tensor("zr_in", [1, T], F32, kind="ExternalInput").ap()
    tri_in = nc.dram_tensor("tri_in", [128, 128], F32, kind="ExternalInput").ap()
    ident_in = nc.dram_tensor("ident_in", [128, 128], F32, kind="ExternalInput").ap()
    y_dram = nc.dram_tensor("y_dram", [NH1, 64, T], BF16, kind="ExternalOutput").ap()

    ctx = ExitStack()
    with ctx:
        E = ctx.enter_context
        P = Prog(nc, ctx)
        poolA = E(nc.sbuf_tensor("poolA", [128, 8192], F32))
        poolB = E(nc.sbuf_tensor("poolB", [128, 32768], BF16))
        poolC = E(nc.sbuf_tensor("poolC", [128, 16384], BF16))
        poolW = E(nc.sbuf_tensor("poolW", [128, 10240], BF16))
        ones_r = E(nc.sbuf_tensor("ones_r", [1, 128], F32))
        ones_cb = E(nc.sbuf_tensor("ones_cb", [128, 1], BF16))
        ident_bf16 = E(nc.sbuf_tensor("ident_bf16", [128, 128], BF16))
        tri_sb = E(nc.sbuf_tensor("tri_sb", [128, 128], BF16))
        zr_sb = E(nc.sbuf_tensor("zr_sb", [1, T], F32))
        psF = [E(nc.psum_tensor(f"psF{i}", [128, 512], F32)) for i in range(6)]
        psB = [E(nc.psum_tensor(f"psB{i}", [128, 1024], BF16)) for i in range(2)]
        blk = E(nc.Block())

        cons = dict(ones_r=ones_r, ones_cb=ones_cb, ident_bf16=ident_bf16,
                    tri_sb=tri_sb, zr_sb=zr_sb,
                    Wq1L_ap=[Wq1L[i, :, :] for i in range(4)],
                    Wk1L_ap=[Wk1L[i, :, :] for i in range(4)],
                    Wv1L_ap=[Wv1L[i, :, :] for i in range(4)])

        P.step("dve", lambda e: e.memset(ones_cb[:, :], 1.0))
        t_init = P.step("dve", lambda e: e.memset(ones_r[:, :], 1.0), inc=1)
        P.wait("pe", t_init)
        tc = [
            P.dma("cns", "sync", zr_sb[:, :], zr_in),
            P.dma("cnsg", "gp", ident_bf16[:, :], ident_in),
            P.dma("cnsg", "gp", tri_sb[:, :], tri_in),
        ]
        xn = [poolB[:, m * T:(m + 1) * T] for m in range(NM)]
        t_xn = [P.dma("xnf", "sync", xn[m], xnin[m * 128:(m + 1) * 128, :])
                for m in range(NM)]
        for e in ("pe", "act", "dve", "gp"):
            P.waits(e, [t_xn[-1], tc[0], P.dtok("cnsg")])

        bufs = dict(
            QT=poolC[:, 0:4096], KT=poolC[:, 4096:8192], VT=poolC[:, 8192:12288],
            Vn=poolC[:, 12288:14368],
            pc=poolA[:, 0:4096],
            of32=poolA[0:64, 4096:4608],
            zf=poolA[0:1, 4608:5120],
            vh=poolW[:, 0:4096],
            A_sb=[poolW[:, 4096:4608], poolW[:, 4608:5120],
                  poolW[:, 8704:9216], poolW[:, 9216:9728]],
            y1t=poolW[0:64, 5120:5632],
            wq=poolW[:, 5632:6656],
            wk=poolW[:, 6656:7680],
            wv=poolW[:, 7680:8704],
            wslots=[[poolW[:, 5632:6656], poolW[:, 6656:7680],
                     poolW[:, 7680:8704]],
                    [poolW[:, 0:1024], poolW[:, 1024:2048],
                     poolW[:, 2048:3072]]],
        )
        for hp in range(4):
            tk = emit_l1_pair(P, hp, xn, y_dram, bufs, psF, psB, cons)
            bufs["head_dep"] = tk
        P.wait("sync", P.dtok("yst"))
        P.finalize(blk)
    return nc


def prep_D_inputs(inp):
    Wq = np.asarray(inp["L1_attn_Wq"])
    Wk = np.asarray(inp["L1_attn_Wk"])
    Wv = np.asarray(inp["L1_attn_Wv"])

    def wslice1(W, hbase):
        Ws = W.reshape(16, 64, NM, 128)[hbase:hbase + 8]
        out = np.zeros((4, 128, NM * 128), np.float32)
        for hp in range(4):
            pair = Ws[hp * 2:hp * 2 + 2].reshape(128, NM, 128)
            out[hp] = pair.transpose(2, 1, 0).reshape(128, NM * 128)
        return np.ascontiguousarray(out)

    pp = np.arange(128)[:, None]
    ff = np.arange(128)[None, :]
    tri = (ff >= pp).astype(np.float32)
    zr = (np.arange(T) + 1 + 2e-6).astype(np.float32)[None, :]
    ident = np.eye(128, dtype=np.float32)

    in_maps = []
    for c in range(8):
        par = c % 2
        hbase = par * 8
        in_maps.append({
            "Wq1L": wslice1(Wq, hbase), "Wk1L": wslice1(Wk, hbase),
            "Wv1L": wslice1(Wv, hbase),
            "zr_in": zr, "tri_in": tri, "ident_in": ident,
        })
    return in_maps



NI = 2816
NIT = NI // 128


def emit_quant(P, src_ap, dst_ap, W, scr, deps=(), eng="dve"):
    NB = W // 16
    y, f1, f2 = scr["y"][:, 0:W], scr["f1"][:, 0:W], scr["f2"][:, 0:W]
    bm, t1s, rr = scr["bm"][:, 0:NB], scr["t1"][:, 0:NB], scr["r"][:, 0:NB]
    P.waits("dve", list(deps) + [scr.get("_t_g")])
    P.step("dve", lambda e: e.tensor_reduce(
        out=bm.rearrange("p (b o) -> p b o", o=1),
        in_=src_ap.rearrange("p (b i) -> p b i", i=16),
        axis=AX.X, op=ALU.max, apply_absolute_value=True))
    P.step("dve", lambda e: e.tensor_scalar_max(out=t1s, in0=bm, scalar1=6e-12))
    P.step("dve", lambda e: e.reciprocal(out=rr, in_=t1s))
    P.step("dve", lambda e: e.tensor_scalar_mul(out=rr, in0=rr, scalar1=6.0))

    def bcast_mul(e, out, in0, in1_small):
        a3 = in0.rearrange("p (b i) -> p b i", i=16)
        b3 = in1_small.rearrange("p (b o) -> p b o", o=1)
        a3b, b3b = bass.broadcast_tensor_aps(a3, b3)
        return e.tensor_tensor(out=out.rearrange("p (b i) -> p b i", i=16),
                               in0=a3b, in1=b3b, op=ALU.mult)
    t_y = P.step("dve", lambda e: bcast_mul(e, y, src_ap, rr), inc=1)
    P.waits(eng, [t_y, scr.get("_t_dst")])
    IW = scr["ints"].shape[1] if hasattr(scr["ints"], "shape") else 512
    IW = int(IW)
    for o in range(0, W, IW):
        w2 = min(IW, W - o)
        P.step(eng, lambda e, o=o, w2=w2: e.tensor_scalar(
            out=scr["ints"][:, 0:w2], in0=y[:, o:o + w2], scalar1=2.0,
            scalar2=4.0, op0=ALU.mult, op1=ALU.min))
        P.step(eng, lambda e, o=o, w2=w2: e.tensor_scalar(
            out=f1[:, o:o + w2], in0=scr["ints"][:, 0:w2], scalar1=0.5,
            scalar2=-2.0, op0=ALU.mult, op1=ALU.max))
        P.step(eng, lambda e, o=o, w2=w2: e.tensor_scalar(
            out=scr["ints"][:, 0:w2], in0=y[:, o:o + w2], scalar1=4.0,
            scalar2=None, op0=ALU.min))
        P.step(eng, lambda e, o=o, w2=w2: e.tensor_scalar(
            out=f2[:, o:o + w2], in0=scr["ints"][:, 0:w2], scalar1=-4.0,
            scalar2=None, op0=ALU.max))
    P.step(eng, lambda e: e.tensor_add(out=f1, in0=f1, in1=f2))
    P.step(eng, lambda e: e.tensor_scalar(out=f2, in0=f2, scalar1=2.0,
                                          scalar2=-2.0, op0=ALU.min, op1=ALU.max))
    P.step(eng, lambda e: e.tensor_sub(out=f1, in0=f1, in1=f2))
    P.step(eng, lambda e: e.tensor_scalar(out=f2, in0=y, scalar1=5.0,
                                          scalar2=2.0, op0=ALU.is_gt, op1=ALU.mult))
    P.step(eng, lambda e: e.tensor_add(out=f1, in0=f1, in1=f2))
    P.step(eng, lambda e: e.tensor_scalar(out=f2, in0=y, scalar1=-5.0,
                                          scalar2=2.0, op0=ALU.is_lt, op1=ALU.mult))
    t_g = P.step(eng, lambda e: e.tensor_sub(out=f1, in0=f1, in1=f2), inc=1)
    scr["_t_g"] = t_g
    P.waits("dve", [t_g])
    P.step("dve", lambda e: e.tensor_scalar_mul(out=t1s, in0=t1s, scalar1=1.0 / 6.0))
    t = P.step("dve", lambda e: bcast_mul(e, dst_ap, f1, t1s), inc=1)
    scr["_t_dst"] = t
    return t


def emit_wtrans_rows(P, rows, S):
    for src_row, dst_row, wt in rows:
        nblk = wt // 128
        for base in range(0, nblk, 8):
            nf = min(8, nblk - base)
            t_ld = P.dma("wtld", "sync", S["buf"][:, 0:nf * 128],
                         src_row[:, base * 128:(base + nf) * 128],
                         deps=[S["t_tr"], S.get("extra_dep")])
            P.waits("pe", [t_ld, S["t_evac"]])
            for j in range(nf):
                P.step("pe", lambda e, j=j: e.transpose(
                    out=S["psB0"][:, j * 128:(j + 1) * 128],
                    in_=S["buf"][:, j * 128:(j + 1) * 128],
                    identity=S["ident"][:, :]), inc=(1 if j == nf - 1 else 0))
            t_tr = P.tok("pe")
            S["t_tr"] = t_tr
            P.waits("act", [t_tr, S["t_store"]])
            t_ev = P.step("act", lambda e, nf=nf: e.copy(
                out=S["evac"][:, 0:nf * 128], in_=S["psB0"][:, 0:nf * 128]),
                inc=1)
            S["t_evac"] = t_ev
            P.dma("wqst", "sync", dst_row[:, base * 128:(base + nf) * 128],
                  S["evac"][:, 0:nf * 128], deps=[t_ev])
            S["t_store"] = P.dtok("wqst")


def build_launch_E():
    nc = bass.Bass("TRN2", target_bir_lowering=False, debug=False)
    y1_in = nc.dram_tensor("y1_in", [NM, 128, T], BF16, kind="ExternalInput").ap()
    xn1_in = nc.dram_tensor("xn1_in", [D, TO], BF16, kind="ExternalInput").ap()
    x1_in = nc.dram_tensor("x1_in", [D, TO], F32, kind="ExternalInput").ap()
    Wo1L = nc.dram_tensor("Wo1L", [NM, 128, NM * 128], F32, kind="ExternalInput").ap()
    wg_in = nc.dram_tensor("wg_in", [NI, D], F32, kind="ExternalInput").ap()
    wu_in = nc.dram_tensor("wu_in", [NI, D], F32, kind="ExternalInput").ap()
    wd_in = nc.dram_tensor("wd_in", [D, NI], F32, kind="ExternalInput").ap()
    ident_in = nc.dram_tensor("ident_in", [128, 128], F32, kind="ExternalInput").ap()
    par_in = nc.dram_tensor("par_in", [1, 1], I32, kind="ExternalInput").ap()

    out_T = nc.dram_tensor("out_T", [D, TO], F32, kind="ExternalOutput").ap()
    import os
    FDBG = os.environ.get("FDBG", "")
    dbg = nc.dram_tensor("dbg", [D, T], F32, kind="ExternalOutput").ap() if FDBG else None

    wgq_d = nc.dram_tensor("wgq_d", [NIT, 128, D], BF16).ap()
    wuq_d = nc.dram_tensor("wuq_d", [NIT, 128, D], BF16).ap()
    wdq_d = nc.dram_tensor("wdq_d", [NM, 128, NI], BF16).ap()
    s1_d = nc.dram_tensor("s1_d", [D, TO], BF16).ap()

    ctx = ExitStack()
    with ctx:
        E = ctx.enter_context
        P = Prog(nc, ctx)
        poolA = E(nc.sbuf_tensor("poolA", [128, 16384], F32))
        poolB = E(nc.sbuf_tensor("poolB", [128, 32768], BF16))
        poolC = E(nc.sbuf_tensor("poolC", [128, 16384], BF16))
        poolW = E(nc.sbuf_tensor("poolW", [128, 10240], BF16))
        ints = E(nc.sbuf_tensor("ints", [128, NI], I32))
        ident_bf16 = E(nc.sbuf_tensor("ident_bf16", [128, 128], BF16))
        par_sb = E(nc.sbuf_tensor("par_sb", [1, 1], I32))
        psF = [E(nc.psum_tensor(f"psF{i}", [128, 512], F32)) for i in range(6)]
        psB = [E(nc.psum_tensor(f"psB{i}", [128, 1024], BF16)) for i in range(2)]
        blk = E(nc.Block())

        emit_load_parity(P, par_in, par_sb, engines=("gp",))
        t_id = P.dma("cnsg", "gp", ident_bf16[:, :], ident_in)
        for e in ("pe", "act", "dve"):
            P.wait(e, t_id)

        scr = dict(y=poolA[:, 0:2816], f1=poolA[:, 2816:5632],
                   f2=poolA[:, 5632:8448], ints=ints[:, :],
                   bm=poolA[0:128, 8448:8624], t1=poolA[:, 8624:8800],
                   r=poolA[:, 8800:8976])
        raw = poolA[:, 9216:12032]
        qnat = poolW[:, 0:2816]

        t_q = None
        t_tr_ev = None
        for wi, (w_ap, nrow, wq_d, wt) in enumerate([
                (wg_in, NIT, wgq_d, 1024), (wu_in, NIT, wuq_d, 1024),
                (wd_in, NM, wdq_d, 2816)]):
            for rt in range(nrow):
                t_ld = P.dma("wraw", "sync", raw[:, 0:wt],
                             w_ap[rt * 128:(rt + 1) * 128, :], deps=[t_q])
                t_q = emit_quant(P, raw[:, 0:wt], qnat[:, 0:wt], wt, scr,
                                 deps=[t_ld, t_tr_ev])
                for bb in range(wt // 128):
                    P.waits("pe", [t_q, t_tr_ev])
                    t_tr = P.step("pe", lambda e, bb=bb: e.transpose(
                        out=psB[0][:, 0:128], in_=qnat[:, bb * 128:(bb + 1) * 128],
                        identity=ident_bf16[:, :]), inc=1)
                    P.wait("act", t_tr)
                    t_tr_ev = P.step("act", lambda e: e.copy(
                        out=poolW[:, 4096:4224], in_=psB[0][:, 0:128]), inc=1)
                    P.dma("wqst", "sync", wq_d[rt, :, bb * 128:(bb + 1) * 128],
                          poolW[:, 4096:4224], deps=[t_tr_ev])
                    t_tr_ev = P.dtok("wqst")

        y1T = [poolB[:, ht * TO:(ht + 1) * TO] for ht in range(NM)]

        def go_y(e, sem):
            par = P.holder["par"]
            with e.If_eq(par, 0):
                for ht in range(NM):
                    e.dma_start(out=y1T[ht], in_=y1_in[ht, :, 0:TO]).then_inc(sem, 16)
            with e.Else():
                for ht in range(NM):
                    e.dma_start(out=y1T[ht], in_=y1_in[ht, :, TO:T]).then_inc(sem, 16)
        t_y = P.raw("yld", "gp", go_y, inc=16 * NM)
        woT = [poolB[:, 16384 + ht * 1024: 16384 + (ht + 1) * 1024]
               for ht in range(NM)]
        t_wo = [P.dma("wold", "gp", woT[ht], Wo1L[ht, :, :]) for ht in range(NM)]
        s1t = poolW[:, 4224:4736]
        t_s1 = None
        for c in range(TO // CHK):
            cs = slice(c * CHK, (c + 1) * CHK)
            for mt in range(NM):
                P.waits("pe", [t_y, t_wo[-1], t_s1])
                for ht in range(NM):
                    P.step("pe", lambda e, ht=ht, mt=mt, cs=cs: e.matmul(
                        psF[0][:, :], woT[ht][:, mt * 128:(mt + 1) * 128],
                        y1T[ht][:, cs], start=(ht == 0), stop=(ht == NM - 1)),
                        inc=(1 if ht == NM - 1 else 0))
                t_mm = P.tok("pe")
                P.waits("act", [t_mm])
                t_s1 = P.step("act", lambda e: e.copy(out=s1t, in_=psF[0][:, :]),
                              inc=1)
                P.dma("s1st", "sync", s1_d[mt * 128:(mt + 1) * 128, cs], s1t,
                      deps=[t_s1])
                t_s1 = P.dtok("s1st")
        P.barrier_all()

        xq = [poolC[:, m * TO:(m + 1) * TO] for m in range(NM)]
        t_x = [P.dma("xnld", "sync", xq[m], xn1_in[m * 128:(m + 1) * 128, :])
               for m in range(NM)]
        xnat = poolW[:, 4736:5760]
        t_nat_ev = None
        t_q2 = None
        for tt in range(TO // 128):
            for m in range(NM):
                P.waits("pe", [t_x[-1], t_nat_ev, t_q2])
                t_tr = P.step("pe", lambda e, m=m, tt=tt: e.transpose(
                    out=psB[0][:, 0:128], in_=xq[m][:, tt * 128:(tt + 1) * 128],
                    identity=ident_bf16[:, :]), inc=1)
                P.wait("act", t_tr)
                t_nat_ev = P.step("act", lambda e, m=m: e.copy(
                    out=xnat[:, m * 128:(m + 1) * 128], in_=psB[0][:, 0:128]), inc=1)
            t_q2 = emit_quant(P, xnat, qnat[:, 0:1024], 1024, scr, deps=[t_nat_ev])
            for m in range(NM):
                P.waits("pe", [t_q2])
                t_tr = P.step("pe", lambda e, m=m: e.transpose(
                    out=psB[0][:, 0:128], in_=qnat[:, m * 128:(m + 1) * 128],
                    identity=ident_bf16[:, :]), inc=1)
                P.wait("act", t_tr)
                t_q2 = P.step("act", lambda e, m=m, tt=tt: e.copy(
                    out=xq[m][:, tt * 128:(tt + 1) * 128], in_=psB[0][:, 0:128]),
                    inc=1)
        P.barrier_all()

        hT = [poolB[:, it * CHK:(it + 1) * CHK] for it in range(NIT)]
        wgb = [poolW[:, 5760 + p * 1024: 5760 + (p + 1) * 1024] for p in range(2)]
        wub = [poolW[:, 7808 + p * 1024: 7808 + (p + 1) * 1024] for p in range(2)]
        wdbuf = [poolB[:, 16384 + p * NI: 16384 + (p + 1) * NI] for p in range(2)]
        sg = poolA[:, 12032 + 0:12032 + 512]
        hnat = poolB[:, 22528:22528 + 2816]
        x1c = poolA[:, 12544:13056]
        g1c = poolA[:, 13056:13568]
        s1ct = poolW[:, 4224:4736]
        t_wg_cons = [None, None]
        t_wu_cons = [None, None]
        t_wd_cons = [None, None]
        t_hev = {}
        t_ps_free = [None, None]
        t_dn_free = [None, None]
        t_out_last = None
        t_out_prev_dve = None
        for c in range(TO // CHK):
            cs = slice(c * CHK, (c + 1) * CHK)
            for it in range(NIT):
                pb = it % 2
                t_lg = P.dma(f"wg{pb}", "gp", wgb[pb], wgq_d[it, :, :],
                             deps=[t_wg_cons[pb]])
                t_lu = P.dma(f"wu{pb}", "gp", wub[pb], wuq_d[it, :, :],
                             deps=[t_wu_cons[pb]])
                P.waits("pe", [t_lg, t_ps_free[0]])
                for m in range(NM):
                    P.step("pe", lambda e, m=m, it=it, pb=pb, cs=cs: e.matmul(
                        psF[0][:, :], wgb[pb][:, m * 128:(m + 1) * 128],
                        xq[m][:, cs], start=(m == 0), stop=(m == NM - 1)),
                        inc=(1 if m == NM - 1 else 0))
                t_g = P.tok("pe")
                t_wg_cons[pb] = t_g
                P.waits("pe", [t_lu, t_ps_free[1]])
                for m in range(NM):
                    P.step("pe", lambda e, m=m, it=it, pb=pb, cs=cs: e.matmul(
                        psF[1][:, :], wub[pb][:, m * 128:(m + 1) * 128],
                        xq[m][:, cs], start=(m == 0), stop=(m == NM - 1)),
                        inc=(1 if m == NM - 1 else 0))
                t_u = P.tok("pe")
                t_wu_cons[pb] = t_u
                P.wait("act", t_g)
                t_sg = P.step("act", lambda e: e.activation(
                    out=sg, in_=psF[0][:, :], func=AF.Silu), inc=1)
                t_ps_free[0] = t_sg
                P.waits("dve", [t_sg, t_u])
                t_h = P.step("dve", lambda e, it=it: e.tensor_tensor(
                    out=hT[it], in0=sg, in1=psF[1][:, :], op=ALU.mult), inc=1)
                t_ps_free[1] = t_h
                t_hev[it] = t_h
            t_qh = None
            t_natv = None
            for tt in range(4):
                for it in range(NIT):
                    P.waits("pe", [t_hev[it], t_natv, t_qh])
                    t_tr = P.step("pe", lambda e, it=it, tt=tt: e.transpose(
                        out=psB[0][:, 0:128], in_=hT[it][:, tt * 128:(tt + 1) * 128],
                        identity=ident_bf16[:, :]), inc=1)
                    P.wait("act", t_tr)
                    t_natv = P.step("act", lambda e, it=it: e.copy(
                        out=hnat[:, it * 128:(it + 1) * 128], in_=psB[0][:, 0:128]),
                        inc=1)
                t_qh = emit_quant(P, hnat, qnat[:, 0:2816], 2816, scr,
                                  deps=[t_natv])
                for it in range(NIT):
                    P.waits("pe", [t_qh])
                    t_tr = P.step("pe", lambda e, it=it: e.transpose(
                        out=psB[0][:, 0:128], in_=qnat[:, it * 128:(it + 1) * 128],
                        identity=ident_bf16[:, :]), inc=1)
                    P.wait("act", t_tr)
                    t_qh = P.step("act", lambda e, it=it, tt=tt: e.copy(
                        out=hT[it][:, tt * 128:(tt + 1) * 128],
                        in_=psB[0][:, 0:128]), inc=1)
            t_hq_all = t_qh
            for mt in range(NM):
                pb = mt % 2
                t_ldw = P.dma(f"wd{pb}", "gp", wdbuf[pb], wdq_d[mt, :, :],
                              deps=[t_wd_cons[pb]])
                P.waits("pe", [t_ldw, t_hq_all, t_dn_free[0]])
                for it in range(NIT):
                    P.step("pe", lambda e, it=it, pb=pb: e.matmul(
                        psF[2][:, :], wdbuf[pb][:, it * 128:(it + 1) * 128],
                        hT[it], start=(it == 0), stop=(it == NIT - 1)),
                        inc=(1 if it == NIT - 1 else 0))
                t_dn = P.tok("pe")
                t_wd_cons[pb] = t_dn
                P.waits("act", [t_dn, t_out_prev_dve])
                t_g1 = P.step("act", lambda e: e.activation(
                    out=g1c, in_=psF[2][:, :], func=AF.Silu), inc=1)
                t_dn_free[0] = t_g1

                def go_x1(e, sem, mt=mt, cs=cs):
                    return e.dma_start(out=x1c,
                                       in_=x1_in[mt * 128:(mt + 1) * 128, cs]
                                       ).then_inc(sem, 16)
                P.waits("sync", [t_out_last])
                t_x1 = P.raw("x1ld", "sync", go_x1, inc=16)
                t_s1r = P.dma("s1rd", "sync", s1ct,
                              s1_d[mt * 128:(mt + 1) * 128, cs],
                              deps=[t_out_last])
                P.waits("dve", [t_g1, t_x1, t_s1r])
                P.step("dve", lambda e: e.tensor_tensor(
                    out=g1c, in0=g1c, in1=s1ct, op=ALU.mult))
                t_x2 = P.step("dve", lambda e: e.tensor_tensor(
                    out=x1c, in0=x1c, in1=g1c, op=ALU.add), inc=1)
                t_out_prev_dve = t_x2
                t_out_last = P.dma("outst", "sync",
                                   out_T[mt * 128:(mt + 1) * 128, cs], x1c,
                                   deps=[t_x2])
        P.wait("sync", P.dtok("outst"))
        P.finalize(blk)
    return nc


def prep_E_inputs(inp):
    Wo = np.asarray(inp["L1_attn_Wo"])
    WoL = np.zeros((NM, 128, NM * 128), np.float32)
    WoT = Wo.T
    for hdt in range(NM):
        for mt in range(NM):
            WoL[hdt, :, mt * 128:(mt + 1) * 128] = WoT[hdt * 128:(hdt + 1) * 128,
                                                       mt * 128:(mt + 1) * 128]
    ident = np.eye(128, dtype=np.float32)
    base = {
        "Wo1L": WoL,
        "wg_in": np.asarray(inp["L1_ffn_w_gate"]),
        "wu_in": np.asarray(inp["L1_ffn_w_up"]),
        "wd_in": np.asarray(inp["L1_ffn_w_down"]),
        "ident_in": ident,
    }
    return base



POLL_LIM = 400000


def emit_exchange(P, nc, ex, go_write, n_write, sh_flag, nonce_sb, flag_rd):
    t_w = P.raw(f"exw{ex}", "gp", go_write, inc=16 * n_write)
    P.waits("gp", [t_w])

    def go_flag(e, sem, ex=ex):
        par = P.holder["par"]
        with e.If_eq(par, 0):
            e.dma_start(out=sh_flag[ex:ex + 1, 0:1], in_=nonce_sb[0:1, 0:1]
                        ).then_inc(sem, 16)
        with e.Else():
            e.dma_start(out=sh_flag[ex:ex + 1, 1:2], in_=nonce_sb[0:1, 0:1]
                        ).then_inc(sem, 16)
    P.raw(f"exf{ex}", "gp", go_flag, inc=16)
    psem = P.holder[f"pollsem{ex}"]

    def poll(e, ex=ex, psem=psem):
        cnt = P.holder["r_cnt"]
        go = P.holder["r_go"]
        f0 = P.holder["r_f0"]
        f1 = P.holder["r_f1"]
        nr = P.holder["r_nonce"]
        e.reg_mov(cnt, 0)
        e.reg_mov(go, 1)
        with e.While(go):
            e.reg_add(cnt, cnt, 16)
            e.dma_start(out=flag_rd[0:1, 0:2], in_=sh_flag[ex:ex + 1, 0:2]
                        ).then_inc(psem, 16)
            e.wait_ge(psem, cnt)
            v0 = e.value_load(flag_rd[0:1, 0:1])
            e.reg_mov(f0, v0)
            v1 = e.value_load(flag_rd[0:1, 1:2])
            e.reg_mov(f1, v1)
            e.reg_alu(f0, f0, nr, ALU.is_equal)
            e.reg_alu(f1, f1, nr, ALU.is_equal)
            e.reg_alu(f0, f0, f1, ALU.bitwise_and)
            e.reg_alu(go, f0, 1, ALU.is_lt)
            e.reg_alu(f1, cnt, 16 * POLL_LIM, ALU.is_lt)
            e.reg_alu(go, go, f1, ALU.bitwise_and)
        return e.engine_nop()
    t_done = P.step("gp", poll, inc=1)
    return t_done


def build_fused():
    nc = bass.Bass("TRN2", target_bir_lowering=False, debug=False)
    x0T = nc.dram_tensor("x0T", [D, T], F32, kind="ExternalInput").ap()
    Wr = nc.dram_tensor("Wr", [128, NM * NE], F32, kind="ExternalInput").ap()
    br_bc = nc.dram_tensor("br_bc", [128, NE], F32, kind="ExternalInput").ap()
    W1L = nc.dram_tensor("W1L", [NE, NF, 128, 1024], F32, kind="ExternalInput").ap()
    W2L = nc.dram_tensor("W2L", [NE, NM, 128, 4096], F32, kind="ExternalInput").ap()
    W2c_d = nc.dram_tensor("W2c_d", [NE, NM, 128, 4096], BF16).ap()
    b1r = nc.dram_tensor("b1r", [128, NE * NF], F32, kind="ExternalInput").ap()
    b2r = nc.dram_tensor("b2r", [128, NE * NM], F32, kind="ExternalInput").ap()
    selc = nc.dram_tensor("selc", [NE, NE * 128], F32, kind="ExternalInput").ap()
    ident_in = nc.dram_tensor("ident_in", [128, 128], F32, kind="ExternalInput").ap()
    par_in = nc.dram_tensor("par_in", [1, 1], I32, kind="ExternalInput").ap()
    nonce_in = nc.dram_tensor("nonce_in", [1, 1], I32, kind="ExternalInput").ap()
    WqL = nc.dram_tensor("WqL", [NH0, 128, 1024], F32, kind="ExternalInput").ap()
    WkL = nc.dram_tensor("WkL", [NH0, 128, 1024], F32, kind="ExternalInput").ap()
    WvL = nc.dram_tensor("WvL", [NH0, 128, 1024], F32, kind="ExternalInput").ap()
    WgT = nc.dram_tensor("WgT", [128, NM * 4], F32, kind="ExternalInput").ap()
    WdlT = nc.dram_tensor("WdlT", [128, NM * 48], F32, kind="ExternalInput").ap()
    bg_in = nc.dram_tensor("bg_in", [4, 1], F32, kind="ExternalInput").ap()
    lp_in = nc.dram_tensor("lp_in", [48, 1], F32, kind="ExternalInput").ap()
    mnat_in = nc.dram_tensor("mnat_in", [128, 2304], F32, kind="ExternalInput").ap()
    mst_in = nc.dram_tensor("mst_in", [128, 3 * 128], F32, kind="ExternalInput").ap()
    WoL = nc.dram_tensor("WoL", [NM, 128, NM * 128], F32, kind="ExternalInput").ap()
    lngb = nc.dram_tensor("lngb", [128, 2 * NM], F32, kind="ExternalInput").ap()
    Wq1L = nc.dram_tensor("Wq1L", [4, 128, 1024], F32, kind="ExternalInput").ap()
    Wk1L = nc.dram_tensor("Wk1L", [4, 128, 1024], F32, kind="ExternalInput").ap()
    Wv1L = nc.dram_tensor("Wv1L", [4, 128, 1024], F32, kind="ExternalInput").ap()
    zr_in = nc.dram_tensor("zr_in", [1, T], F32, kind="ExternalInput").ap()
    tri_in = nc.dram_tensor("tri_in", [128, 128], F32, kind="ExternalInput").ap()
    Wo1L = nc.dram_tensor("Wo1L", [NM, 128, NM * 128], F32, kind="ExternalInput").ap()
    wg_in = nc.dram_tensor("wg_in", [NI, D], F32, kind="ExternalInput").ap()
    wu_in = nc.dram_tensor("wu_in", [NI, D], F32, kind="ExternalInput").ap()
    wd_in = nc.dram_tensor("wd_in", [D, NI], F32, kind="ExternalInput").ap()

    out_T = nc.dram_tensor("out_T", [D, TO], F32, kind="ExternalOutput").ap()
    import os
    FDBG = os.environ.get("FDBG", "")
    dbg = nc.dram_tensor("dbg", [D, T], F32, kind="ExternalOutput").ap() if FDBG else None

    xn_d = nc.dram_tensor("xn_d", [D, T], BF16).ap()
    moe_d = nc.dram_tensor("moe_d", [D, TO], F32).ap()
    y0_d = nc.dram_tensor("y0_d", [NH0, 128, T], BF16).ap()
    x1_d = nc.dram_tensor("x1_d", [D, TO], F32).ap()
    xn1_d = nc.dram_tensor("xn1_d", [D, TO], BF16).ap()
    y1_d = nc.dram_tensor("y1_d", [NH1, 64, T], BF16).ap()
    wgq_d = nc.dram_tensor("wgq_d", [NIT, 128, D], BF16).ap()
    wuq_d = nc.dram_tensor("wuq_d", [NIT, 128, D], BF16).ap()
    wdq_d = nc.dram_tensor("wdq_d", [NM, 128, NI], BF16).ap()
    wgn_d = nc.dram_tensor("wgn_d", [NIT, 128, D], BF16).ap()
    wun_d = nc.dram_tensor("wun_d", [NIT, 128, D], BF16).ap()
    wdn_d = nc.dram_tensor("wdn_d", [NM, 128, NI], BF16).ap()
    s1_d = nc.dram_tensor("s1_d", [D, TO], BF16).ap()
    sh_y0 = nc.dram_tensor("sh_y0", [D, T], BF16, addr_space="Shared").ap()
    sh_xn1 = nc.dram_tensor("sh_xn1", [D, T], BF16, addr_space="Shared").ap()
    sh_y1 = nc.dram_tensor("sh_y1", [D, T], BF16, addr_space="Shared").ap()
    sh_flag = nc.dram_tensor("sh_flag", [4, 2], I32, addr_space="Shared").ap()

    ctx = ExitStack()
    with ctx:
        E = ctx.enter_context
        P = Prog(nc, ctx)
        poolA = E(nc.sbuf_tensor("poolA", [128, 16384], F32))
        poolB = E(nc.sbuf_tensor("poolB", [128, 32768], BF16))
        poolC = E(nc.sbuf_tensor("poolC", [128, 16384], BF16))
        poolS = E(nc.sbuf_tensor("poolS", [128, 4608], F32))
        poolW = E(nc.sbuf_tensor("poolW", [128, 11264], BF16))
        ints = E(nc.sbuf_tensor("ints", [128, 512], I32))
        ones_c = E(nc.sbuf_tensor("ones_c", [128, 1], F32))
        ones_r = E(nc.sbuf_tensor("ones_r", [1, 128], F32))
        ones_cb = E(nc.sbuf_tensor("ones_cb", [128, 1], BF16))
        ident_f32 = poolS[:, 1928:2056]
        ident_bf16 = E(nc.sbuf_tensor("ident_bf16", [128, 128], BF16))
        eps_b = E(nc.sbuf_tensor("eps_b", [1, 1], F32))
        par_sb = E(nc.sbuf_tensor("par_sb", [1, 1], I32))
        nonce_sb = E(nc.sbuf_tensor("nonce_sb", [1, 1], I32))
        flag_rd = E(nc.sbuf_tensor("flag_rd", [1, 2], I32))
        lngb_sb = E(nc.sbuf_tensor("lngb_sb", [128, 2 * NM], F32))
        bgs_sb = E(nc.sbuf_tensor("bgs_sb", [4, 1], F32))
        lps_sb = E(nc.sbuf_tensor("lps_sb", [48, 1], F32))
        wT_sb = E(nc.sbuf_tensor("wT_sb", [NE, TO], BF16))
        psF = [E(nc.psum_tensor(f"psF{i}", [128, 512], F32)) for i in range(6)]
        psB = [E(nc.psum_tensor(f"psB{i}", [128, 1024], BF16)) for i in range(2)]
        blk = E(nc.Block())

        P.step("dve", lambda e: e.memset(ones_c[:, :], 1.0))
        P.step("dve", lambda e: e.memset(eps_b[:, :], LN_EPS))
        P.step("dve", lambda e: e.memset(ones_cb[:, :], 1.0))
        t_init = P.step("dve", lambda e: e.memset(ones_r[:, :], 1.0), inc=1)
        for en in ("pe", "act", "gp"):
            P.wait(en, t_init)
        emit_load_parity(P, par_in, par_sb, engines=("gp", "dve"))
        t_nonce = P.dma("par", "gp", nonce_sb[0:1, 0:1], nonce_in)
        for i in range(3):
            P.holder[f"pollsem{i}"] = E(nc.semaphore(f"pollsem{i}"))

        def alloc_regs(e):
            for rn in ("r_cnt", "r_go", "r_f0", "r_f1", "r_nonce"):
                P.holder[rn] = e.alloc_register(rn)
            return e.reg_load(P.holder["r_nonce"], nonce_sb[0:1, 0:1])
        P.wait("gp", t_nonce)
        P.step("gp", alloc_regs)

        tc = [
            P.dma("cns", "sync", poolS[:, 1536:1600], Wr),
            P.dma("cns", "sync", poolS[0:128, 1600:1608], br_bc),
            P.dma("cns", "sync", poolS[:, 1608:1864], b1r),
            P.dma("cns", "sync", poolS[:, 1864:1928], b2r),
            P.dma("cnsg", "gp", poolW[0:NE, 10240:11264], selc),
            P.dma("cns", "sync", ident_f32, ident_in),
            P.dma("cnsg", "gp", ident_bf16[:, :], ident_in),
            P.dma("cns", "sync", lngb_sb[:, :], lngb),
            P.dma("cns", "sync", bgs_sb[:, :], bg_in),
            P.dma("cns", "sync", lps_sb[:, :], lp_in),
        ]
        cons_A = dict(ones_c=ones_c, ones_r=ones_r, ident_f32=ident_f32,
                      wr32_sb=poolS[:, 1536:1600], br_sb=poolS[0:128, 1600:1608],
                      b1_sb=poolS[:, 1608:1864], b2r_sb=poolS[:, 1864:1928],
                      sel_sb=poolW[0:NE, 10240:11264], psF=psF, psB=psB,
                      eps_b=eps_b)

        ln_bufs = dict(
            xbuf=[[poolA[:, (p * NM + m) * CHK:(p * NM + m + 1) * CHK]
                   for m in range(NM)] for p in range(2)],
            xsq=[poolS[:, 0:512], poolS[:, 512:1024]],
            rows=[poolA[0:1, 9216 + i * 512: 9216 + (i + 1) * 512] for i in range(5)],
            xnc=[poolB[:, m * CHK:(m + 1) * CHK] for m in range(NM)],
            tmp=[poolA[:, 8192 + (i % 2) * CHK: 8192 + (i % 2 + 1) * CHK]
                 for i in range(4)],
            rstd_all=poolA[0:1, 11776:13824],
            mur_all=poolA[0:1, 13824:15872],
        )
        ln_stores = emit_ln(P, x0T, xn_d, ln_bufs, psF, cons_A)
        P.barrier_all()

        xn_own = [poolC[:, m * TO:(m + 1) * TO] for m in range(NM)]
        t_own = emit_read_own(P, xn_d, xn_own)
        for en in ("pe", "dve", "act"):
            P.wait(en, t_own)
        rt_bufs = dict(
            logit=poolS[:, 0:8], m8=poolS[:, 8:16], p1=poolS[:, 16:17],
            p2=poolS[:, 17:18], eq1=poolS[:, 24:32], eq2=poolS[:, 32:40],
            wt=poolS[:, 40:48],
            xf=[poolA[:, m * CHK:(m + 1) * CHK] for m in range(NM)],
            xnf=[poolA[:, 4096 + m * CHK: 4096 + (m + 1) * CHK] for m in range(NM)],
        )
        for f16 in range(16):
            t_w1pre = P.dma("w1r", "gp",
                            poolB[:, f16 * 1024:(f16 + 1) * 1024],
                            W1L[0, f16, :, :])
        P.waits("pe", tc)
        emit_router_f32(P, x0T, wT_sb, ln_bufs["rstd_all"], ln_bufs["mur_all"],
                        rt_bufs, psF, cons_A)
        P.barrier_all()

        y_acc = [poolA[:, m * TO:(m + 1) * TO] for m in range(NM)]
        moe_bufs = dict(
            w1r=poolB[:, 0:16384],
            w2b=[poolW[:, p * 2048:(p + 1) * 2048] for p in range(2)],
            hT=[poolB[:, 16384 + f * 1024: 16384 + (f + 1) * 1024]
                for f in range(16)],
            wb_sb=poolS[:, 0:1024],
            ytmp=poolS[:, 1024:1536],
            y_acc_flat=poolA[:, :],
            W2c_d=W2c_d,
        )
        wq_jobs = []
        for rt in range(NIT):
            for s2 in range(2):
                wq_jobs.append((wg_in[rt * 128:(rt + 1) * 128,
                                      s2 * 512:(s2 + 1) * 512],
                                wgn_d[rt, :, s2 * 512:(s2 + 1) * 512], 512))
        for rt in range(NIT):
            for s2 in range(2):
                wq_jobs.append((wu_in[rt * 128:(rt + 1) * 128,
                                      s2 * 512:(s2 + 1) * 512],
                                wun_d[rt, :, s2 * 512:(s2 + 1) * 512], 512))
        wd_jobs = []
        for rt in range(NM):
            for o in range(0, NI, 512):
                w2 = min(512, NI - o)
                wd_jobs.append((wd_in[rt * 128:(rt + 1) * 128, o:o + w2],
                                wdn_d[rt, :, o:o + w2], w2))
        S3 = dict(flip=0, t_st=[None, None], pending=None,
                  y2=[poolS[:, 2056:2568], poolS[:, 2568:3080]],
                  scr=dict(f1=poolS[:, 3080:3592], f2=poolS[:, 3592:4104],
                           bm=poolS[:, 4104:4136], t1=poolS[:, 4136:4168],
                           r=poolS[:, 4168:4200], ints=ints[:, :]))

        def emit_wq_job(src_ap, dst_ap, w2):
            p = S3["flip"]
            S3["flip"] ^= 1
            t_ld = P.dma("wraw", "sync", S3["y2"][p][:, 0:w2], src_ap,
                         deps=[S3["t_st"][p]])
            scr = dict(S3["scr"])
            scr["y"] = S3["y2"][p]
            t_q = emit_quant(P, S3["y2"][p][:, 0:w2], S3["y2"][p][:, 0:w2],
                             w2, scr, deps=[t_ld])
            if S3["pending"] is not None:
                ps_, pd_, pw_, pt_, pp_ = S3["pending"]
                P.dma("wqn", "gp", pd_, ps_[:, 0:pw_], deps=[pt_])
                S3["t_st"][pp_] = P.dtok("wqn")
            S3["pending"] = (S3["y2"][p], dst_ap, w2, t_q, p)

        alljobs = wq_jobs + wd_jobs

        def moe_cb(i):
            n0 = (len(alljobs) * i) // 32
            n1 = (len(alljobs) * (i + 1)) // 32
            for j in range(n0, n1):
                emit_wq_job(*alljobs[j])

        moe_bufs["w1_pre_tok"] = t_w1pre
        t_moe = emit_moe(P, xn_own, wT_sb, y_acc, W1L, W2L, moe_bufs, psF, cons_A,
                         idle_cb=moe_cb)
        if S3["pending"] is not None:
            ps_, pd_, pw_, pt_, pp_ = S3["pending"]
            P.dma("wqn", "gp", pd_, ps_[:, 0:pw_], deps=[pt_])
        st_moe = [P.dma("moest", "sync", moe_d[m * 128:(m + 1) * 128, :], y_acc[m],
                        deps=[t_moe]) for m in range(NM)]
        P.barrier_all()

        cons_B = dict(ones_r=ones_r, ident_f32=ident_f32, ident_bf16=ident_bf16,
                      wg_sb=poolW[:, 5632:5664], wdl_sb=poolW[:, 5664:6048],
                      bgs_sb=bgs_sb, lps_sb=lps_sb,
                      mnat_sb=poolC[:, 12288:14592], mst_sb=poolS[:, 256:640],
                      WqL_ap=[WqL[i, :, :] for i in range(NH0)],
                      WkL_ap=[WkL[i, :, :] for i in range(NH0)],
                      WvL_ap=[WvL[i, :, :] for i in range(NH0)])
        tcb = [
            P.dma("cnsg", "gp", cons_B["wg_sb"], WgT),
            P.dma("cnsg", "gp", cons_B["wdl_sb"], WdlT),
            P.dma("cnsg2", "gp", cons_B["mnat_sb"], mnat_in),
            P.dma("cns2", "sync", cons_B["mst_sb"], mst_in),
        ]
        xnB = [poolB[:, m * T:(m + 1) * T] for m in range(NM)]
        t_xnB = [P.dma("xnf", "sync", xnB[m], xn_d[m * 128:(m + 1) * 128, :])
                 for m in range(NM)]
        for en in ("pe", "act", "dve", "gp"):
            P.waits(en, [t_xnB[-1], tcb[1], P.dtok("cns2"), P.dtok("cnsg2")])
        bufs_B = dict(
            attn_rows=poolA[:, 0:4096],
            lam_nat=poolA[:, 4096:5632],
            rowpool=poolA[64:65, 11648:15360],
            Rsb=poolA[:, 5632:8704],
            exin=poolA[:, 8704:9216],
            expt=poolA[:, 9216:9728],
            exin_b=poolA[:, 9728:10240],
            expt_b=poolA[:, 10240:10752],
            exin_c=poolA[:, 15360:15872],
            expt_c=poolA[:, 15872:16384],
            natx=poolA[:, 10752:11008],
            nate=poolA[:, 11008:11264],
            natl=poolA[:, 11264:11520],
            dmx=poolA[:, 11520:11648],
            sp_scr=[poolA[:, 11648:12160][0:48, :], poolA[:, 12160:12672][0:48, :]],
            spg_scr=poolA[96:100, 12672:13184],
            SPcol=poolS[:, 0:128],
            negSPcol=poolS[:, 128:256],
            QT=poolC[:, 0:4096],
            KT=poolC[:, 4096:8192],
            Vnat=poolC[:, 8192:12288],
            vtmp=poolW[:, 2048:2560],
            A_sb=[poolW[:, 0:512], poolW[:, 512:1024], poolW[:, 2048:2560]],
            ytile=poolW[:, 1024:1536],
            natA=poolW[:, 1536:1792],
            wq=poolW[:, 2560:3584],
            wk=poolW[:, 3584:4608],
            wv=poolW[:, 4608:5632],
            stateT32=poolS[:, 4200:4328],
            gcol=poolS[:, 4328:4332],
            mccol=poolS[:, 4332:4333],
            dmcol=poolS[:, 4333:4334],
            frow=poolS[0:1, 640:1152],
            spbnd=poolS[64:65, 1152:1153],
            tmp11=poolS[64:65, 1153:1154],
            ones96=poolS[64:65, 4334:4462],
            stateT=poolW[:, 8096:8224],
            knat=poolW[:, 8224:8736],
            vsc=poolW[:, 8736:9248],
            qsc=poolW[:, 9248:9760],
            natAT=[poolW[:, 9760:10016], poolW[:, 1792:2048]],
        )
        emit_l0_prep(P, xnB, bufs_B, psF, cons_B)
        P.barrier_all()
        ST4 = dict(t_tr=None, t_evac=None, t_store=None,
                   buf=poolW[:, 6048:7072], evac=poolW[:, 7072:8096],
                   psB0=psB[0][:, :], ident=ident_bf16)
        rows_t4 = [(wgn_d[rt, :, :], wgq_d[rt, :, :], 1024)
                   for rt in range(NIT)] + \
                  [(wun_d[rt, :, :], wuq_d[rt, :, :], 1024)
                   for rt in range(NIT)]
        def go_noop(e, sem):
            return None

        for h in range(NH0):
            tk = emit_l0_head(P, h, xnB, y0_d, bufs_B, psF, psB, cons_B)
            bufs_B["head_dep"] = tk

            def go_y0h(e, sem, h=h):
                par = P.holder["par"]
                with e.If_eq(par, 0):
                    e.dma_start(out=sh_y0[h * 128:(h + 1) * 128, :],
                                in_=y0_d[h, :, :]).then_inc(sem, 16)
                with e.Else():
                    e.dma_start(out=sh_y0[(NH0 + h) * 128:(NH0 + h + 1) * 128, :],
                                in_=y0_d[h, :, :]).then_inc(sem, 16)
            P.waits("gp", [P.dtok("yst")])
            P.raw("y0cp", "gp", go_y0h, inc=16)
            emit_wtrans_rows(P, rows_t4[h * 11:(h + 1) * 11], ST4)
        P.barrier_all()

        P.wait("gp", P.dtok("y0cp"))
        t_ex0 = emit_exchange(P, nc, 0, go_noop, 0, sh_flag, nonce_sb, flag_rd)
        for en in ENGINES:
            P.wait(en, t_ex0)

        yT = [poolB[:, ht * TO:(ht + 1) * TO] for ht in range(NM)]

        def go_yrd(e, sem):
            par = P.holder["par"]
            with e.If_eq(par, 0):
                for ht in range(NM):
                    e.dma_start(out=yT[ht], in_=sh_y0[ht * 128:(ht + 1) * 128, 0:TO]
                                ).then_inc(sem, 16)
            with e.Else():
                for ht in range(NM):
                    e.dma_start(out=yT[ht], in_=sh_y0[ht * 128:(ht + 1) * 128, TO:T]
                                ).then_inc(sem, 16)
        t_y = P.raw("yld", "gp", go_yrd, inc=16 * NM)
        xn_ownC = [poolC[:, m * TO:(m + 1) * TO] for m in range(NM)]
        t_xnC = emit_read_own(P, xn_d, xn_ownC)
        s_pre = [poolA[:, m * TO:(m + 1) * TO] for m in range(NM)]
        s_sb = [poolB[:, 16384 + m * TO: 16384 + (m + 1) * TO] for m in range(NM)]
        rows_off = [1024, 1536, 2048, 2560, 1536]
        lnC_bufs = dict(
            xsq=[poolS[:, 0:512], poolS[:, 512:1024]],
            rows=[poolS[0:1, o:o + 512] for o in rows_off],
        )
        g_sbC = poolS[:, 4096:4608]
        woT = [poolB[:, 24576 + ht * 1024: 24576 + (ht + 1) * 1024]
               for ht in range(NM)]
        t_wo = [P.dma("wold", "gp", woT[ht], WoL[ht, :, :]) for ht in range(NM)]
        cons_C = dict(ones_c=ones_c, ones_r=ones_r, eps_b=eps_b)
        NCH = TO // CHK
        t_spre = None
        for c in range(NCH):
            cs = slice(c * CHK, (c + 1) * CHK)
            for mt in range(NM):
                P.waits("pe", [t_y, t_wo[-1], t_spre])
                for ht in range(NM):
                    P.step("pe", lambda e, ht=ht, mt=mt, cs=cs: e.matmul(
                        psF[0][:, :], woT[ht][:, mt * 128:(mt + 1) * 128],
                        yT[ht][:, cs], start=(ht == 0), stop=(ht == NM - 1)),
                        inc=(1 if ht == NM - 1 else 0))
                t_mm = P.tok("pe")
                P.waits("dve", [t_mm, t_xnC])
                t_spre = P.step("dve", lambda e, mt=mt, cs=cs: e.tensor_tensor(
                    out=s_pre[mt][:, cs], in0=psF[0][:, :], in1=xn_ownC[mt][:, cs],
                    op=ALU.add), inc=1)
        P.barrier_all()
        for c in range(NCH):
            cs = slice(c * CHK, (c + 1) * CHK)

            def write_s(m, ps_b1, ps_b2, cs=cs):
                P.step("dve", lambda e, m=m: e.tensor_mul(
                    out=lnC_bufs["xsq"][m % 2], in0=s_pre[m][:, cs], in1=ps_b1))
                P.step("dve", lambda e, m=m: e.tensor_sub(
                    out=lnC_bufs["xsq"][m % 2], in0=lnC_bufs["xsq"][m % 2],
                    in1=ps_b2))
                return P.step("dve", lambda e, m=m: e.tensor_scalar(
                    out=s_sb[m][:, cs], in0=lnC_bufs["xsq"][m % 2],
                    scalar1=lngb_sb[:, m:m + 1],
                    scalar2=lngb_sb[:, NM + m:NM + m + 1],
                    op0=ALU.mult, op1=ALU.add), inc=1)
            src = [s_pre[m][:, cs] for m in range(NM)]
            t_ln2 = emit_ln_stats_apply(P, src, CHK, lnC_bufs, psF, cons_C, write_s)
            P.barrier_all()
            x1c = [poolA[:, m * TO + c * CHK: m * TO + (c + 1) * CHK]
                   for m in range(NM)]
            t_g = None
            for m in range(NM):
                t_moe2 = P.dma("moeld", "sync", poolS[:, 3584:4096],
                               moe_d[m * 128:(m + 1) * 128, cs], deps=[t_g])

                def go_x0(e, sem, m=m, cs=cs, c=c):
                    par = P.holder["par"]
                    with e.If_eq(par, 0):
                        e.dma_start(out=poolS[:, 3072:3584],
                                    in_=x0T[m * 128:(m + 1) * 128, cs]
                                    ).then_inc(sem, 16)
                    with e.Else():
                        e.dma_start(out=poolS[:, 3072:3584],
                                    in_=x0T[m * 128:(m + 1) * 128,
                                            TO + c * CHK:TO + (c + 1) * CHK]
                                    ).then_inc(sem, 16)
                t_x0 = P.raw("x0ld", "gp", go_x0, inc=16)
                P.waits("act", [t_moe2])
                t_g = P.step("act", lambda e: e.activation(
                    out=g_sbC, in_=poolS[:, 3584:4096], func=AF.Silu), inc=1)
                P.waits("dve", [t_g, t_ln2, t_x0])
                P.step("dve", lambda e, m=m, cs=cs: e.tensor_mul(
                    out=g_sbC, in0=g_sbC, in1=s_sb[m][:, cs]))
                t_x1 = P.step("dve", lambda e, m=m, x1c=x1c: e.tensor_tensor(
                    out=x1c[m], in0=poolS[:, 3072:3584], in1=g_sbC, op=ALU.add),
                    inc=1)
                P.dma("x1st", "sync", x1_d[m * 128:(m + 1) * 128, cs], x1c[m],
                      deps=[t_x1])
                P.waits("gp", [t_x1])
                P.waits("sync", [t_x1])
            P.barrier_all()
            xn1c = [poolC[:, m * TO + c * CHK: m * TO + (c + 1) * CHK]
                    for m in range(NM)]

            def write_xn1(m, ps_b1, ps_b2, cs=cs, x1c=x1c, xn1c=xn1c):
                P.step("dve", lambda e, m=m, x1c=x1c: e.tensor_mul(
                    out=lnC_bufs["xsq"][m % 2], in0=x1c[m], in1=ps_b1))
                t = P.step("dve", lambda e, m=m, xn1c=xn1c: e.tensor_sub(
                    out=xn1c[m], in0=lnC_bufs["xsq"][m % 2], in1=ps_b2), inc=1)
                P.dma("xn1st", "sync", xn1_d[m * 128:(m + 1) * 128, cs],
                      xn1c[m], deps=[t])
                return t
            emit_ln_stats_apply(P, x1c, CHK, lnC_bufs, psF, cons_C, write_xn1)

            def go_xn1c(e, sem, cs=cs, c=c):
                par = P.holder["par"]
                with e.If_eq(par, 0):
                    e.dma_start(out=sh_xn1[:, c * CHK:(c + 1) * CHK],
                                in_=xn1_d[:, cs]).then_inc(sem, 16)
                with e.Else():
                    e.dma_start(out=sh_xn1[:, TO + c * CHK:TO + (c + 1) * CHK],
                                in_=xn1_d[:, cs]).then_inc(sem, 16)
            P.waits("gp", [P.dtok("xn1st")])
            P.raw("xn1cp", "gp", go_xn1c, inc=16)
            P.barrier_all()
        P.wait("gp", P.dtok("x1st"))
        P.wait("gp", P.dtok("xn1cp"))
        P.wait("gp", st_moe[-1])

        t_ex1 = emit_exchange(P, nc, 1, go_noop, 0, sh_flag, nonce_sb, flag_rd)
        for en in ENGINES:
            P.wait(en, t_ex1)

        cons_D = dict(ones_r=ones_r, ones_cb=ones_cb, ident_bf16=ident_bf16,
                      tri_sb=poolW[:, 8864:8992], zr_sb=poolA[0:1, 5120:9216],
                      Wq1L_ap=[Wq1L[i, :, :] for i in range(4)],
                      Wk1L_ap=[Wk1L[i, :, :] for i in range(4)],
                      Wv1L_ap=[Wv1L[i, :, :] for i in range(4)])
        tcd = [
            P.dma("cns3", "sync", cons_D["zr_sb"], zr_in),
            P.dma("cnsg2", "gp", cons_D["tri_sb"], tri_in),
        ]
        xnD = [poolB[:, m * T:(m + 1) * T] for m in range(NM)]
        t_xnD = [P.dma("xn1f", "sync", xnD[m], sh_xn1[m * 128:(m + 1) * 128, :])
                 for m in range(NM)]
        for en in ("pe", "act", "dve", "gp"):
            P.waits(en, [t_xnD[-1], tcd[0], P.dtok("cnsg2")])
        bufs_D = dict(
            QT=poolC[:, 0:4096], KT=poolC[:, 4096:8192], VT=poolC[:, 8192:12288],
            Vn=poolC[:, 12288:14368],
            pc=poolA[:, 0:4096],
            of32=poolA[0:64, 4096:4608],
            zf=poolA[0:1, 4608:5120],
            vh=poolW[:, 0:4096],
            A_sb=[poolW[:, 4096:4608], poolW[:, 4608:5120],
                  poolW[:, 8992:9504], poolW[:, 9504:10016]],
            y1t=poolW[0:64, 5120:5632],
            wq=poolW[:, 5792:6816],
            wk=poolW[:, 6816:7840],
            wv=poolW[:, 7840:8864],
            wslots=[[poolW[:, 5792:6816], poolW[:, 6816:7840],
                     poolW[:, 7840:8864]],
                    [poolW[:, 0:1024], poolW[:, 1024:2048],
                     poolW[:, 2048:3072]]],
        )
        ST8 = dict(t_tr=None, t_evac=None, t_store=None,
                   buf=poolW[:, 10016:11040], evac=poolC[:, 14368:15392],
                   psB0=psB[0][:, :], ident=ident_bf16)
        rows_t8 = [(wdn_d[rt, :, :], wdq_d[rt, :, :], NI) for rt in range(NM)]
        for hp in range(4):
            tk = emit_l1_pair(P, hp, xnD, y1_d, bufs_D, psF, psB, cons_D)
            bufs_D["head_dep"] = tk

            def go_y1p(e, sem, hp=hp):
                par = P.holder["par"]
                src = y1_d[hp * 2:(hp + 1) * 2, :, :].rearrange(
                    "a b t -> (a b) t")
                with e.If_eq(par, 0):
                    e.dma_start(out=sh_y1[hp * 128:(hp + 1) * 128, :],
                                in_=src).then_inc(sem, 16)
                with e.Else():
                    e.dma_start(out=sh_y1[512 + hp * 128:512 + (hp + 1) * 128, :],
                                in_=src).then_inc(sem, 16)
            P.waits("gp", [P.dtok("yst")])
            P.raw("y1cp", "gp", go_y1p, inc=16)
            emit_wtrans_rows(P, rows_t8[hp * 2:(hp + 1) * 2], ST8)
        P.barrier_all()

        P.wait("gp", P.dtok("y1cp"))
        t_ex2 = emit_exchange(P, nc, 2, go_noop, 0, sh_flag, nonce_sb, flag_rd)
        for en in ENGINES:
            P.wait(en, t_ex2)

        scr = dict(y=poolA[:, 0:2816], f1=poolA[:, 2816:5632],
                   f2=poolA[:, 5632:8448], ints=ints[:, :],
                   bm=poolA[0:128, 8448:8624], t1=poolA[:, 8624:8800],
                   r=poolA[:, 8800:8976])
        qnat = poolW[:, 0:2816]
        y1T = [poolB[:, ht * TO:(ht + 1) * TO] for ht in range(NM)]

        def go_y1rd(e, sem):
            par = P.holder["par"]
            with e.If_eq(par, 0):
                for ht in range(NM):
                    e.dma_start(out=y1T[ht], in_=sh_y1[ht * 128:(ht + 1) * 128, 0:TO]
                                ).then_inc(sem, 16)
            with e.Else():
                for ht in range(NM):
                    e.dma_start(out=y1T[ht], in_=sh_y1[ht * 128:(ht + 1) * 128, TO:T]
                                ).then_inc(sem, 16)
        t_y1rd = P.raw("y1ld", "gp", go_y1rd, inc=16 * NM)
        wo1T = [poolB[:, 16384 + ht * 1024: 16384 + (ht + 1) * 1024]
                for ht in range(NM)]
        t_wo1 = [P.dma("wo1ld", "gp", wo1T[ht], Wo1L[ht, :, :]) for ht in range(NM)]
        s1t = poolW[:, 4224:4736]
        t_s1 = None
        for c in range(TO // CHK):
            cs = slice(c * CHK, (c + 1) * CHK)
            for mt in range(NM):
                P.waits("pe", [t_y1rd, t_wo1[-1], t_s1])
                for ht in range(NM):
                    P.step("pe", lambda e, ht=ht, mt=mt, cs=cs: e.matmul(
                        psF[0][:, :], wo1T[ht][:, mt * 128:(mt + 1) * 128],
                        y1T[ht][:, cs], start=(ht == 0), stop=(ht == NM - 1)),
                        inc=(1 if ht == NM - 1 else 0))
                t_mm = P.tok("pe")
                P.waits("act", [t_mm])
                t_s1 = P.step("act", lambda e: e.copy(out=s1t, in_=psF[0][:, :]),
                              inc=1)
                P.dma("s1st", "sync", s1_d[mt * 128:(mt + 1) * 128, cs], s1t,
                      deps=[t_s1])
                t_s1 = P.dtok("s1st")
        P.barrier_all()
        xq = [poolC[:, m * TO:(m + 1) * TO] for m in range(NM)]
        t_x = [P.dma("xnld", "sync", xq[m], xn1_d[m * 128:(m + 1) * 128, :])
               for m in range(NM)]
        xnat = poolW[:, 2816:3840]
        PS0 = dict(r=None)
        xst = dict(qfree=None, natfree=None)
        xq_ready = [None] * 16

        def xquant_tile(tt):
            P.waits("pe", [t_x[-1], PS0["r"]])
            for m in range(NM):
                P.step("pe", lambda e, m=m, tt=tt: e.transpose(
                    out=psB[0][:, m * 128:(m + 1) * 128],
                    in_=xq[m][:, tt * 128:(tt + 1) * 128],
                    identity=ident_bf16[:, :]), inc=(1 if m == NM - 1 else 0))
            t_tr = P.tok("pe")
            P.waits("act", [t_tr, xst["natfree"]])
            t_natv = P.step("act", lambda e: e.copy(out=xnat[:, 0:1024],
                                                    in_=psB[0][:, :]), inc=1)
            PS0["r"] = t_natv
            t_q2 = emit_quant(P, xnat[:, 0:1024], qnat[:, 0:1024], 1024, scr,
                              deps=[t_natv, xst["qfree"]])
            xst["natfree"] = t_q2
            P.waits("pe", [t_q2, PS0["r"]])
            for m in range(NM):
                P.step("pe", lambda e, m=m: e.transpose(
                    out=psB[0][:, m * 128:(m + 1) * 128],
                    in_=qnat[:, m * 128:(m + 1) * 128],
                    identity=ident_bf16[:, :]), inc=(1 if m == NM - 1 else 0))
            t_tr2 = P.tok("pe")
            xst["qfree"] = t_tr2
            P.waits("act", [t_tr2])
            for m in range(NM):
                P.step("act", lambda e, m=m, tt=tt: e.copy(
                    out=xq[m][:, tt * 128:(tt + 1) * 128],
                    in_=psB[0][:, m * 128:(m + 1) * 128]),
                    inc=(1 if m == NM - 1 else 0))
            t_bk = P.tok("act")
            PS0["r"] = t_bk
            xq_ready[tt] = t_bk

        for tt in range(4):
            xquant_tile(tt)
        P.barrier_all()
        hT2 = [[poolB[:, hb * 11264 + it * CHK: hb * 11264 + (it + 1) * CHK]
                for it in range(NIT)] for hb in range(2)]
        hreg = [poolB[:, 0:11264], poolB[:, 11264:22528]]
        hnat = poolB[:, 22528:25344]
        wdbuf = [poolB[:, 25344 + p * NI: 25344 + (p + 1) * NI] for p in range(2)]
        sg2 = [poolA[:, 12032:12544], poolA[:, 13568:14080]]
        x1cE = poolA[:, 12544:13056]
        g1c = poolA[:, 13056:13568]
        s1ct = poolW[:, 4224:4736]
        ps_g = [psF[0][:, :], psF[2][:, :]]
        ps_u = [psF[1][:, :], psF[3][:, :]]
        ps_dn = psF[4][:, :]
        t_wg_cons = [None, None]
        t_wu_cons = [None, None]
        t_wd_cons = [None, None]
        t_psg_free = [None, None]
        t_psu_free = [None, None]
        t_sg_cons = [None, None]
        t_dn_free = None
        t_out_last = None
        t_out_prev_dve = None
        t_h_cons = [None, None]
        t_hq_done = [None, None]

        def emit_gateup(c):
            hb = c % 2
            H = hT2[hb]
            cs = slice(c * CHK, (c + 1) * CHK)
            nxt = [4 + c * 4 + j for j in range(4)] if c < 3 else []
            for it in range(NIT):
                pb = it % 2
                t_lg = P.dma(f"wg{pb}", "gp", wgb[pb], wgq_d[it, :, :],
                             deps=[t_wg_cons[pb]])
                t_lu = P.dma(f"wu{pb}", "gp", wub[pb], wuq_d[it, :, :],
                             deps=[t_wu_cons[pb]])
                P.waits("pe", [t_lg, t_psg_free[pb], t_h_cons[hb] if it == 0 else None,
                               xq_ready[c * 4 + 3] if it == 0 else None])
                for m in range(NM):
                    P.step("pe", lambda e, m=m, pb=pb, cs=cs: e.matmul(
                        ps_g[pb], wgb[pb][:, m * 128:(m + 1) * 128],
                        xq[m][:, cs], start=(m == 0), stop=(m == NM - 1)),
                        inc=(1 if m == NM - 1 else 0))
                t_g2 = P.tok("pe")
                t_wg_cons[pb] = t_g2
                P.waits("pe", [t_lu, t_psu_free[pb]])
                for m in range(NM):
                    P.step("pe", lambda e, m=m, pb=pb, cs=cs: e.matmul(
                        ps_u[pb], wub[pb][:, m * 128:(m + 1) * 128],
                        xq[m][:, cs], start=(m == 0), stop=(m == NM - 1)),
                        inc=(1 if m == NM - 1 else 0))
                t_u = P.tok("pe")
                t_wu_cons[pb] = t_u
                P.waits("act", [t_g2, t_sg_cons[pb]])
                t_sg = P.step("act", lambda e, pb=pb: e.activation(
                    out=sg2[pb], in_=ps_g[pb], func=AF.Silu), inc=1)
                t_psg_free[pb] = t_sg
                P.waits("dve", [t_sg, t_u, t_h_cons[hb] if it == 0 else None])
                t_h = P.step("dve", lambda e, it=it, pb=pb, H=H: e.tensor_tensor(
                    out=H[it], in0=sg2[pb], in1=ps_u[pb], op=ALU.mult), inc=1)
                t_sg_cons[pb] = t_h
                t_psu_free[pb] = t_h
                if nxt and it in (4, 9, 14, 19):
                    xquant_tile(nxt.pop(0))
            return P.tok("dve")

        def emit_hquant(c, t_hmults):
            hb = c % 2
            H = hT2[hb]
            hr3 = hreg[hb].rearrange("p (i c) -> p i c", c=CHK)
            t_qh = None
            for tt in range(4):
                for base in range(0, NIT, 8):
                    nf = min(8, NIT - base)
                    P.waits("pe", [t_hmults, PS0["r"]])
                    for j in range(nf):
                        P.step("pe", lambda e, j=j, base=base, tt=tt, H=H:
                               e.transpose(
                            out=psB[0][:, j * 128:(j + 1) * 128],
                            in_=H[base + j][:, tt * 128:(tt + 1) * 128],
                            identity=ident_bf16[:, :]),
                            inc=(1 if j == nf - 1 else 0))
                    t_tr = P.tok("pe")
                    P.waits("act", [t_tr, xst["natfree"] if base == 0 else None])
                    t_hn = P.step("act", lambda e, base=base, nf=nf: e.copy(
                        out=hnat[:, base * 128:(base + nf) * 128],
                        in_=psB[0][:, 0:nf * 128]), inc=1)
                    PS0["r"] = t_hn
                t_qh = emit_quant(P, hnat[:, :], qnat[:, 0:2816], 2816, scr,
                                  deps=[t_hn, xst["qfree"]])
                xst["natfree"] = t_qh
                for base in range(0, NIT, 8):
                    nf = min(8, NIT - base)
                    P.waits("pe", [t_qh, PS0["r"]])
                    for j in range(nf):
                        P.step("pe", lambda e, j=j, base=base: e.transpose(
                            out=psB[0][:, j * 128:(j + 1) * 128],
                            in_=qnat[:, (base + j) * 128:(base + j + 1) * 128],
                            identity=ident_bf16[:, :]),
                            inc=(1 if j == nf - 1 else 0))
                    t_tr2 = P.tok("pe")
                    if base + nf == NIT:
                        xst["qfree"] = t_tr2
                    P.waits("dve", [t_tr2])
                    t_bk = P.step("dve", lambda e, base=base, nf=nf, tt=tt,
                                  hr3=hr3: e.tensor_copy(
                        out=hr3[:, base:base + nf, tt * 128:(tt + 1) * 128],
                        in_=psB[0][:, 0:nf * 128].rearrange(
                            "p (i c) -> p i c", c=128)),
                        inc=1)
                    PS0["r"] = t_bk
            t_hq_done[hb] = P.tok("dve")

        def emit_down(c):
            nonlocal t_dn_free, t_out_last, t_out_prev_dve
            hb = c % 2
            H = hT2[hb]
            cs = slice(c * CHK, (c + 1) * CHK)
            t_last_mm = None
            for mt in range(NM):
                pb = mt % 2
                t_ldw = P.dma(f"wd{pb}", "gp", wdbuf[pb], wdq_d[mt, :, :],
                              deps=[t_wd_cons[pb]])
                P.waits("pe", [t_ldw, t_hq_done[hb] if mt == 0 else None,
                               t_dn_free])
                for it in range(NIT):
                    P.step("pe", lambda e, it=it, pb=pb, H=H: e.matmul(
                        ps_dn, wdbuf[pb][:, it * 128:(it + 1) * 128],
                        H[it], start=(it == 0), stop=(it == NIT - 1)),
                        inc=(1 if it == NIT - 1 else 0))
                t_dn = P.tok("pe")
                t_last_mm = t_dn
                t_wd_cons[pb] = t_dn
                P.waits("act", [t_dn, t_out_prev_dve])
                t_g1 = P.step("act", lambda e: e.activation(
                    out=g1c, in_=ps_dn, func=AF.Silu), inc=1)
                t_dn_free = t_g1

                def go_x1(e, sem, mt=mt, cs=cs):
                    return e.dma_start(out=x1cE,
                                       in_=x1_d[mt * 128:(mt + 1) * 128, cs]
                                       ).then_inc(sem, 16)
                P.waits("sync", [t_out_last])
                t_x1r = P.raw("x1ldE", "sync", go_x1, inc=16)
                t_s1r = P.dma("s1rd", "sync", s1ct,
                              s1_d[mt * 128:(mt + 1) * 128, cs],
                              deps=[t_out_last])
                P.waits("dve", [t_g1, t_x1r, t_s1r])
                P.step("dve", lambda e: e.tensor_tensor(
                    out=g1c, in0=g1c, in1=s1ct, op=ALU.mult))
                t_x2 = P.step("dve", lambda e: e.tensor_tensor(
                    out=x1cE, in0=x1cE, in1=g1c, op=ALU.add), inc=1)
                t_out_prev_dve = t_x2
                t_out_last = P.dma("outst", "sync",
                                   out_T[mt * 128:(mt + 1) * 128, cs], x1cE,
                                   deps=[t_x2])
            t_h_cons[hb] = t_last_mm

        wgb = [poolW[:, 4736 + p * 1024: 4736 + (p + 1) * 1024] for p in range(2)]
        wub = [poolW[:, 6784 + p * 1024: 6784 + (p + 1) * 1024] for p in range(2)]
        for c in range(5):
            t_hm = None
            if c < 4:
                t_hm = emit_gateup(c)
            if c >= 1:
                emit_down(c - 1)
            if c < 4:
                emit_hquant(c, t_hm)

        P.wait("sync", P.dtok("outst"))
        if FDBG:
            P.barrier_all()
            srcs = {"xn": (xn_d, T, BF16), "moe": (moe_d, TO, F32),
                    "x1": (x1_d, TO, F32), "xn1": (xn1_d, TO, BF16),
                    "shy0": (sh_y0, T, BF16), "shxn1": (sh_xn1, T, BF16),
                    "shy1": (sh_y1, T, BF16), "s1": (s1_d, TO, BF16)}
            sap, w, dt = srcs[FDBG]
            for m in range(NM):
                tdl = P.dma("dbgl", "gp", poolA[:, 0:w], sap[m * 128:(m + 1) * 128, 0:w])
                P.dma("dbgs", "sync", dbg[m * 128:(m + 1) * 128, 0:w], poolA[:, 0:w],
                      deps=[tdl])
                P.wait("gp", P.dtok("dbgs"))
            P.wait("sync", P.dtok("dbgs"))
        P.finalize(blk)
    return nc


def prep_fused_inputs(inp):
    maps_A = prep_A_inputs(inp)
    maps_B = prep_B_inputs(inp)
    base_C = prep_C_inputs(inp)
    maps_D = prep_D_inputs(inp)
    base_E = prep_E_inputs(inp)
    nonce = np.array([[np.random.randint(1, 2**31 - 1)]], np.int32)
    in_maps = []
    for c in range(8):
        m = {}
        m.update(maps_A[c])
        m.update(maps_B[c])
        m.update({k: v for k, v in base_C.items() if k != "ident_in"})
        m.update(maps_D[c])
        m.update({k: v for k, v in base_E.items() if k != "ident_in"})
        m["nonce_in"] = nonce
        in_maps.append(m)
    return in_maps


def kernel_fused(**inputs):
    inp = {k: np.asarray(v) for k, v in inputs.items()}
    in_maps = prep_fused_inputs(inp)
    res = run_bass_kernel_spmd(build_fused(), in_maps, core_ids=list(range(8)))
    out = np.empty((B, T, D), np.float32)
    for c in range(8):
        b, par = c // 2, c % 2
        out[b, par * TO:(par + 1) * TO, :] = np.asarray(res.results[c]["out_T"]).T
    return out


# version 26
# speedup vs baseline: 1.1184x; 1.1184x over previous
import math
from contextlib import ExitStack

import numpy as np
import ml_dtypes

import concourse.bass as bass
import concourse.mybir as mybir
from concourse.bass_utils import run_bass_kernel_spmd

F32 = mybir.dt.float32
BF16 = mybir.dt.bfloat16
I32 = mybir.dt.int32
AF = mybir.ActivationFunctionType
ALU = mybir.AluOpType
AX = mybir.AxisListType

B = 4
T = 4096
TO = 2048
D = 1024
NM = D // 128
FF = 4096
NF = FF // 128
NE = 8
CHK = 512
NCHK = T // CHK
LN_EPS = 1e-5

BF = ml_dtypes.bfloat16

ENGINES = ("sync", "gp", "pe", "act", "dve")


class Prog:

    def __init__(self, nc, ctx):
        self.nc = nc
        self.q = {e: [] for e in ENGINES}
        self.sem = {e: ctx.enter_context(nc.semaphore(f"sem_{e}")) for e in ENGINES}
        self.cnt = {e: 0 for e in ENGINES}
        self.ssem = {}
        self.scnt = {}
        self._ctx = ctx
        self.waited = {e: {} for e in ENGINES}
        self.holder = {}

    def on(self, eng, fn):
        self.q[eng].append(fn)

    def tok(self, eng):
        return ("c", eng, self.cnt[eng])

    def dtok(self, stream):
        return ("d", stream, self.scnt.get(stream, 0))

    def _stream_sem(self, stream):
        if stream not in self.ssem:
            self.ssem[stream] = self._ctx.enter_context(
                self.nc.semaphore(f"ds_{stream}"))
            self.scnt[stream] = 0
        return self.ssem[stream]

    def wait(self, eng, token):
        if token is None:
            return
        kind, peng, val = token
        if kind == "c" and peng == eng:
            return
        key = (kind, peng)
        if self.waited[eng].get(key, 0) >= val:
            return
        self.waited[eng][key] = val
        sem = self.sem[peng] if kind == "c" else self._stream_sem(peng)
        self.on(eng, lambda e, sem=sem, val=val: e.wait_ge(sem, val))

    def waits(self, eng, tokens):
        for t in tokens:
            self.wait(eng, t)

    SERIAL = ("dve", "act", "gp")

    def step(self, eng, fn, inc=0):
        if eng in self.SERIAL:
            sem = self.sem[eng]
            prev = self.cnt[eng]
            if prev:
                self.on(eng, lambda e, sem=sem, prev=prev: e.wait_ge(sem, prev))
            self.cnt[eng] += 1
            val = self.cnt[eng]

            def wrapped(e, fn=fn, sem=sem):
                ins = fn(e)
                ins.then_inc(sem, 1)
            self.on(eng, wrapped)
            return ("c", eng, val)
        if inc:
            self.cnt[eng] += inc
            val = self.cnt[eng]
            sem = self.sem[eng]

            def wrapped(e, fn=fn, sem=sem, inc=inc):
                ins = fn(e)
                ins.then_inc(sem, inc)
            self.on(eng, wrapped)
            return ("c", eng, val)
        self.on(eng, fn)
        return None

    def raw(self, stream, eng, fn, inc):
        sem = self._stream_sem(stream)
        self.scnt[stream] += inc
        self.on(eng, lambda e, fn=fn, sem=sem: fn(e, sem))
        return ("d", stream, self.scnt[stream])

    def par_step(self, eng, fn_even, fn_odd, deps=()):
        self.waits(eng, deps)
        sem = self.sem[eng]
        prev = self.cnt[eng]
        if eng in self.SERIAL and prev:
            self.on(eng, lambda e, sem=sem, prev=prev: e.wait_ge(sem, prev))
        self.cnt[eng] += 1
        val = self.cnt[eng]
        holder = self.holder
        id_key = f"par_{eng}"

        def go(e, fn_even=fn_even, fn_odd=fn_odd, sem=sem, id_key=id_key):
            par = holder[id_key]
            with e.If_eq(par, 0):
                fn_even(e).then_inc(sem, 1)
            with e.Else():
                fn_odd(e).then_inc(sem, 1)
        self.on(eng, go)
        return ("c", eng, val)

    def dma(self, stream, eng, out, in_, deps=()):
        self.waits(eng, deps)
        sem = self._stream_sem(stream)
        self.scnt[stream] += 16
        val = self.scnt[stream]

        def wrapped(e, sem=sem, out=out, in_=in_):
            e.dma_start(out=out, in_=in_).then_inc(sem, 16)
        self.on(eng, wrapped)
        return ("d", stream, val)

    def barrier_all(self):
        toks = [self.tok(e) for e in ENGINES] + \
               [("d", s, self.scnt[s]) for s in self.scnt]
        for e in ENGINES:
            self.waits(e, toks)

    def finalize(self, block):
        def run(name):
            def body(e):
                for fn in self.q[name]:
                    fn(e)
            return body
        block.sync(run("sync"))
        block.gpsimd(run("gp"))
        block.tensor(run("pe"))
        block.scalar(run("act"))
        block.vector(run("dve"))



def emit_ln(P, x0T_ap, xn_dram, bufs, psF, cons, chunks=NCHK):
    xbuf, xsq, xnc, tmp = bufs["xbuf"], bufs["xsq"], bufs["xnc"], bufs["tmp"]
    mu, msq, var, rstd, mur = bufs["rows"]
    ones_c, ones_r = cons["ones_c"], cons["ones_r"]
    ps_stat, ps_stat2, ps_b1, ps_b2 = psF[0][0:1, :], psF[1][0:1, :], psF[2][:, :], psF[3][:, :]

    last_apply = [None, None]
    last_stats = None
    last_bc_cons = None
    last_sq_mm = [None, None]
    stores = []

    for k in range(chunks):
        p = k % 2
        ld = [P.dma(f"lnx{p}", "sync", xbuf[p][m],
                    x0T_ap[m * 128:(m + 1) * 128, k * CHK:(k + 1) * CHK],
                    deps=[last_apply[p]]) for m in range(NM)]
        P.waits("pe", [ld[-1], last_stats])
        for m in range(NM):
            P.step("pe", lambda e, m=m, p=p: e.matmul(
                ps_stat, ones_c[:, :], xbuf[p][m],
                start=(m == 0), stop=(m == NM - 1)),
                inc=(1 if m == NM - 1 else 0))
        t_sum = P.tok("pe")
        for m in range(NM):
            P.waits("act", [ld[-1], last_sq_mm[m % 2]])
            t_sq_m = P.step("act", lambda e, m=m, p=p: e.activation(
                out=xsq[m % 2], in_=xbuf[p][m], func=AF.Square), inc=1)
            P.wait("pe", t_sq_m)
            last_sq_mm[m % 2] = P.step("pe", lambda e, m=m: e.matmul(
                ps_stat2, ones_c[:, :], xsq[m % 2],
                start=(m == 0), stop=(m == NM - 1)), inc=1)
        t_sq = P.tok("pe")
        P.wait("dve", t_sum)
        P.step("dve", lambda e: e.tensor_scalar_mul(out=mu, in0=ps_stat, scalar1=1.0 / D))
        P.wait("dve", t_sq)
        P.step("dve", lambda e: e.tensor_scalar_mul(out=msq, in0=ps_stat2, scalar1=1.0 / D))
        P.step("dve", lambda e: e.tensor_mul(out=var, in0=mu, in1=mu))
        t_var = P.step("dve", lambda e: e.tensor_sub(out=var, in0=msq, in1=var), inc=1)
        last_stats = t_var
        P.wait("act", t_var)
        P.step("act", lambda e: e.activation(out=rstd, in_=var, func=AF.Ln,
                                             bias=cons["eps_b"][0:1, 0:1]))
        t_rstd = P.step("act", lambda e: e.activation(out=rstd, in_=rstd, func=AF.Exp,
                                                      scale=-0.5), inc=1)
        P.wait("dve", t_rstd)
        t_mur = P.step("dve", lambda e: e.tensor_mul(out=mur, in0=mu, in1=rstd), inc=1)
        P.waits("pe", [t_rstd, t_mur, last_bc_cons])
        P.step("pe", lambda e: e.matmul(ps_b1, ones_r[:, :], rstd, start=True, stop=True))
        t_bc = P.step("pe", lambda e: e.matmul(ps_b2, ones_r[:, :], mur, start=True,
                                               stop=True), inc=1)
        rsave, msave = bufs["rstd_all"], bufs["mur_all"]
        kk = k - (0 if k < NCHK // 2 else NCHK // 2)

        def mknop(e):
            return e.engine_nop()
        if k < NCHK // 2:
            P.par_step("dve",
                       lambda e, kk=kk: e.tensor_copy(
                           out=rsave[0:1, kk * CHK:(kk + 1) * CHK], in_=rstd),
                       mknop)
            P.par_step("dve",
                       lambda e, kk=kk: e.tensor_copy(
                           out=msave[0:1, kk * CHK:(kk + 1) * CHK], in_=mur),
                       mknop)
        else:
            P.par_step("dve", mknop,
                       lambda e, kk=kk: e.tensor_copy(
                           out=rsave[0:1, kk * CHK:(kk + 1) * CHK], in_=rstd))
            P.par_step("dve", mknop,
                       lambda e, kk=kk: e.tensor_copy(
                           out=msave[0:1, kk * CHK:(kk + 1) * CHK], in_=mur))
        P.wait("dve", t_bc)
        for m in range(NM):
            P.step("dve", lambda e, m=m, p=p: e.tensor_mul(
                out=tmp[m % 4], in0=xbuf[p][m], in1=ps_b1))
            P.step("dve", lambda e, m=m: e.tensor_sub(
                out=xnc[m], in0=tmp[m % 4], in1=ps_b2))
        t_apply = P.step("dve", lambda e: e.engine_nop(), inc=1)
        last_apply[p] = t_apply
        last_bc_cons = t_apply
        st = [P.dma("lnst", "sync", xn_dram[m * 128:(m + 1) * 128, k * CHK:(k + 1) * CHK],
                    xnc[m], deps=[t_apply]) for m in range(NM)]
        stores += st
        P.waits("dve", [st[-1]])
    return stores



def emit_read_own(P, src_dram, dst_tiles, deps=(), dtype_cast=False, width=TO):
    eng = "gp"
    P.waits(eng, deps)

    def go(e, sem):
        par = P.holder["par"]
        with e.If_eq(par, 0):
            for m in range(NM):
                e.dma_start(out=dst_tiles[m],
                            in_=src_dram[m * 128:(m + 1) * 128, 0:width]
                            ).then_inc(sem, 16)
        with e.Else():
            for m in range(NM):
                e.dma_start(out=dst_tiles[m],
                            in_=src_dram[m * 128:(m + 1) * 128, T - width:T]
                            ).then_inc(sem, 16)
    return P.raw("own", eng, go, inc=16 * NM)


def emit_load_parity(P, par_in_ap, par_sb, engines=("gp",)):
    t = P.dma("par", "gp", par_sb[0:1, 0:1], par_in_ap)

    def mk(eng):
        def alloc(e):
            par = e.alloc_register(f"par_reg_{eng}")
            P.holder[f"par_{eng}"] = par
            if eng == "gp":
                P.holder["par"] = par
            return e.reg_load(par, par_sb[0:1, 0:1])
        return alloc
    for eng in engines:
        P.wait(eng, t)
        P.step(eng, mk(eng))



def emit_router_f32(P, x0T_ap, wT_sb, rstd_all, mur_all, bufs, psF, cons,
                    deps=()):
    wr32, br_sb = cons["wr32_sb"], cons["br_sb"]
    ident = cons["ident_f32"]
    ones_r = cons["ones_r"]
    xf, xnf = bufs["xf"], bufs["xnf"]
    logit, m8, p1, p2, eq1, eq2, wt = (bufs["logit"], bufs["m8"], bufs["p1"],
                                       bufs["p2"], bufs["eq1"], bufs["eq2"],
                                       bufs["wt"])
    ps_r = psF[0][:, 0:NE]
    ps_wt = psF[1][0:NE, 0:128]
    ps_b1, ps_b2 = psF[2][:, :], psF[3][:, :]
    P.waits("pe", list(deps))
    t_xnf_done = None
    t_ev = None
    t_tr_ev = None
    t_bc_cons = None
    for kk in range(TO // CHK):
        def go(e, sem, kk=kk):
            par = P.holder["par"]
            with e.If_eq(par, 0):
                for m in range(NM):
                    e.dma_start(out=xf[m],
                                in_=x0T_ap[m * 128:(m + 1) * 128,
                                           kk * CHK:(kk + 1) * CHK]
                                ).then_inc(sem, 16)
            with e.Else():
                for m in range(NM):
                    e.dma_start(out=xf[m],
                                in_=x0T_ap[m * 128:(m + 1) * 128,
                                           TO + kk * CHK:TO + (kk + 1) * CHK]
                                ).then_inc(sem, 16)
        P.waits("gp", [t_xnf_done])
        t_ld = P.raw("rtx", "gp", go, inc=16 * NM)
        P.waits("pe", [t_bc_cons])
        P.step("pe", lambda e, kk=kk: e.matmul(
            ps_b1, ones_r[:, :], rstd_all[0:1, kk * CHK:(kk + 1) * CHK],
            start=True, stop=True))
        t_bc = P.step("pe", lambda e, kk=kk: e.matmul(
            ps_b2, ones_r[:, :], mur_all[0:1, kk * CHK:(kk + 1) * CHK],
            start=True, stop=True), inc=1)
        P.waits("dve", [t_bc, t_ld])
        for m in range(NM):
            P.step("dve", lambda e, m=m: e.tensor_mul(out=xnf[m], in0=xf[m],
                                                      in1=ps_b1))
            P.step("dve", lambda e, m=m: e.tensor_sub(out=xnf[m], in0=xnf[m],
                                                      in1=ps_b2))
        t_xnf_done = P.step("dve", lambda e: e.engine_nop(), inc=1)
        t_bc_cons = t_xnf_done
        for tt in range(CHK // 128):
            P.waits("pe", [t_xnf_done, t_ev])
            for m in range(NM):
                P.step("pe", lambda e, m=m, tt=tt: e.matmul(
                    ps_r, xnf[m][:, tt * 128:(tt + 1) * 128],
                    wr32[:, m * NE:(m + 1) * NE],
                    start=(m == 0), stop=(m == NM - 1)),
                    inc=(1 if m == NM - 1 else 0))
            t_mm = P.tok("pe")
            P.wait("dve", t_mm)
            P.step("dve", lambda e: e.tensor_tensor(out=logit, in0=ps_r,
                                                    in1=br_sb[:, :], op=ALU.add))
            P.step("dve", lambda e: e.max(out=m8, in_=logit))
            t_d = P.step("dve", lambda e: e.tensor_sub(out=p2, in0=m8[:, 1:2],
                                                       in1=m8[:, 0:1]), inc=1)
            P.wait("act", t_d)
            t_sg = P.step("act", lambda e: e.activation(out=p2, in_=p2,
                                                        func=AF.Sigmoid), inc=1)
            P.wait("dve", t_sg)
            P.step("dve", lambda e: e.tensor_scalar(out=p1, in0=p2, scalar1=-1.0,
                                                    scalar2=1.0, op0=ALU.mult,
                                                    op1=ALU.add))
            P.step("dve", lambda e: e.tensor_scalar(out=eq1, in0=logit,
                                                    scalar1=m8[:, 0:1], scalar2=None,
                                                    op0=ALU.is_equal))
            P.step("dve", lambda e: e.tensor_scalar(out=eq2, in0=logit,
                                                    scalar1=m8[:, 1:2], scalar2=None,
                                                    op0=ALU.is_equal))
            P.step("dve", lambda e: e.tensor_scalar(out=eq2, in0=eq2,
                                                    scalar1=p2[:, 0:1], scalar2=None,
                                                    op0=ALU.mult))
            t_w = P.step("dve", lambda e: e.scalar_tensor_tensor(
                out=wt, in0=eq1, scalar=p1[:, 0:1], in1=eq2,
                op0=ALU.mult, op1=ALU.add), inc=1)
            P.waits("pe", [t_w, t_tr_ev])
            t_tr = P.step("pe", lambda e: e.transpose(out=ps_wt, in_=wt,
                                                      identity=ident[:, :]), inc=1)
            P.wait("dve", t_tr)
            g_tt = kk * 4 + tt
            t_tr_ev = P.step("dve", lambda e, g_tt=g_tt: e.tensor_copy(
                out=wT_sb[:, g_tt * 128:(g_tt + 1) * 128], in_=ps_wt), inc=1)
            t_ev = t_tr_ev
    return t_tr_ev



def emit_moe(P, xn_own, wT_sb, y_acc, W1L_ap, W2L_ap, bufs, psF, cons, deps=(),
             idle_cb=None):
    w1r = bufs["w1r"]
    w2b = bufs["w2b"]
    hT = bufs["hT"]
    wb_sb = bufs["wb_sb"]
    ytmp = bufs["ytmp"]
    W2c_d = bufs["W2c_d"]
    b1_sb, b2r_sb, sel_sb = cons["b1_sb"], cons["b2r_sb"], cons["sel_sb"]
    MCH = 1024
    NC = TO // MCH
    NFH = NF // 2

    P.waits("dve", list(deps))
    t0 = P.step("dve", lambda e: e.memset(bufs["y_acc_flat"], 0.0), inc=1)

    t_w2_cons = [None, None]
    t_w2c = [None]
    t_h_ev = {}
    t_ps_h_free = [None, None]
    t_ps_y_free = [None, None]
    t_wb_free = None
    t_yadd_last = t0
    ps_wb = [psF[0][:, :], psF[1][:, :]]
    ps_h = [psF[2][:, :], psF[3][:, :]]
    ps_y = [psF[4][:, :], psF[5][:, :]]

    def w1_load(e_i, fh, dep):
        t = None
        for f16 in range(NFH):
            t = P.dma("w1r", "gp", w1r[:, f16 * 1024:(f16 + 1) * 1024],
                      W1L_ap[e_i, fh * NFH + f16, :, :],
                      deps=[dep] if f16 == 0 else [])
        return t

    t_w1_ld = bufs.get("w1_pre_tok")
    if t_w1_ld is None:
        t_w1_ld = w1_load(0, 0, None)
    it = 0
    for e_i in range(NE):
        for fh in range(2):
            t_w1mm_last = None
            for c in range(NC):
                P.waits("pe", [t_wb_free])
                for s in range(2):
                    P.step("pe", lambda e, e_i=e_i, c=c, s=s: e.matmul(
                        ps_wb[s], sel_sb[:, e_i * 128:(e_i + 1) * 128],
                        wT_sb[:, c * MCH + s * 512: c * MCH + (s + 1) * 512],
                        start=True, stop=True), inc=(1 if s == 1 else 0))
                t_wbmm = P.tok("pe")
                P.waits("act", [t_wbmm, t_yadd_last])
                for s in range(2):
                    P.step("act", lambda e, s=s: e.copy(
                        out=wb_sb[:, s * 512:(s + 1) * 512], in_=ps_wb[s]),
                        inc=(1 if s == 1 else 0))
                t_wb_ev = P.tok("act")
                t_wb_free = t_wb_ev

                for f16 in range(NFH):
                    f = fh * NFH + f16
                    for s in range(2):
                        P.waits("pe", [t_w1_ld, t_ps_h_free[s]])
                        for m in range(NM):
                            P.step("pe", lambda e, f16=f16, m=m, c=c, s=s:
                                   e.matmul(
                                ps_h[s], w1r[:, f16 * 1024 + m * 128:
                                             f16 * 1024 + (m + 1) * 128],
                                xn_own[m][:, c * MCH + s * 512:
                                          c * MCH + (s + 1) * 512],
                                start=(m == 0), stop=(m == NM - 1)),
                                inc=(1 if m == NM - 1 else 0))
                        t_hmm = P.tok("pe")
                        P.wait("act", t_hmm)
                        tk = P.step("act", lambda e, f16=f16, f=f, s=s, e_i=e_i:
                                    e.activation(
                            out=hT[f16][:, s * 512:(s + 1) * 512], in_=ps_h[s],
                            func=AF.Relu,
                            bias=b1_sb[:, e_i * NF + f: e_i * NF + f + 1]),
                            inc=1)
                        t_ps_h_free[s] = tk
                        t_h_ev[(f16, s)] = tk
                t_h_all = P.tok("act")
                t_w1mm_last = P.tok("pe")
                if c == NC - 1:
                    nxt = e_i * 2 + fh + 1
                    if nxt < 16:
                        t_w1_ld = w1_load(nxt // 2, nxt % 2, t_w1mm_last)
                for mt in range(NM):
                    pb = mt % 2
                    if c == 0:
                        t_ld = P.dma(f"w2_{pb}", "gp", w2b[pb],
                                     W2L_ap[e_i, mt, :,
                                            fh * 2048:(fh + 1) * 2048],
                                     deps=[t_w2_cons[pb], t_w2c[0]])
                        P.dma("w2c", "sync",
                              W2c_d[e_i, mt, :, fh * 2048:(fh + 1) * 2048],
                              w2b[pb], deps=[t_ld])
                        t_w2c[0] = P.dtok("w2c")
                    else:
                        t_ld = P.dma(f"w2_{pb}", "sync", w2b[pb],
                                     W2c_d[e_i, mt, :,
                                           fh * 2048:(fh + 1) * 2048],
                                     deps=[t_w2_cons[pb], t_w2c[0]])
                    for s in range(2):
                        P.waits("pe", [t_ld, t_h_all, t_ps_y_free[s]])
                        for f16 in range(NFH):
                            P.step("pe", lambda e, pb=pb, f16=f16, s=s: e.matmul(
                                ps_y[s], w2b[pb][:, f16 * 128:(f16 + 1) * 128],
                                hT[f16][:, s * 512:(s + 1) * 512],
                                start=(f16 == 0), stop=(f16 == NFH - 1)),
                                inc=(1 if f16 == NFH - 1 else 0))
                        t_ymm = P.tok("pe")
                        if s == 1:
                            t_w2_cons[pb] = t_ymm
                        P.waits("dve", [t_ymm, t_wb_ev])
                        if fh == 0:
                            P.step("dve", lambda e, s=s, e_i=e_i, mt=mt:
                                   e.scalar_tensor_tensor(
                                out=ytmp, in0=ps_y[s],
                                scalar=b2r_sb[:, e_i * NM + mt:
                                              e_i * NM + mt + 1],
                                in1=wb_sb[:, s * 512:(s + 1) * 512],
                                op0=ALU.add, op1=ALU.mult))
                        else:
                            P.step("dve", lambda e, s=s: e.scalar_tensor_tensor(
                                out=ytmp, in0=ps_y[s], scalar=0.0,
                                in1=wb_sb[:, s * 512:(s + 1) * 512],
                                op0=ALU.add, op1=ALU.mult))
                        tk = P.step("dve", lambda e, mt=mt, c=c, s=s:
                                    e.tensor_tensor(
                            out=y_acc[mt][:, c * MCH + s * 512:
                                          c * MCH + (s + 1) * 512],
                            in0=y_acc[mt][:, c * MCH + s * 512:
                                          c * MCH + (s + 1) * 512],
                            in1=ytmp, op=ALU.add), inc=1)
                        t_ps_y_free[s] = tk
                        t_yadd_last = tk
                if idle_cb is not None:
                    idle_cb(it)
                it += 1
    return t_yadd_last


def build_launch_A():
    nc = bass.Bass("TRN2", target_bir_lowering=False, debug=False)
    x0T = nc.dram_tensor("x0T", [D, T], F32, kind="ExternalInput").ap()
    Wr = nc.dram_tensor("Wr", [128, NM * NE], F32, kind="ExternalInput").ap()
    br_bc = nc.dram_tensor("br_bc", [128, NE], F32, kind="ExternalInput").ap()
    W1L = nc.dram_tensor("W1L", [NE, NF, 128, 1024], F32, kind="ExternalInput").ap()
    W2L = nc.dram_tensor("W2L", [NE, NM, 128, 4096], F32, kind="ExternalInput").ap()
    W2c_d = nc.dram_tensor("W2c_d", [NE, NM, 128, 4096], BF16).ap()
    b1r = nc.dram_tensor("b1r", [128, NE * NF], F32, kind="ExternalInput").ap()
    b2r = nc.dram_tensor("b2r", [128, NE * NM], F32, kind="ExternalInput").ap()
    selc = nc.dram_tensor("selc", [NE, NE * 128], F32, kind="ExternalInput").ap()
    ident_in = nc.dram_tensor("ident_in", [128, 128], F32, kind="ExternalInput").ap()
    par_in = nc.dram_tensor("par_in", [1, 1], I32, kind="ExternalInput").ap()

    xn_dram = nc.dram_tensor("xn_dram", [D, T], BF16, kind="ExternalOutput").ap()
    moe_out = nc.dram_tensor("moe_out", [D, TO], F32, kind="ExternalOutput").ap()

    ctx = ExitStack()
    with ctx:
        E = ctx.enter_context
        P = Prog(nc, ctx)
        poolA = E(nc.sbuf_tensor("poolA", [128, 16384], F32))
        poolB = E(nc.sbuf_tensor("poolB", [128, 32768], BF16))
        poolC = E(nc.sbuf_tensor("poolC", [128, 16384], BF16))
        poolS = E(nc.sbuf_tensor("poolS", [128, 2048], F32))
        poolW = E(nc.sbuf_tensor("poolW", [128, 10240], BF16))
        ones_c = E(nc.sbuf_tensor("ones_c", [128, 1], F32))
        ones_r = E(nc.sbuf_tensor("ones_r", [1, 128], F32))
        ident_f32 = E(nc.sbuf_tensor("ident_f32", [128, 128], F32))
        wr32_sb = E(nc.sbuf_tensor("wr32_sb", [128, NM * NE], F32))
        br_sb = E(nc.sbuf_tensor("br_sb", [128, NE], F32))
        b1_sb = E(nc.sbuf_tensor("b1_sb", [128, NE * NF], F32))
        b2r_sb = E(nc.sbuf_tensor("b2r_sb", [128, NE * NM], F32))
        sel_sb = E(nc.sbuf_tensor("sel_sb", [NE, NE * 128], BF16))
        par_sb = E(nc.sbuf_tensor("par_sb", [1, 1], I32))
        eps_b = E(nc.sbuf_tensor("eps_b", [1, 1], F32))
        wT_sb = E(nc.sbuf_tensor("wT_sb", [NE, TO], BF16))
        psF = [E(nc.psum_tensor(f"psF{i}", [128, 512], F32)) for i in range(6)]
        psB = [E(nc.psum_tensor(f"psB{i}", [128, 1024], BF16)) for i in range(2)]

        blk = E(nc.Block())

        cons = dict(ones_c=ones_c, ones_r=ones_r, ident_f32=ident_f32,
                    wr32_sb=wr32_sb, br_sb=br_sb, b1_sb=b1_sb, b2r_sb=b2r_sb,
                    sel_sb=sel_sb, psF=psF, psB=psB, eps_b=eps_b)

        P.step("dve", lambda e: e.memset(ones_c[:, :], 1.0))
        P.step("dve", lambda e: e.memset(eps_b[:, :], LN_EPS))
        t_init = P.step("dve", lambda e: e.memset(ones_r[:, :], 1.0), inc=1)
        P.wait("pe", t_init)
        P.wait("act", t_init)
        emit_load_parity(P, par_in, par_sb, engines=("gp", "dve"))
        tc = [
            P.dma("cns", "sync", wr32_sb[:, :], Wr),
            P.dma("cns", "sync", br_sb[:, :], br_bc),
            P.dma("cns", "sync", b1_sb[:, :], b1r),
            P.dma("cns", "sync", b2r_sb[:, :], b2r),
            P.dma("cnsg", "gp", sel_sb[:, :], selc),
            P.dma("cns", "sync", ident_f32[:, :], ident_in),
        ]

        ln_bufs = dict(
            xbuf=[[poolA[:, (p * NM + m) * CHK:(p * NM + m + 1) * CHK]
                   for m in range(NM)] for p in range(2)],
            xsq=[poolS[:, 0:512], poolS[:, 512:1024]],
            rows=[poolA[0:1, 9216 + i * 512: 9216 + (i + 1) * 512] for i in range(5)],
            xnc=[poolB[:, m * CHK:(m + 1) * CHK] for m in range(NM)],
            tmp=[poolA[:, 8192 + (i % 2) * CHK: 8192 + (i % 2 + 1) * CHK]
                 for i in range(4)],
            rstd_all=poolA[0:1, 11776:13824],
            mur_all=poolA[0:1, 13824:15872],
        )
        ln_stores = emit_ln(P, x0T, xn_dram, ln_bufs, psF, cons)

        P.barrier_all()

        xn_own = [poolC[:, m * TO:(m + 1) * TO] for m in range(NM)]
        t_own = emit_read_own(P, xn_dram, xn_own, deps=[])
        for e in ("pe", "dve", "act"):
            P.wait(e, t_own)

        rt_bufs = dict(
            logit=poolS[:, 0:8], m8=poolS[:, 8:16], p1=poolS[:, 16:17],
            p2=poolS[:, 17:18], eq1=poolS[:, 24:32], eq2=poolS[:, 32:40],
            wt=poolS[:, 40:48],
            xf=[poolA[:, m * CHK:(m + 1) * CHK] for m in range(NM)],
            xnf=[poolA[:, 4096 + m * CHK: 4096 + (m + 1) * CHK] for m in range(NM)],
        )
        P.waits("pe", tc)
        t_rt = emit_router_f32(P, x0T, wT_sb, ln_bufs["rstd_all"],
                               ln_bufs["mur_all"], rt_bufs, psF, cons)

        P.barrier_all()

        y_acc = [poolA[:, m * TO:(m + 1) * TO] for m in range(NM)]
        moe_bufs = dict(
            w1r=poolB[:, 0:16384],
            w2b=[poolW[:, p * 2048:(p + 1) * 2048] for p in range(2)],
            hT=[poolB[:, 16384 + f * 1024: 16384 + (f + 1) * 1024]
                for f in range(16)],
            wb_sb=poolS[:, 0:1024],
            ytmp=poolS[:, 1024:1536],
            y_acc_flat=poolA[:, :],
            W2c_d=W2c_d,
        )
        t_moe = emit_moe(P, xn_own, wT_sb, y_acc, W1L, W2L, moe_bufs, psF, cons)

        st = [P.dma("ost", "sync", moe_out[m * 128:(m + 1) * 128, :], y_acc[m],
                    deps=[t_moe]) for m in range(NM)]
        P.wait("sync", st[-1])
        P.wait("sync", ln_stores[-1])
        P.finalize(blk)
    return nc



def prep_A_inputs(inp):
    x0 = np.asarray(inp["L0_attn_x"])
    Wr = np.asarray(inp["L0_ffn_W_router"])
    br = np.asarray(inp["L0_ffn_b_router"])
    W1 = np.asarray(inp["L0_ffn_W1"])
    b1 = np.asarray(inp["L0_ffn_b1"])
    W2 = np.asarray(inp["L0_ffn_W2"])
    b2 = np.asarray(inp["L0_ffn_b2"])

    WrL = np.ascontiguousarray(
        Wr.reshape(NM, 128, NE).transpose(1, 0, 2).reshape(128, NM * NE))
    br_bc = np.ascontiguousarray(np.broadcast_to(br[None, :], (128, NE))).astype(np.float32)
    W1L = np.ascontiguousarray(
        W1.reshape(NE, NM, 128, NF, 128).transpose(0, 3, 2, 1, 4)
        .reshape(NE, NF, 128, NM * 128))
    W2L = np.ascontiguousarray(
        W2.reshape(NE, NF, 128, NM, 128).transpose(0, 3, 2, 1, 4)
        .reshape(NE, NM, 128, NF * 128))
    b1r = np.ascontiguousarray(b1.reshape(NE, NF, 128).transpose(2, 0, 1)
                               .reshape(128, NE * NF))
    b2r = np.ascontiguousarray(b2.reshape(NE, NM, 128).transpose(2, 0, 1)
                               .reshape(128, NE * NM))
    sel = np.zeros((NE, NE * 128), np.float32)
    for e in range(NE):
        sel[e, e * 128:(e + 1) * 128] = 1.0
    ident = np.eye(128, dtype=np.float32)

    in_maps = []
    for c in range(8):
        b = c // 2
        x0T = np.ascontiguousarray(x0[b].T)
        in_maps.append({
            "x0T": x0T, "Wr": WrL, "br_bc": br_bc, "W1L": W1L, "W2L": W2L,
            "b1r": b1r, "b2r": b2r, "selc": sel, "ident_in": ident,
            "par_in": np.array([[c % 2]], np.int32),
        })
    return in_maps


def kernel(**inputs):
    return kernel_fused(**inputs)


def kernel_unfused(**inputs):
    inp = {k: np.asarray(v) for k, v in inputs.items()}

    maps_A = prep_A_inputs(inp)
    res_A = run_bass_kernel_spmd(build_launch_A(), maps_A, core_ids=list(range(8)))
    xn = [res_A.results[c]["xn_dram"] for c in range(8)]
    moe = [res_A.results[c]["moe_out"] for c in range(8)]

    maps_B = prep_B_inputs(inp)
    for c in range(8):
        maps_B[c]["xnin"] = xn[c]
    res_B = run_bass_kernel_spmd(build_launch_B(), maps_B, core_ids=list(range(8)))

    base_C = prep_C_inputs(inp)
    maps_C = []
    for c in range(8):
        b, par = c // 2, c % 2
        m = dict(base_C)
        m["yT_in"] = np.ascontiguousarray(np.concatenate(
            [np.asarray(res_B.results[2 * b]["y_dram"]),
             np.asarray(res_B.results[2 * b + 1]["y_dram"])], axis=0))
        m["xn0_in"] = xn[c]
        m["moe_in"] = moe[c]
        m["x0T"] = maps_A[c]["x0T"]
        m["par_in"] = np.array([[par]], np.int32)
        maps_C.append(m)
    res_C = run_bass_kernel_spmd(build_launch_C(), maps_C, core_ids=list(range(8)))

    maps_D = prep_D_inputs(inp)
    xn1_full = {}
    for b in range(4):
        xn1_full[b] = np.ascontiguousarray(np.concatenate(
            [np.asarray(res_C.results[2 * b]["xn1_out"]),
             np.asarray(res_C.results[2 * b + 1]["xn1_out"])], axis=1))
    for c in range(8):
        maps_D[c]["xnin"] = xn1_full[c // 2]
    res_D = run_bass_kernel_spmd(build_launch_D(), maps_D, core_ids=list(range(8)))

    base_E = prep_E_inputs(inp)
    maps_E = []
    for c in range(8):
        b, par = c // 2, c % 2
        m = dict(base_E)
        y1cat = np.concatenate(
            [np.asarray(res_D.results[2 * b]["y_dram"]).reshape(512, T),
             np.asarray(res_D.results[2 * b + 1]["y_dram"]).reshape(512, T)],
            axis=0)
        m["y1_in"] = np.ascontiguousarray(y1cat.reshape(NM, 128, T))
        m["xn1_in"] = np.asarray(res_C.results[c]["xn1_out"])
        m["x1_in"] = np.asarray(res_C.results[c]["x1_out"])
        m["par_in"] = np.array([[par]], np.int32)
        maps_E.append(m)
    res_E = run_bass_kernel_spmd(build_launch_E(), maps_E, core_ids=list(range(8)))

    out = np.empty((B, T, D), np.float32)
    for c in range(8):
        b, par = c // 2, c % 2
        out[b, par * TO:(par + 1) * TO, :] = np.asarray(
            res_E.results[c]["out_T"]).T
    return out



NH0 = 4
LV = 12


def lmin_of_d(d):
    v = 128 * (d + 1)
    l = 0
    while (1 << l) < v:
        l += 1
    return min(l, 11)


def emit_l0_prep(P, xn, bufs, psF, cons, deps=()):
    wg_sb, wdl_sb = cons["wg_sb"], cons["wdl_sb"]
    bgs, Lps = cons["bgs_sb"], cons["lps_sb"]
    ident = cons["ident_f32"]
    rows = bufs["attn_rows"]
    lam_nat = bufs["lam_nat"]
    sp_scr = bufs["sp_scr"]
    spg_scr = bufs["spg_scr"]
    SPcol = bufs["SPcol"]
    negSPcol = bufs["negSPcol"]
    ps_dl = psF[0][0:48, :]
    ps_g = psF[1][0:4, :]
    ps_tr = psF[2]

    loglam = rows[0:48, :]
    SP = rows[64:68, :]
    sprow = rows[96:100, :]

    P.waits("pe", list(deps))
    t_dl_cons = None
    t_g_cons = None
    t_tr_cons = None
    t_ll = None
    for c in range(NCHK):
        cs = slice(c * CHK, (c + 1) * CHK)
        P.waits("pe", [t_dl_cons, t_ll])
        for m in range(NM):
            P.step("pe", lambda e, m=m, c=c: e.matmul(
                ps_dl, wdl_sb[:, m * 48:(m + 1) * 48],
                xn[m][:, c * CHK:(c + 1) * CHK],
                start=(m == 0), stop=(m == NM - 1)),
                inc=(1 if m == NM - 1 else 0))
        t_dlmm = P.tok("pe")
        P.wait("act", t_dlmm)
        P.step("act", lambda e: e.activation(out=sp_scr[0], in_=ps_dl, func=AF.Abs,
                                             bias=Lps[:, 0:1]))
        P.step("act", lambda e: e.activation(out=sp_scr[0], in_=sp_scr[0],
                                             func=AF.Exp, scale=-1.0))
        t_l = P.step("act", lambda e: e.activation(out=sp_scr[0], in_=sp_scr[0],
                                                   func=AF.Ln, bias=1.0))
        P.waits("dve", [t_dlmm, t_l, t_ll, t_tr_cons])
        P.step("dve", lambda e: e.tensor_scalar(out=sp_scr[1], in0=ps_dl,
                                                scalar1=Lps[:, 0:1], scalar2=0.0,
                                                op0=ALU.add, op1=ALU.max))
        P.step("dve", lambda e: e.tensor_tensor(out=sp_scr[1], in0=sp_scr[1],
                                                in1=sp_scr[0], op=ALU.add))
        t_lam = P.step("dve", lambda e: e.tensor_scalar_mul(
            out=sp_scr[1], in0=sp_scr[1], scalar1=1.0 / math.sqrt(128.0)), inc=1)
        t_dl_cons = t_lam
        P.wait("act", t_lam)
        t_ll = P.step("act", lambda e, cs=cs: e.activation(
            out=loglam[:, cs], in_=sp_scr[1], func=AF.Ln), inc=1)
        for tt in range(4):
            q = c * 4 + tt
            P.waits("pe", [t_lam, t_tr_cons])
            t_tr = P.step("pe", lambda e, tt=tt: e.transpose(
                out=ps_tr[:, 0:48], in_=sp_scr[1][0:48, tt * 128:(tt + 1) * 128],
                identity=ident[0:48, 0:48]), inc=1)
            P.wait("dve", t_tr)
            t_tr_cons = P.step("dve", lambda e, q=q: e.tensor_copy(
                out=lam_nat[:, q * 48:(q + 1) * 48], in_=ps_tr[:, 0:48]), inc=1)
        P.waits("pe", [t_g_cons])
        for m in range(NM):
            P.step("pe", lambda e, m=m, c=c: e.matmul(
                ps_g, wg_sb[:, m * 4:(m + 1) * 4],
                xn[m][:, c * CHK:(c + 1) * CHK],
                start=(m == 0), stop=(m == NM - 1)),
                inc=(1 if m == NM - 1 else 0))
        t_gmm = P.tok("pe")
        P.waits("act", [t_gmm, t_g_cons])
        P.step("act", lambda e: e.activation(out=spg_scr, in_=ps_g, func=AF.Abs,
                                             bias=bgs[:, 0:1]))
        P.step("act", lambda e: e.activation(out=spg_scr, in_=spg_scr,
                                             func=AF.Exp, scale=-1.0))
        t_gl = P.step("act", lambda e: e.activation(out=spg_scr, in_=spg_scr,
                                                    func=AF.Ln, bias=1.0))
        P.waits("dve", [t_gmm, t_gl])
        P.step("dve", lambda e, cs=cs: e.tensor_scalar(
            out=sprow[:, cs], in0=ps_g, scalar1=bgs[:, 0:1], scalar2=0.0,
            op0=ALU.add, op1=ALU.max))
        P.step("dve", lambda e, cs=cs: e.tensor_tensor(
            out=sprow[:, cs], in0=sprow[:, cs], in1=spg_scr, op=ALU.add))
        t_g_cons = P.step("dve", lambda e, cs=cs: e.tensor_scalar_mul(
            out=sprow[:, cs], in0=sprow[:, cs], scalar1=0.5), inc=1)
    t_scan = P.step("dve", lambda e: e.tensor_tensor_scan(
        out=SP, data0=sprow, data1=sprow, initial=0.0,
        op0=ALU.add, op1=ALU.add), inc=1)
    t_spc = None
    for k in range(32):
        P.waits("pe", [t_scan, t_spc])
        t_tr = P.step("pe", lambda e, k=k: e.transpose(
            out=ps_tr[:, 0:4], in_=SP[:, k * 128:(k + 1) * 128],
            identity=ident[64:68, 64:68]), inc=1)
        P.wait("dve", t_tr)
        t_spc = P.step("dve", lambda e, k=k: e.tensor_copy(
            out=SPcol[:, k * 4:(k + 1) * 4], in_=ps_tr[:, 0:4]), inc=1)
    t_neg = P.step("dve", lambda e: e.tensor_scalar_mul(
        out=negSPcol[:, :], in0=SPcol[:, :], scalar1=-1.0), inc=1)
    return t_neg


def emit_l0_head(P, h, xn, y_dram, bufs, psF, psB, cons):
    wq, wk, wv = bufs["wq"], bufs["wk"], bufs["wv"]
    QT, KT, Vnat = bufs["QT"], bufs["KT"], bufs["Vnat"]
    vtmp = bufs["vtmp"]
    rows = bufs["attn_rows"]
    lam_nat = bufs["lam_nat"]
    rowpool = bufs["rowpool"]
    Rsb = bufs["Rsb"]
    exinL = [bufs["exin"], bufs["exin_b"], bufs["exin_c"]]
    exptL = [bufs["expt"], bufs["expt_b"], bufs["expt_c"]]
    natx, nate, natl = bufs["natx"], bufs["nate"], bufs["natl"]
    dmx = bufs["dmx"]
    natA = bufs["natA"]
    natAT = bufs["natAT"]
    A_sb = bufs["A_sb"]
    ytile = bufs["ytile"]
    stT32, stT = bufs["stateT32"], bufs["stateT"]
    knat, vsc, qsc = bufs["knat"], bufs["vsc"], bufs["qsc"]
    gcol, mccol, dmcol = bufs["gcol"], bufs["mccol"], bufs["dmcol"]
    frow, spbnd = bufs["frow"], bufs["spbnd"]
    tmp11 = bufs["tmp11"]
    SPcol, negSPcol = bufs["SPcol"], bufs["negSPcol"]
    mnat, mst = cons["mnat_sb"], cons["mst_sb"]
    ones_r = cons["ones_r"]
    ones96 = bufs["ones96"]

    ps_sc3 = [psF[0][:, :], psF[1][:, :], psF[3][:, :]]
    ps_o = psF[2][:, :]
    ps_rb = psF[3][:, :]
    ps_dm = psF[4][:, 0:128]
    ps_st = psF[4][:, 384:512]
    ps_bt = psB[1][:, 0:512]

    loglam = rows[0:48, :]
    SP = rows[64:68, :]
    RSOFF = {0: 0, 1: 512, 2: 1024, 3: 1536, 5: 2048, 6: 2560}
    SPW = 3072
    RS = lambda i, w=512: rowpool[0:1, RSOFF[i]: RSOFF[i] + w]

    P.step("dve", lambda e: e.memset(ones96[:, :], 1.0))
    hd = bufs.get("head_dep")
    t_ld = [P.dma("wqkv", "gp", wq[:, :], cons["WqL_ap"][h], deps=[hd]),
            P.dma("wqkv", "gp", wk[:, :], cons["WkL_ap"][h]),
            P.dma("wqkv", "gp", wv[:, :], cons["WvL_ap"][h])]
    ev_bank = [None, None, None]
    t_vn_ev = None
    t_tr_prev = None
    for c in range(NCHK):
        for b, (wsb, dst) in enumerate([(wq, QT), (wk, KT)]):
            P.waits("pe", [t_ld[-1] if c == 0 else None, ev_bank[b]])
            for m in range(NM):
                P.step("pe", lambda e, wsb=wsb, m=m, c=c, b=b: e.matmul(
                    ps_sc3[b], wsb[:, m * 128:(m + 1) * 128],
                    xn[m][:, c * CHK:(c + 1) * CHK],
                    start=(m == 0), stop=(m == NM - 1)),
                    inc=(1 if m == NM - 1 else 0))
            t_mm = P.tok("pe")
            P.wait("act", t_mm)
            ev_bank[b] = P.step("act", lambda e, dst=dst, c=c, b=b: e.copy(
                out=dst[:, c * CHK:(c + 1) * CHK], in_=ps_sc3[b]), inc=1)
        P.waits("pe", [t_ld[-1] if c == 0 else None, ev_bank[2]])
        for m in range(NM):
            P.step("pe", lambda e, m=m, c=c: e.matmul(
                ps_sc3[2], wv[:, m * 128:(m + 1) * 128],
                xn[m][:, c * CHK:(c + 1) * CHK],
                start=(m == 0), stop=(m == NM - 1)),
                inc=(1 if m == NM - 1 else 0))
        t_vmm = P.tok("pe")
        P.waits("act", [t_vmm, t_tr_prev])
        t_vtmp = P.step("act", lambda e: e.copy(out=vtmp[:, :], in_=ps_sc3[2]),
                        inc=1)
        ev_bank[2] = t_vtmp
        P.waits("pe", [t_vtmp, t_vn_ev])
        for tt in range(4):
            P.step("pe", lambda e, tt=tt: e.transpose(
                out=ps_bt[:, tt * 128:(tt + 1) * 128],
                in_=vtmp[:, tt * 128:(tt + 1) * 128],
                identity=cons["ident_bf16"][:, :]), inc=(1 if tt == 3 else 0))
        t_tr_prev = P.tok("pe")
        P.wait("act", t_tr_prev)
        t_vn_ev = P.step("act", lambda e, c=c: e.copy(
            out=Vnat[:, c * 512:(c + 1) * 512], in_=ps_bt), inc=1)

    P.step("dve", lambda e: e.memset(exinL[0][:, :], 0.0))
    P.step("dve", lambda e: e.memset(exinL[1][:, :], 0.0))

    t_sc_free = [None, None, None]
    t_A_cons = [None, None, None]
    t_exp_done = [None, None, None]
    t_amult = [None, None, None]
    t_natAT_ev = None
    t_merge = None
    t_apply = bufs.get("apply_dep")
    t_st_mm = None
    t_kn_ev = None
    t_stT = None
    t_qsc = None
    t_y_ev = None
    t_fr = None
    t_d11 = None
    t_pe_prev = hd
    t_rowread = hd
    t_rb_ev = None
    t_vsc = None
    t_scn_cons = None
    for c in range(NCHK):
        nk = 4 * c + 4
        k0 = max(0, 4 * c - 8)
        cs = slice(c * CHK, (c + 1) * CHK)
        stg = []
        for slot, row_ap in [(0, loglam[12 * h + 8:12 * h + 9, cs]),
                             (1, loglam[12 * h + 9:12 * h + 10, cs]),
                             (2, loglam[12 * h + 10:12 * h + 11, cs]),
                             (3, loglam[12 * h + 11:12 * h + 12, cs])]:
            stg.append(P.dma("rstg", "sync", RS(slot), row_ap,
                             deps=[t_rowread, t_fr, t_d11]))
        w0 = max(0, (4 * c - 1) * 128)
        w1 = (4 * c + 4) * 128
        stg.append(P.dma("rstg", "sync", rowpool[0:1, SPW:SPW + (w1 - w0)],
                         SP[h:h + 1, w0:w1]))
        if c >= 4:
            stg.append(P.dma("rstg", "sync", spbnd[:, :],
                             SP[h:h + 1, 512 * (c - 1):512 * (c - 1) + 1],
                             deps=[t_d11]))
        spwin = lambda k, w0=w0: rowpool[0:1, SPW + k * 128 - w0:
                                         SPW + k * 128 - w0 + 128]
        spoff = SPW + (512 * c - w0)
        winSP = rowpool[0:1, spoff:spoff + 512]
        P.waits("dve", [stg[-1]])
        P.step("dve", lambda e: e.tensor_sub(out=RS(0), in0=RS(0), in1=RS(1)))
        P.step("dve", lambda e: e.tensor_copy(out=RS(5), in_=RS(1)))
        P.step("dve", lambda e: e.tensor_sub(out=RS(5), in0=RS(5), in1=RS(2)))
        P.step("dve", lambda e: e.tensor_copy(out=RS(6), in_=RS(2)))
        P.step("dve", lambda e: e.tensor_sub(out=RS(6), in0=RS(6), in1=RS(3)))
        P.step("dve", lambda e, winSP=winSP: e.tensor_sub(
            out=RS(1), in0=RS(1), in1=winSP))
        P.step("dve", lambda e, winSP=winSP: e.tensor_sub(
            out=RS(2), in0=RS(2), in1=winSP))
        t_drv = P.step("dve", lambda e, winSP=winSP: e.tensor_sub(
            out=RS(3), in0=RS(3), in1=winSP), inc=1)
        for i, slot in enumerate([1, 2, 3, 0, 5, 6]):
            bnk = i % 3
            P.waits("pe", [t_drv, t_sc_free[bnk],
                           ev_bank[bnk] if c == 0 else None])
            t_b = P.step("pe", lambda e, slot=slot, bnk=bnk: e.matmul(
                ps_sc3[bnk], ones96[:, :], RS(slot), start=True, stop=True),
                inc=1)
            P.wait("act", t_b)
            t_rb_ev = P.step("act", lambda e, i=i, bnk=bnk: e.copy(
                out=Rsb[:, i * 512:(i + 1) * 512], in_=ps_sc3[bnk]), inc=1)
            t_sc_free[bnk] = t_rb_ev

        def emit_state():
            nonlocal t_kn_ev, t_merge, t_apply, t_st_mm, t_stT, t_qsc, t_fr, \
                t_d11, t_b3_ready, t_natAT_ev, t_vsc, t_scn_cons, t_y_ev
            idxc = (4 * c) * 4 + h
            P.waits("pe", [t_rb_ev])
            t_mc = P.step("pe", lambda e, idxc=idxc: e.matmul(
                ps_rb[:, 0:1], ones_r[:, :], negSPcol[0:1, idxc:idxc + 1],
                start=True, stop=True), inc=1)
            P.wait("act", t_mc)
            t_mcc = P.step("act", lambda e: e.copy(out=mccol[:, :],
                                                   in_=ps_rb[:, 0:1]), inc=1)
            t_dmc = None
            if c >= 4:
                P.waits("dve", [stg[-1]])
                t_d11 = P.step("dve", lambda e, winSP=winSP: e.tensor_sub(
                    out=tmp11[:, :], in0=spbnd[:, :],
                    in1=winSP[0:1, 0:1]), inc=1)
                P.waits("pe", [t_d11, t_mcc])
                t_dmr = P.step("pe", lambda e: e.matmul(
                    ps_rb[:, 1:2], ones96[:, :], tmp11[:, :],
                    start=True, stop=True), inc=1)
                P.wait("act", t_dmr)
                t_dmc = P.step("act", lambda e: e.activation(
                    out=dmcol[:, :], in_=ps_rb[:, 1:2], func=AF.Exp), inc=1)
            newk = [4 * c - 12 + j for j in range(4)]
            for j, kj in enumerate(newk):
                P.step("act", lambda e, j=j, kj=kj: e.activation(
                    out=gcol[:, j:j + 1], in_=SPcol[:, kj * 4 + h:kj * 4 + h + 1],
                    func=AF.Exp, bias=mccol[:, 0:1]), inc=(1 if j == 3 else 0))
            t_gc = P.tok("act")
            P.waits("pe", [t_kn_ev, t_natAT_ev, t_vn_ev if c == 3 else None])
            for j, kj in enumerate(newk):
                P.step("pe", lambda e, j=j, kj=kj: e.transpose(
                    out=ps_bt[:, j * 128:(j + 1) * 128],
                    in_=KT[:, kj * 128:(kj + 1) * 128],
                    identity=cons["ident_bf16"][:, :]), inc=(1 if j == 3 else 0))
            t_ktr = P.tok("pe")
            P.wait("act", t_ktr)
            t_kn_ev = P.step("act", lambda e: e.copy(out=knat[:, :], in_=ps_bt),
                             inc=1)
            P.waits("dve", [t_gc, t_st_mm])
            for j, kj in enumerate(newk):
                P.step("dve", lambda e, j=j, kj=kj: e.tensor_scalar(
                    out=vsc[:, j * 128:(j + 1) * 128],
                    in0=Vnat[:, kj * 128:(kj + 1) * 128],
                    scalar1=gcol[:, j:j + 1], scalar2=None, op0=ALU.mult),
                    inc=(1 if j == 3 else 0))
            t_vsc = P.tok("dve")
            P.waits("pe", [t_kn_ev, t_vsc, t_scn_cons, t_merge])
            for j in range(4):
                P.step("pe", lambda e, j=j: e.matmul(
                    ps_st, knat[:, j * 128:(j + 1) * 128],
                    vsc[:, j * 128:(j + 1) * 128],
                    start=(j == 0), stop=(j == 3)), inc=(1 if j == 3 else 0))
            t_st_mm = P.tok("pe")
            if c == 3:
                P.waits("dve", [t_st_mm])
                t_merge = P.step("dve", lambda e: e.tensor_copy(
                    out=stT32[:, :], in_=ps_st), inc=1)
            else:
                P.waits("dve", [t_st_mm, t_dmc])
                t_merge = P.step("dve", lambda e: e.scalar_tensor_tensor(
                    out=stT32[:, :], in0=stT32[:, :], scalar=dmcol[:, 0:1],
                    in1=ps_st, op0=ALU.mult, op1=ALU.add), inc=1)
            P.waits("dve", [t_apply])
            t_stT = P.step("dve", lambda e: e.tensor_copy(out=stT[:, :],
                                                          in_=stT32[:, :]), inc=1)
            P.waits("act", [t_drv])
            t_fr = P.step("act", lambda e, winSP=winSP: e.activation(
                out=frow[:, :], in_=RS(3), func=AF.Exp,
                bias=winSP[0:1, 0:1]), inc=1)
            P.waits("pe", [t_fr, t_dmc if c >= 4 else t_mcc, t_sc_free[2]])
            t_fb = P.step("pe", lambda e: e.matmul(
                ps_rb, ones_r[:, :], frow[:, :], start=True, stop=True), inc=1)
            P.waits("dve", [t_fb, t_apply])
            t_qsc = P.step("dve", lambda e, cs=cs: e.tensor_tensor(
                out=qsc[:, :], in0=QT[:, cs], in1=ps_rb, op=ALU.mult), inc=1)
            P.waits("pe", [t_stT, t_qsc])
            t_apply = P.step("pe", lambda e: e.matmul(
                ps_o, stT[:, :], qsc[:, :], start=False, stop=False), inc=1)
            t_b3_ready = t_qsc


        t_b3_ready = t_rb_ev
        pend = []
        amq = []
        first_av = [True]
        done = [False]

        def flush_av():
            k2, slot2, tA2 = pend.pop(0)
            first = first_av[0]
            last = (len(pend) == 0 and len(amq) == 0 and done[0])
            first_av[0] = False
            P.waits("pe", [tA2, t_y_ev if first else None])
            t_av = P.step("pe", lambda e, k2=k2, slot2=slot2, first=first,
                          last=last:
                          e.matmul(ps_o, Vnat[:, k2 * 128:(k2 + 1) * 128],
                                   A_sb[slot2], start=first, stop=last),
                          inc=1)
            t_A_cons[slot2] = t_av
            return t_av

        def emit_amult(ent):
            k, slot, pp, b, t_s, t_e = ent
            P.waits("dve", [t_e, t_s, t_A_cons[slot]])
            t_A = P.step("dve", lambda e, slot=slot, pp=pp, b=b: e.tensor_tensor(
                out=A_sb[slot], in0=exptL[pp], in1=ps_sc3[b], op=ALU.mult), inc=1)
            t_amult[pp] = t_A
            t_sc_free[b] = t_A
            pend.append((k, slot, t_A))

        for idx, k in enumerate(range(k0, nk)):
            b = idx % 3
            pp = idx % 3
            slot = idx % 3
            if len(amq) >= 3:
                emit_amult(amq.pop(0))
            P.waits("pe", [t_sc_free[b], t_b3_ready if b == 2 else None])
            t_s = P.step("pe", lambda e, k=k, cs=cs, b=b: e.matmul(
                ps_sc3[b], KT[:, k * 128:(k + 1) * 128], QT[:, cs],
                start=True, stop=True), inc=1)
            if len(pend) > 2:
                flush_av()
            if c >= 3 and idx == 5:
                emit_state()
            zero_subs = []
            corr_subs = []
            plain_subs = []
            P.waits("dve", [t_exp_done[pp], t_rb_ev])
            for sub in range(4):
                d = 4 * c + sub - k
                ss = slice(sub * 128, (sub + 1) * 128)
                if d in (0, 1) or d < 0:
                    zero_subs.append(ss)
                    continue
                if d >= 9:
                    roff = 1024 + sub * 128
                elif d in (2, 3):
                    roff = sub * 128
                elif d in (4, 5, 6, 7):
                    roff = 512 + sub * 128
                else:
                    roff = 1024 + sub * 128
                if d in (2, 4, 8):
                    mcol = {2: 0, 4: 128, 8: 256}[d]
                    rdoff = {2: 3 * 512, 4: 4 * 512, 8: 5 * 512}[d] + sub * 128
                    P.step("dve", lambda e, mcol=mcol, rdoff=rdoff:
                           e.scalar_tensor_tensor(
                        out=dmx, in0=Rsb[:, rdoff:rdoff + 128], scalar=1.0,
                        in1=mst[:, mcol:mcol + 128], op0=ALU.mult, op1=ALU.mult))
                    P.step("dve", lambda e, ss=ss, pp=pp, roff=roff:
                           e.tensor_tensor(
                        out=exinL[pp][:, ss], in0=Rsb[:, roff:roff + 128],
                        in1=dmx, op=ALU.add))
                    corr_subs.append(ss)
                else:
                    plain_subs.append((ss, roff))
            t_asm = P.tok("dve")
            P.waits("act", [t_asm, t_amult[pp]])
            for ss, roff in plain_subs:
                P.step("act", lambda e, ss=ss, roff=roff, pp=pp, k=k:
                       e.activation(
                    out=exptL[pp][:, ss], in_=Rsb[:, roff:roff + 128],
                    func=AF.Exp, bias=SPcol[:, (k * 4 + h):(k * 4 + h + 1)]))
            for ss in corr_subs:
                P.step("act", lambda e, ss=ss, pp=pp, k=k: e.activation(
                    out=exptL[pp][:, ss], in_=exinL[pp][:, ss],
                    func=AF.Exp, bias=SPcol[:, (k * 4 + h):(k * 4 + h + 1)]))
            for ss in zero_subs:
                P.step("dve", lambda e, ss=ss, pp=pp: e.memset(
                    exptL[pp][:, ss], 0.0))
            t_e = P.tok("act")
            t_exp_done[pp] = t_e
            amq.append((k, slot, pp, b, t_s, t_e))
        while amq:
            while len(pend) > 2:
                flush_av()
            emit_amult(amq.pop(0))

        nat_pairs = []
        for sub in range(4):
            q = 4 * c + sub
            nat_pairs.append((q, sub, q - 1 >= 0))
        ps_scn2 = [psF[5][:, 0:256], psF[5][:, 256:512]]
        ps_sn = psF[4][:, 128:384]
        ps_nt2 = [psB[1][:, 512:768], psB[1][:, 768:1024]]
        t_natx2 = [None, None]
        t_sn_cons = None
        t_tr_prev_nat = None
        t_ev2 = [t_natAT_ev, t_natAT_ev]
        t_avn2 = [None, None]
        if first_av[0] and pend:
            flush_av()

        def nat_scores(j):
            q, sub, hasd1 = nat_pairs[j]
            r = j % 2
            P.waits("pe", [t_natx2[r], t_sn_cons,
                           t_merge if j == 0 else None,
                           stg[-1] if j == 0 else None])
            P.step("pe", lambda e, q=q, r=r: e.matmul(
                ps_scn2[r][:, 0:128], QT[:, q * 128:(q + 1) * 128],
                KT[:, q * 128:(q + 1) * 128], start=True, stop=True))
            if hasd1:
                P.step("pe", lambda e, q=q, r=r: e.matmul(
                    ps_scn2[r][:, 128:256], QT[:, q * 128:(q + 1) * 128],
                    KT[:, (q - 1) * 128:q * 128], start=True, stop=True))
            P.step("pe", lambda e, q=q, spwin=spwin: e.matmul(
                ps_sn[:, 0:128], ones96[:, :], spwin(q),
                start=True, stop=True), inc=(0 if hasd1 else 1))
            if hasd1:
                P.step("pe", lambda e, q=q, spwin=spwin: e.matmul(
                    ps_sn[:, 128:256], ones96[:, :], spwin(q - 1),
                    start=True, stop=True), inc=1)
            return P.tok("pe")

        def nat_dve(j, t_sb):
            nonlocal t_scn_cons, t_sn_cons
            q, sub, hasd1 = nat_pairs[j]
            r = j % 2
            w = 256 if hasd1 else 128
            P.waits("dve", [t_sb])
            t_nx = P.step("dve", lambda e, q=q, w=w: e.tensor_scalar(
                out=natx[:, 0:w], in0=ps_sn[:, 0:w],
                scalar1=negSPcol[:, (q * 4 + h):(q * 4 + h + 1)],
                scalar2=0.0, op0=ALU.add, op1=ALU.min), inc=1)
            t_sn_cons = t_nx
            P.wait("act", t_nx)
            t_ne = P.step("act", lambda e, w=w: e.activation(
                out=nate[:, 0:w], in_=natx[:, 0:w], func=AF.Exp), inc=1)
            first = True
            for l in range(0, 9):
                if not hasd1 and l == 8:
                    continue
                lam_col = lam_nat[:, q * 48 + h * 12 + l:
                                  q * 48 + h * 12 + l + 1]
                mk = mnat[:, l * 256: l * 256 + w]
                if first:
                    P.step("dve", lambda e, mk=mk, lam_col=lam_col, w=w:
                           e.tensor_scalar(out=natl[:, 0:w], in0=mk,
                                           scalar1=lam_col, scalar2=None,
                                           op0=ALU.mult))
                    first = False
                else:
                    P.step("dve", lambda e, mk=mk, lam_col=lam_col, w=w:
                           e.scalar_tensor_tensor(
                               out=natl[:, 0:w], in0=mk, scalar=lam_col,
                               in1=natl[:, 0:w], op0=ALU.mult, op1=ALU.add))
            P.waits("dve", [t_ne, t_tr_prev_nat])
            P.step("dve", lambda e, r=r, w=w: e.tensor_tensor(
                out=natx[:, 0:w], in0=nate[:, 0:w], in1=ps_scn2[r][:, 0:w],
                op=ALU.mult))
            t_An = P.step("dve", lambda e, w=w: e.tensor_tensor(
                out=natA[:, 0:w], in0=natx[:, 0:w], in1=natl[:, 0:w],
                op=ALU.mult), inc=1)
            t_natx2[r] = t_An
            t_scn_cons = t_An
            return t_An

        def nat_pe_tail(j, t_An):
            nonlocal t_tr_prev_nat
            q, sub, hasd1 = nat_pairs[j]
            r = j % 2
            nh = 2 if hasd1 else 1
            P.waits("pe", [t_An, t_ev2[r]])
            for i2 in range(nh):
                P.step("pe", lambda e, r=r, i2=i2: e.transpose(
                    out=ps_nt2[r][:, i2 * 128:(i2 + 1) * 128],
                    in_=natA[:, i2 * 128:(i2 + 1) * 128],
                    identity=cons["ident_bf16"][:, :]),
                    inc=(1 if i2 == nh - 1 else 0))
            t_tr = P.tok("pe")
            t_tr_prev_nat = t_tr
            P.waits("act", [t_tr, t_avn2[r]])
            t_ev = P.step("act", lambda e, r=r, nh=nh: e.copy(
                out=natAT[r][:, 0:nh * 128], in_=ps_nt2[r][:, 0:nh * 128]),
                inc=1)
            t_ev2[r] = t_ev
            P.waits("pe", [t_ev])
            for i2 in range(nh):
                kk = q if i2 == 0 else q - 1
                P.step("pe", lambda e, kk=kk, sub=sub, r=r, i2=i2: e.matmul(
                    ps_o[:, sub * 128:(sub + 1) * 128],
                    Vnat[:, kk * 128:(kk + 1) * 128],
                    natAT[r][:, i2 * 128:(i2 + 1) * 128],
                    start=False, stop=False), inc=(1 if i2 == nh - 1 else 0))
            t_avn = P.tok("pe")
            t_avn2[r] = t_avn

        prev = None
        for j in range(len(nat_pairs)):
            t_sb = nat_scores(j)
            t_rowread = t_sb
            if prev is not None:
                nat_pe_tail(prev[0], prev[1])
            t_An = nat_dve(j, t_sb)
            prev = (j, t_An)
        if prev is not None:
            nat_pe_tail(prev[0], prev[1])
        t_natAT_ev = t_ev2[(len(nat_pairs) - 1) % 2]
        done[0] = True
        while pend:
            flush_av()
        t_pe_prev = P.tok("pe")
        P.waits("act", [t_pe_prev, bufs.get("last_yst")])
        t_y = P.step("act", lambda e: e.copy(out=ytile[:, :], in_=ps_o), inc=1)
        t_y_ev = t_y
        bufs["last_yst"] = P.dma("yst", "sync",
                                 y_dram[h, :, c * CHK:(c + 1) * CHK], ytile[:, :],
                                 deps=[t_y])
    bufs["apply_dep"] = t_apply
    return P.tok("pe")


def build_launch_B():
    nc = bass.Bass("TRN2", target_bir_lowering=False, debug=False)
    xnin = nc.dram_tensor("xnin", [D, T], BF16, kind="ExternalInput").ap()
    WqL = nc.dram_tensor("WqL", [NH0, 128, 1024], F32, kind="ExternalInput").ap()
    WkL = nc.dram_tensor("WkL", [NH0, 128, 1024], F32, kind="ExternalInput").ap()
    WvL = nc.dram_tensor("WvL", [NH0, 128, 1024], F32, kind="ExternalInput").ap()
    WgT = nc.dram_tensor("WgT", [128, NM * 4], F32, kind="ExternalInput").ap()
    WdlT = nc.dram_tensor("WdlT", [128, NM * 48], F32, kind="ExternalInput").ap()
    bg_in = nc.dram_tensor("bg_in", [4, 1], F32, kind="ExternalInput").ap()
    lp_in = nc.dram_tensor("lp_in", [48, 1], F32, kind="ExternalInput").ap()
    mnat_in = nc.dram_tensor("mnat_in", [128, 2304], F32, kind="ExternalInput").ap()
    mst_in = nc.dram_tensor("mst_in", [128, 3 * 128], F32, kind="ExternalInput").ap()
    ident_in = nc.dram_tensor("ident_in", [128, 128], F32, kind="ExternalInput").ap()
    y_dram = nc.dram_tensor("y_dram", [NH0, 128, T], BF16, kind="ExternalOutput").ap()

    ctx = ExitStack()
    with ctx:
        E = ctx.enter_context
        P = Prog(nc, ctx)
        poolA = E(nc.sbuf_tensor("poolA", [128, 16384], F32))
        poolB = E(nc.sbuf_tensor("poolB", [128, 32768], BF16))
        poolC = E(nc.sbuf_tensor("poolC", [128, 12288], BF16))
        poolS = E(nc.sbuf_tensor("poolS", [128, 2048], F32))
        poolW = E(nc.sbuf_tensor("poolW", [128, 10240], BF16))
        ones_r = E(nc.sbuf_tensor("ones_r", [1, 128], F32))
        ident_f32 = E(nc.sbuf_tensor("ident_f32", [128, 128], F32))
        ident_bf16 = E(nc.sbuf_tensor("ident_bf16", [128, 128], BF16))
        wg_sb = E(nc.sbuf_tensor("wg_sb", [128, NM * 4], BF16))
        wdl_sb = E(nc.sbuf_tensor("wdl_sb", [128, NM * 48], BF16))
        bgs_sb = E(nc.sbuf_tensor("bgs_sb", [4, 1], F32))
        lps_sb = E(nc.sbuf_tensor("lps_sb", [48, 1], F32))
        mnat_sb = E(nc.sbuf_tensor("mnat_sb", [128, 2304], F32))
        mst_sb = E(nc.sbuf_tensor("mst_sb", [128, 3 * 128], F32))
        psF = [E(nc.psum_tensor(f"psF{i}", [128, 512], F32)) for i in range(6)]
        psB = [E(nc.psum_tensor(f"psB{i}", [128, 1024], BF16)) for i in range(2)]
        blk = E(nc.Block())

        cons = dict(ones_r=ones_r, ident_f32=ident_f32, ident_bf16=ident_bf16,
                    wg_sb=wg_sb, wdl_sb=wdl_sb, bgs_sb=bgs_sb, lps_sb=lps_sb,
                    mnat_sb=mnat_sb, mst_sb=mst_sb,
                    WqL_ap=[WqL[hh, :, :] for hh in range(NH0)],
                    WkL_ap=[WkL[hh, :, :] for hh in range(NH0)],
                    WvL_ap=[WvL[hh, :, :] for hh in range(NH0)])

        t_init = P.step("dve", lambda e: e.memset(ones_r[:, :], 1.0), inc=1)
        P.wait("pe", t_init)
        tc = [
            P.dma("cns", "sync", ident_f32[:, :], ident_in),
            P.dma("cns", "sync", bgs_sb[:, :], bg_in),
            P.dma("cns", "sync", lps_sb[:, :], lp_in),
            P.dma("cns", "sync", mnat_sb[:, :], mnat_in),
            P.dma("cns", "sync", mst_sb[:, :], mst_in),
            P.dma("cnsg", "gp", ident_bf16[:, :], ident_in),
            P.dma("cnsg", "gp", wg_sb[:, :], WgT),
            P.dma("cnsg", "gp", wdl_sb[:, :], WdlT),
        ]
        xn = [poolB[:, m * T:(m + 1) * T] for m in range(NM)]
        t_xn = [P.dma("xnf", "sync", xn[m], xnin[m * 128:(m + 1) * 128, :])
                for m in range(NM)]
        for e in ("pe", "act", "dve", "gp"):
            P.waits(e, [t_xn[-1], tc[-1], P.dtok("cns")])

        bufs = dict(
            attn_rows=poolA[:, 0:4096],
            lam_nat=poolA[:, 4096:5632],
            rowpool=poolA[64:65, 11648:15360],
            Rsb=poolA[:, 5632:8704],
            exin=poolA[:, 8704:9216],
            expt=poolA[:, 9216:9728],
            exin_b=poolA[:, 9728:10240],
            expt_b=poolA[:, 10240:10752],
            exin_c=poolA[:, 15360:15872],
            expt_c=poolA[:, 15872:16384],
            natx=poolA[:, 10752:11008],
            nate=poolA[:, 11008:11264],
            natl=poolA[:, 11264:11520],
            dmx=poolA[:, 11520:11648],
            sp_scr=[poolA[:, 11648:12160][0:48, :], poolA[:, 12160:12672][0:48, :]],
            spg_scr=poolA[96:100, 12672:13184],
            SPcol=poolS[:, 0:128],
            negSPcol=poolS[:, 128:256],
            QT=poolC[:, 0:4096],
            KT=poolC[:, 4096:8192],
            Vnat=poolC[:, 8192:12288],
            vtmp=poolW[:, 2048:2560],
            A_sb=[poolW[:, 0:512], poolW[:, 512:1024], poolW[:, 2048:2560]],
            ytile=poolW[:, 1024:1536],
            natA=poolW[:, 1536:1792],
            wq=poolW[:, 2560:3584],
            wk=poolW[:, 3584:4608],
            wv=poolW[:, 4608:5632],
            stateT32=poolS[:, 256:384],
            gcol=poolS[:, 384:388],
            mccol=poolS[:, 388:389],
            dmcol=poolS[:, 389:390],
            frow=poolS[0:1, 390:902],
            spbnd=poolS[64:65, 902:903],
            tmp11=poolS[64:65, 903:904],
            ones96=poolS[64:65, 904:1032],
            stateT=poolW[:, 5632:5760],
            knat=poolW[:, 5760:6272],
            vsc=poolW[:, 6272:6784],
            qsc=poolW[:, 6784:7296],
            natAT=[poolW[:, 7296:7552], poolW[:, 7552:7808]],
        )
        t_prep = emit_l0_prep(P, xn, bufs, psF, cons)
        P.barrier_all()
        import os
        nheads = int(os.environ.get("L0_NHEADS", str(NH0)))
        for h in range(nheads):
            tk = emit_l0_head(P, h, xn, y_dram, bufs, psF, psB, cons)
            bufs["head_dep"] = tk
        P.wait("sync", P.dtok("yst"))
        P.finalize(blk)
    return nc


def prep_B_inputs(inp):
    Wq = np.asarray(inp["L0_attn_Wq"])
    Wk = np.asarray(inp["L0_attn_Wk"])
    Wv = np.asarray(inp["L0_attn_Wv"])
    Wg = np.asarray(inp["L0_attn_Wg"])
    bg = np.asarray(inp["L0_attn_bg"])
    Wdl = np.asarray(inp["L0_attn_Wdl"])
    Lp = np.asarray(inp["L0_attn_Lparam"])
    ident = np.eye(128, dtype=np.float32)

    def wslice(W, hbase, hd):
        Ws = W.reshape(8 * hd // hd, hd, NM, 128)
        Ws = W.reshape(8, hd, NM, 128)
        out = Ws[hbase:hbase + NH0].transpose(0, 3, 2, 1).reshape(NH0, 128, NM * hd)
        return np.ascontiguousarray(out)

    pp = np.arange(128)[:, None]
    ff = np.arange(128)[None, :]

    def lvl_of(d):
        d = np.asarray(d)
        with np.errstate(divide="ignore"):
            l = np.where(d <= 0, 0,
                         np.floor(np.log2(np.maximum(d, 1))).astype(np.int64) + 1)
        return np.minimum(l, 11)

    mnat = np.zeros((128, 9 * 256), np.float32)
    d0 = pp - ff
    d1 = 128 + pp - ff
    for l in range(9):
        if l <= 7:
            mnat[:, l * 256:l * 256 + 128] = ((d0 >= 0) & (lvl_of(d0) == l))
        if l >= 1:
            mnat[:, l * 256 + 128:(l + 1) * 256] = (lvl_of(d1) == l)
    mst = np.zeros((128, 3 * 128), np.float32)
    for i in range(3):
        mst[:, i * 128:(i + 1) * 128] = (ff < pp)

    in_maps = []
    for c in range(8):
        par = c % 2
        hbase = par * NH0
        wg = np.ascontiguousarray(
            Wg[hbase:hbase + NH0].reshape(NH0, NM, 128).transpose(2, 1, 0)
            .reshape(128, NM * NH0))
        wdl = np.ascontiguousarray(
            Wdl.reshape(8, 12, NM, 128)[hbase:hbase + NH0]
            .transpose(3, 2, 0, 1).reshape(128, NM * 48))
        in_maps.append({
            "WqL": wslice(Wq, hbase, 128), "WkL": wslice(Wk, hbase, 128),
            "WvL": wslice(Wv, hbase, 128),
            "WgT": wg, "WdlT": wdl,
            "bg_in": np.ascontiguousarray(bg[hbase:hbase + NH0][:, None]),
            "lp_in": np.ascontiguousarray(
                Lp[hbase:hbase + NH0].reshape(48)[:, None]),
            "mnat_in": mnat, "mst_in": mst, "ident_in": ident,
        })
    return in_maps



def emit_ln_stats_apply(P, src_tiles, cs_w, bufs, psF, cons, out_write,
                        affine=None, eps=LN_EPS):
    ones_c, ones_r = cons["ones_c"], cons["ones_r"]
    xsq = bufs["xsq"]
    mu, msq, var, rstd, mur = bufs["rows"]
    ps_stat, ps_stat2 = psF[0][0:1, 0:cs_w], psF[1][0:1, 0:cs_w]
    ps_b1, ps_b2 = psF[2][:, 0:cs_w], psF[3][:, 0:cs_w]

    for m in range(NM):
        P.step("pe", lambda e, m=m: e.matmul(
            ps_stat, ones_c[:, :], src_tiles[m],
            start=(m == 0), stop=(m == NM - 1)), inc=(1 if m == NM - 1 else 0))
    t_sum = P.tok("pe")
    last_sq_mm = [None, None]
    for m in range(NM):
        P.waits("act", [last_sq_mm[m % 2]])
        t_sq = P.step("act", lambda e, m=m: e.activation(
            out=xsq[m % 2], in_=src_tiles[m], func=AF.Square), inc=1)
        P.wait("pe", t_sq)
        last_sq_mm[m % 2] = P.step("pe", lambda e, m=m: e.matmul(
            ps_stat2, ones_c[:, :], xsq[m % 2][:, 0:cs_w],
            start=(m == 0), stop=(m == NM - 1)), inc=1)
    t_sq2 = P.tok("pe")
    P.waits("dve", [t_sum, t_sq2])
    P.step("dve", lambda e: e.tensor_scalar_mul(out=mu[0:1, 0:cs_w], in0=ps_stat,
                                                scalar1=1.0 / D))
    P.step("dve", lambda e: e.tensor_scalar_mul(out=msq[0:1, 0:cs_w], in0=ps_stat2,
                                                scalar1=1.0 / D))
    P.step("dve", lambda e: e.tensor_mul(out=var[0:1, 0:cs_w], in0=mu[0:1, 0:cs_w],
                                         in1=mu[0:1, 0:cs_w]))
    t_var = P.step("dve", lambda e: e.tensor_sub(
        out=var[0:1, 0:cs_w], in0=msq[0:1, 0:cs_w], in1=var[0:1, 0:cs_w]), inc=1)
    P.wait("act", t_var)
    P.step("act", lambda e: e.activation(out=rstd[0:1, 0:cs_w],
                                         in_=var[0:1, 0:cs_w], func=AF.Ln,
                                         bias=cons["eps_b"][0:1, 0:1]))
    t_rstd = P.step("act", lambda e: e.activation(out=rstd[0:1, 0:cs_w],
                                                  in_=rstd[0:1, 0:cs_w],
                                                  func=AF.Exp, scale=-0.5), inc=1)
    P.wait("dve", t_rstd)
    t_mur = P.step("dve", lambda e: e.tensor_mul(
        out=mur[0:1, 0:cs_w], in0=mu[0:1, 0:cs_w], in1=rstd[0:1, 0:cs_w]), inc=1)
    P.waits("pe", [t_rstd, t_mur])
    P.step("pe", lambda e: e.matmul(ps_b1, ones_r[:, :], rstd[0:1, 0:cs_w],
                                    start=True, stop=True))
    t_bc = P.step("pe", lambda e: e.matmul(ps_b2, ones_r[:, :], mur[0:1, 0:cs_w],
                                           start=True, stop=True), inc=1)
    P.wait("dve", t_bc)
    toks = []
    for m in range(NM):
        toks.append(out_write(m, ps_b1, ps_b2))
    return toks[-1]


def build_launch_C():
    nc = bass.Bass("TRN2", target_bir_lowering=False, debug=False)
    yT_in = nc.dram_tensor("yT_in", [NM, 128, T], BF16, kind="ExternalInput").ap()
    xn0_in = nc.dram_tensor("xn0_in", [D, T], BF16, kind="ExternalInput").ap()
    moe_in = nc.dram_tensor("moe_in", [D, TO], F32, kind="ExternalInput").ap()
    x0T = nc.dram_tensor("x0T", [D, T], F32, kind="ExternalInput").ap()
    WoL = nc.dram_tensor("WoL", [NM, 128, NM * 128], F32, kind="ExternalInput").ap()
    lngb = nc.dram_tensor("lngb", [128, 2 * NM], F32, kind="ExternalInput").ap()
    par_in = nc.dram_tensor("par_in", [1, 1], I32, kind="ExternalInput").ap()
    ident_in = nc.dram_tensor("ident_in", [128, 128], F32, kind="ExternalInput").ap()

    x1_out = nc.dram_tensor("x1_out", [D, TO], F32, kind="ExternalOutput").ap()
    xn1_out = nc.dram_tensor("xn1_out", [D, TO], BF16, kind="ExternalOutput").ap()
    sdbg_out = nc.dram_tensor("sdbg_out", [D, TO], F32, kind="ExternalOutput").ap()
    s2dbg_out = nc.dram_tensor("s2dbg_out", [D, TO], BF16, kind="ExternalOutput").ap()

    ctx = ExitStack()
    with ctx:
        E = ctx.enter_context
        P = Prog(nc, ctx)
        poolA = E(nc.sbuf_tensor("poolA", [128, 16384], F32))
        poolB = E(nc.sbuf_tensor("poolB", [128, 32768], BF16))
        poolC = E(nc.sbuf_tensor("poolC", [128, 16384], BF16))
        poolS = E(nc.sbuf_tensor("poolS", [128, 5120], F32))
        ones_c = E(nc.sbuf_tensor("ones_c", [128, 1], F32))
        ones_r = E(nc.sbuf_tensor("ones_r", [1, 128], F32))
        eps_b = E(nc.sbuf_tensor("eps_b", [1, 1], F32))
        lngb_sb = E(nc.sbuf_tensor("lngb_sb", [128, 2 * NM], F32))
        par_sb = E(nc.sbuf_tensor("par_sb", [1, 1], I32))
        psF = [E(nc.psum_tensor(f"psF{i}", [128, 512], F32)) for i in range(6)]
        blk = E(nc.Block())

        cons = dict(ones_c=ones_c, ones_r=ones_r, eps_b=eps_b)

        P.step("dve", lambda e: e.memset(ones_c[:, :], 1.0))
        P.step("dve", lambda e: e.memset(eps_b[:, :], LN_EPS))
        t_init = P.step("dve", lambda e: e.memset(ones_r[:, :], 1.0), inc=1)
        P.wait("pe", t_init)
        P.wait("act", t_init)
        emit_load_parity(P, par_in, par_sb, engines=("gp",))
        tcn = P.dma("cns", "sync", lngb_sb[:, :], lngb)

        yT = [poolB[:, ht * TO:(ht + 1) * TO] for ht in range(NM)]

        def go_y(e, sem):
            par = P.holder["par"]
            with e.If_eq(par, 0):
                for ht in range(NM):
                    e.dma_start(out=yT[ht], in_=yT_in[ht, :, 0:TO]).then_inc(sem, 16)
            with e.Else():
                for ht in range(NM):
                    e.dma_start(out=yT[ht], in_=yT_in[ht, :, TO:T]).then_inc(sem, 16)
        t_y = P.raw("yld", "gp", go_y, inc=16 * NM)
        xn_own = [poolC[:, m * TO:(m + 1) * TO] for m in range(NM)]
        t_xn = emit_read_own(P, xn0_in, xn_own)

        s_pre = [poolA[:, m * TO:(m + 1) * TO] for m in range(NM)]
        s_sb = [poolB[:, 16384 + m * TO: 16384 + (m + 1) * TO] for m in range(NM)]
        ln_bufs = dict(
            xsq=[poolS[:, 0:512], poolS[:, 512:1024]],
            rows=[poolS[0:1, 1024 + i * 512: 1024 + (i + 1) * 512] for i in range(5)],
        )
        g_sb = poolS[:, 4608:5120]

        NCH = TO // CHK
        t_spre = None
        woT = [poolB[:, 24576 + ht * 1024: 24576 + (ht + 1) * 1024]
               for ht in range(NM)]
        t_wo = [P.dma("wold", "gp", woT[ht], WoL[ht, :, :]) for ht in range(NM)]

        t_soff = None
        for c in range(NCH):
            cs = slice(c * CHK, (c + 1) * CHK)
            for mt in range(NM):
                P.waits("pe", [t_y, t_wo[-1], t_spre])
                for ht in range(NM):
                    P.step("pe", lambda e, ht=ht, mt=mt, cs=cs: e.matmul(
                        psF[0][:, :], woT[ht][:, mt * 128:(mt + 1) * 128],
                        yT[ht][:, cs], start=(ht == 0), stop=(ht == NM - 1)),
                        inc=(1 if ht == NM - 1 else 0))
                t_mm = P.tok("pe")
                P.waits("dve", [t_mm, t_xn])
                t_spre = P.step("dve", lambda e, mt=mt, cs=cs: e.tensor_tensor(
                    out=s_pre[mt][:, cs], in0=psF[0][:, :], in1=xn_own[mt][:, cs],
                    op=ALU.add), inc=1)
        P.barrier_all()
        for m in range(NM):
            P.dma("sdbg", "sync", sdbg_out[m * 128:(m + 1) * 128, :], s_pre[m])
        P.barrier_all()

        for c in range(NCH):
            cs = slice(c * CHK, (c + 1) * CHK)

            def write_s(m, ps_b1, ps_b2, cs=cs):
                P.step("dve", lambda e, m=m: e.tensor_mul(
                    out=ln_bufs["xsq"][m % 2], in0=s_pre[m][:, cs], in1=ps_b1))
                P.step("dve", lambda e, m=m: e.tensor_sub(
                    out=ln_bufs["xsq"][m % 2], in0=ln_bufs["xsq"][m % 2], in1=ps_b2))
                return P.step("dve", lambda e, m=m: e.tensor_scalar(
                    out=s_sb[m][:, cs], in0=ln_bufs["xsq"][m % 2],
                    scalar1=lngb_sb[:, m:m + 1], scalar2=lngb_sb[:, NM + m:NM + m + 1],
                    op0=ALU.mult, op1=ALU.add), inc=1)
            P.wait("dve", tcn)
            src = [s_pre[m][:, cs] for m in range(NM)]
            t_ln2 = emit_ln_stats_apply(P, src, CHK, ln_bufs, psF, cons, write_s)
            P.barrier_all()

            x1c = [poolA[:, m * TO + c * CHK: m * TO + (c + 1) * CHK]
                   for m in range(NM)]
            t_g = None
            for m in range(NM):
                t_moe = P.dma("moeld", "sync", poolS[:, 4096:4608],
                              moe_in[m * 128:(m + 1) * 128, cs], deps=[t_g])

                def go_x0(e, sem, m=m, cs=cs, c=c):
                    par = P.holder["par"]
                    with e.If_eq(par, 0):
                        e.dma_start(out=poolS[:, 3584:4096],
                                    in_=x0T[m * 128:(m + 1) * 128, cs]).then_inc(sem, 16)
                    with e.Else():
                        e.dma_start(out=poolS[:, 3584:4096],
                                    in_=x0T[m * 128:(m + 1) * 128,
                                            TO + c * CHK:TO + (c + 1) * CHK]
                                    ).then_inc(sem, 16)
                t_x0 = P.raw("x0ld", "gp", go_x0, inc=16)
                P.waits("act", [t_moe])
                t_g = P.step("act", lambda e: e.activation(
                    out=g_sb, in_=poolS[:, 4096:4608], func=AF.Silu), inc=1)
                P.waits("dve", [t_g, t_ln2, t_x0])
                P.step("dve", lambda e, m=m, cs=cs: e.tensor_mul(
                    out=g_sb, in0=g_sb, in1=s_sb[m][:, cs]))
                t_x1 = P.step("dve", lambda e, m=m, x1c=x1c: e.tensor_tensor(
                    out=x1c[m], in0=poolS[:, 3584:4096], in1=g_sb, op=ALU.add),
                    inc=1)
                P.dma("x1st", "sync", x1_out[m * 128:(m + 1) * 128, cs], x1c[m],
                      deps=[t_x1])
                P.waits("gp", [t_x1])
                P.waits("sync", [t_x1])
            P.barrier_all()
            xn1c = [poolC[:, m * TO + c * CHK: m * TO + (c + 1) * CHK]
                    for m in range(NM)]

            def write_xn1(m, ps_b1, ps_b2, cs=cs, x1c=x1c, xn1c=xn1c):
                P.step("dve", lambda e, m=m, x1c=x1c: e.tensor_mul(
                    out=ln_bufs["xsq"][m % 2], in0=x1c[m], in1=ps_b1))
                t = P.step("dve", lambda e, m=m, xn1c=xn1c: e.tensor_sub(
                    out=xn1c[m], in0=ln_bufs["xsq"][m % 2], in1=ps_b2), inc=1)
                P.dma("xn1st", "sync", xn1_out[m * 128:(m + 1) * 128, cs],
                      xn1c[m], deps=[t])
                return t
            t_ln1 = emit_ln_stats_apply(P, x1c, CHK, ln_bufs, psF, cons, write_xn1)
            P.barrier_all()
        for m in range(NM):
            P.dma("s2dbg", "sync", s2dbg_out[m * 128:(m + 1) * 128, :], s_sb[m])
        P.wait("sync", P.dtok("s2dbg"))
        P.wait("sync", P.dtok("x1st"))
        P.wait("sync", P.dtok("xn1st"))
        P.finalize(blk)
    return nc


def prep_C_inputs(inp):
    Wo = np.asarray(inp["L0_attn_Wo"])
    lng = np.asarray(inp["L0_attn_ln_g"])
    lnb = np.asarray(inp["L0_attn_ln_b"])
    WoL = np.ascontiguousarray(
        Wo.reshape(NM, 128, NM, 128).transpose(3, 1, 2, 0).reshape(NM, 128, NM * 128))
    WoL2 = np.zeros((NM, 128, NM * 128), np.float32)
    WoT = Wo.T
    for hdt in range(NM):
        for mt in range(NM):
            WoL2[hdt, :, mt * 128:(mt + 1) * 128] = WoT[hdt * 128:(hdt + 1) * 128,
                                                        mt * 128:(mt + 1) * 128]
    lngb = np.zeros((128, 2 * NM), np.float32)
    for m in range(NM):
        lngb[:, m] = lng[m * 128:(m + 1) * 128]
        lngb[:, NM + m] = lnb[m * 128:(m + 1) * 128]
    ident = np.eye(128, dtype=np.float32)
    base = {"WoL": WoL2, "lngb": lngb, "ident_in": ident}
    return base



NH1 = 8


def emit_l1_pair(P, hp, xn, y_dram, bufs, psF, psB, cons):
    wq, wk, wv = bufs["wslots"][hp % 2]
    QT, KT, VT, Vn = bufs["QT"], bufs["KT"], bufs["VT"], bufs["Vn"]
    pc = bufs["pc"]
    vh = bufs["vh"]
    A_sb = bufs["A_sb"]
    of32 = bufs["of32"]
    zf = bufs["zf"]
    y1t = bufs["y1t"]
    zr_sb, tri = cons["zr_sb"], cons["tri_sb"]
    ident_bf16 = cons["ident_bf16"]
    ones_r = cons["ones_r"]

    ps_scL = [psF[0][:, :], psF[1][:, :], psF[3][:, :], psF[4][:, :]]
    ps_oz = psF[2][0:65, :]
    ps_o = psF[2][0:64, :]
    ps_z = psF[2][64:65, :]
    ps_rb = psF[5][0:64, :]
    ps_tr = psB[1][:, :]
    nA = len(A_sb)
    nB = len(ps_scL)
    Vn3 = Vn.rearrange("p (k c) -> p k c", c=65)
    ps_tr3 = ps_tr.rearrange("p (k c) -> p k c", c=64)

    hd = bufs.get("head_dep")
    tok_w = bufs.get("wpre_tok")
    if tok_w is None:
        P.dma("wqkv", "gp", wq[:, :], cons["Wq1L_ap"][hp], deps=[hd])
        P.dma("wqkv", "gp", wk[:, :], cons["Wk1L_ap"][hp])
        tok_w = P.dma("wqkv", "gp", wv[:, :], cons["Wv1L_ap"][hp])
    t_ld = [tok_w]
    evs = []
    fi = 0
    for c in range(NCHK):
        for which, (wsb, dst) in enumerate([(wq, QT), (wk, KT), (wv, VT)]):
            b2 = fi % 4
            P.waits("pe", [t_ld[-1] if fi == 0 else None,
                           evs[-4] if len(evs) >= 4 else None])
            for m in range(NM):
                P.step("pe", lambda e, wsb=wsb, m=m, c=c, b2=b2: e.matmul(
                    ps_scL[b2], wsb[:, m * 128:(m + 1) * 128],
                    xn[m][:, c * CHK:(c + 1) * CHK],
                    start=(m == 0), stop=(m == NM - 1)),
                    inc=(1 if m == NM - 1 else 0))
            t_mm = P.tok("pe")
            P.wait("act", t_mm)
            t_ev = P.step("act", lambda e, dst=dst, c=c, b2=b2: e.copy(
                out=dst[:, c * CHK:(c + 1) * CHK], in_=ps_scL[b2]), inc=1)
            evs.append(t_ev)
            fi += 1
    t_ev = evs[-1]
    P.waits("dve", [t_ev])
    t_vn_ms = P.step("dve", lambda e: e.memset(Vn[:, :], 1.0), inc=1)
    P.step("dve", lambda e: e.tensor_scalar_mul(out=vh[:, :], in0=VT[:, :],
                                                scalar1=0.5))
    t_pc = P.step("dve", lambda e: e.tensor_tensor_scan(
        out=pc[:, :], data0=vh[:, :], data1=vh[:, :], initial=0.0,
        op0=ALU.add, op1=ALU.add), inc=1)
    if hp < 3:
        nxt = bufs["wslots"][(hp + 1) % 2]
        P.dma("wqkv", "gp", nxt[0][:, :], cons["Wq1L_ap"][hp + 1], deps=[t_pc])
        P.dma("wqkv", "gp", nxt[1][:, :], cons["Wk1L_ap"][hp + 1])
        bufs["wpre_tok"] = P.dma("wqkv", "gp", nxt[2][:, :],
                                 cons["Wv1L_ap"][hp + 1])
    else:
        bufs["wpre_tok"] = None

    t_A_cons = [None] * nA
    t_sc_free = [None] * nB
    t_v_ev = bufs.get("ps_tr_dep")
    t_vn = [None, None]
    epi = [None]
    t_yst = [bufs.get("yst_dep")]

    def flush_epi():
        t_z0, hl0, c0 = epi[0]
        epi[0] = None
        P.waits("pe", [t_z0])
        t_rb = P.step("pe", lambda e: e.matmul(
            ps_rb, ones_r[0:1, 0:64], zf[:, :], start=True, stop=True), inc=1)
        P.waits("dve", [t_rb, t_yst[0]])
        t_y = P.step("dve", lambda e: e.tensor_tensor(
            out=y1t[:, :], in0=of32[:, :], in1=ps_rb, op=ALU.mult), inc=1)
        P.dma("yst", "sync",
              y_dram[hp * 2 + hl0, :, c0 * CHK:(c0 + 1) * CHK],
              y1t[:, :], deps=[t_y])
        t_yst[0] = P.dtok("yst")

    for hl in range(2):
        hb = hl * 64
        if epi[0] is not None:
            flush_epi()
        for grp in range(2):
            P.waits("pe", [t_ev, t_v_ev])
            for j in range(16):
                k = grp * 16 + j
                P.step("pe", lambda e, k=k, j=j, hb=hb: e.transpose(
                    out=ps_tr[:, j * 64:(j + 1) * 64],
                    in_=VT[hb:hb + 64, k * 128:(k + 1) * 128],
                    identity=ident_bf16[hb:hb + 64, hb:hb + 64]),
                    inc=(1 if j == 15 else 0))
            t_tr = P.tok("pe")
            P.waits("dve", [t_tr, t_vn_ms])
            t_v_ev = P.step("dve", lambda e, grp=grp: e.tensor_copy(
                out=Vn3[:, grp * 16:(grp + 1) * 16, 0:64],
                in_=ps_tr3[:, :, :]), inc=1)
        t_vn[hl] = t_v_ev
        t_z_prev = [None]
        for c in range(NCHK):
            nk = 4 * c + 4
            cs = slice(c * CHK, (c + 1) * CHK)
            pend = []
            first_av = [True]

            def flush_av(hl=hl):
                k2, slot2, tA2 = pend.pop(0)
                first = first_av[0]
                last = (len(pend) == 0)
                first_av[0] = False
                P.waits("pe", [tA2, t_z_prev[0] if first else None,
                               t_vn[hl] if first else None, t_pc if first else None])
                t_av = P.step("pe", lambda e, k2=k2, first=first, last=last:
                              e.matmul(ps_oz, Vn[:, k2 * 65:(k2 + 1) * 65],
                                       A_sb[slot2], start=first, stop=last),
                              inc=1)
                t_A_cons[slot2] = t_av
                return t_av

            for idx, k in enumerate(range(nk - 1, -1, -1)):
                b = idx % nB
                slot = idx % nA
                P.waits("pe", [t_sc_free[b]])
                t_s = P.step("pe", lambda e, k=k, cs=cs, hb=hb, b=b: e.matmul(
                    ps_scL[b], KT[hb:hb + 64, k * 128:(k + 1) * 128],
                    QT[hb:hb + 64, cs], start=True, stop=True), inc=1)
                if idx == 2 and epi[0] is not None:
                    flush_epi()
                P.waits("act", [t_s, t_A_cons[slot]])
                t_sq = P.step("act", lambda e, b=b, slot=slot: e.activation(
                    out=A_sb[slot], in_=ps_scL[b], func=AF.Square,
                    scale=0.125, bias=1.0), inc=1)
                t_sc_free[b] = t_sq
                t_A = t_sq
                for sub in range(4):
                    d = 4 * c + sub - k
                    if d == 0:
                        P.waits("dve", [t_sq])
                        t_A = P.step("dve", lambda e, slot=slot, sub=sub:
                                     e.tensor_tensor(
                            out=A_sb[slot][:, sub * 128:(sub + 1) * 128],
                            in0=A_sb[slot][:, sub * 128:(sub + 1) * 128],
                            in1=tri[:, :], op=ALU.mult), inc=1)
                    elif d < 0:
                        P.waits("dve", [t_sq])
                        t_A = P.step("dve", lambda e, slot=slot, sub=sub: e.memset(
                            A_sb[slot][:, sub * 128:(sub + 1) * 128], 0.0), inc=1)
                pend.append((k, slot, t_A))
                if idx >= 3:
                    flush_av()
            while pend:
                flush_av()
            t_pe_done = P.tok("pe")
            P.waits("dve", [t_pe_done])
            P.step("dve", lambda e, hb=hb, cs=cs: e.tensor_tensor(
                out=of32[:, :], in0=ps_o, in1=pc[hb:hb + 64, cs], op=ALU.add))
            P.step("dve", lambda e, cs=cs: e.tensor_tensor(
                out=zf[:, :], in0=ps_z, in1=zr_sb[0:1, cs], op=ALU.add))
            t_z = P.step("dve", lambda e: e.reciprocal(out=zf[:, :], in_=zf[:, :]),
                         inc=1)
            t_z_prev[0] = t_z
            epi[0] = (t_z, hl, c)
    flush_epi()
    bufs["ps_tr_dep"] = t_v_ev
    bufs["yst_dep"] = t_yst[0]
    return P.tok("pe")


def build_launch_D():
    nc = bass.Bass("TRN2", target_bir_lowering=False, debug=False)
    xnin = nc.dram_tensor("xnin", [D, T], BF16, kind="ExternalInput").ap()
    Wq1L = nc.dram_tensor("Wq1L", [4, 128, 1024], F32, kind="ExternalInput").ap()
    Wk1L = nc.dram_tensor("Wk1L", [4, 128, 1024], F32, kind="ExternalInput").ap()
    Wv1L = nc.dram_tensor("Wv1L", [4, 128, 1024], F32, kind="ExternalInput").ap()
    zr_in = nc.dram_tensor("zr_in", [1, T], F32, kind="ExternalInput").ap()
    tri_in = nc.dram_tensor("tri_in", [128, 128], F32, kind="ExternalInput").ap()
    ident_in = nc.dram_tensor("ident_in", [128, 128], F32, kind="ExternalInput").ap()
    y_dram = nc.dram_tensor("y_dram", [NH1, 64, T], BF16, kind="ExternalOutput").ap()

    ctx = ExitStack()
    with ctx:
        E = ctx.enter_context
        P = Prog(nc, ctx)
        poolA = E(nc.sbuf_tensor("poolA", [128, 8192], F32))
        poolB = E(nc.sbuf_tensor("poolB", [128, 32768], BF16))
        poolC = E(nc.sbuf_tensor("poolC", [128, 16384], BF16))
        poolW = E(nc.sbuf_tensor("poolW", [128, 10240], BF16))
        ones_r = E(nc.sbuf_tensor("ones_r", [1, 128], F32))
        ones_cb = E(nc.sbuf_tensor("ones_cb", [128, 1], BF16))
        ident_bf16 = E(nc.sbuf_tensor("ident_bf16", [128, 128], BF16))
        tri_sb = E(nc.sbuf_tensor("tri_sb", [128, 128], BF16))
        zr_sb = E(nc.sbuf_tensor("zr_sb", [1, T], F32))
        psF = [E(nc.psum_tensor(f"psF{i}", [128, 512], F32)) for i in range(6)]
        psB = [E(nc.psum_tensor(f"psB{i}", [128, 1024], BF16)) for i in range(2)]
        blk = E(nc.Block())

        cons = dict(ones_r=ones_r, ones_cb=ones_cb, ident_bf16=ident_bf16,
                    tri_sb=tri_sb, zr_sb=zr_sb,
                    Wq1L_ap=[Wq1L[i, :, :] for i in range(4)],
                    Wk1L_ap=[Wk1L[i, :, :] for i in range(4)],
                    Wv1L_ap=[Wv1L[i, :, :] for i in range(4)])

        P.step("dve", lambda e: e.memset(ones_cb[:, :], 1.0))
        t_init = P.step("dve", lambda e: e.memset(ones_r[:, :], 1.0), inc=1)
        P.wait("pe", t_init)
        tc = [
            P.dma("cns", "sync", zr_sb[:, :], zr_in),
            P.dma("cnsg", "gp", ident_bf16[:, :], ident_in),
            P.dma("cnsg", "gp", tri_sb[:, :], tri_in),
        ]
        xn = [poolB[:, m * T:(m + 1) * T] for m in range(NM)]
        t_xn = [P.dma("xnf", "sync", xn[m], xnin[m * 128:(m + 1) * 128, :])
                for m in range(NM)]
        for e in ("pe", "act", "dve", "gp"):
            P.waits(e, [t_xn[-1], tc[0], P.dtok("cnsg")])

        bufs = dict(
            QT=poolC[:, 0:4096], KT=poolC[:, 4096:8192], VT=poolC[:, 8192:12288],
            Vn=poolC[:, 12288:14368],
            pc=poolA[:, 0:4096],
            of32=poolA[0:64, 4096:4608],
            zf=poolA[0:1, 4608:5120],
            vh=poolW[:, 0:4096],
            A_sb=[poolW[:, 4096:4608], poolW[:, 4608:5120],
                  poolW[:, 8704:9216], poolW[:, 9216:9728]],
            y1t=poolW[0:64, 5120:5632],
            wq=poolW[:, 5632:6656],
            wk=poolW[:, 6656:7680],
            wv=poolW[:, 7680:8704],
            wslots=[[poolW[:, 5632:6656], poolW[:, 6656:7680],
                     poolW[:, 7680:8704]],
                    [poolW[:, 0:1024], poolW[:, 1024:2048],
                     poolW[:, 2048:3072]]],
        )
        for hp in range(4):
            tk = emit_l1_pair(P, hp, xn, y_dram, bufs, psF, psB, cons)
            bufs["head_dep"] = tk
        P.wait("sync", P.dtok("yst"))
        P.finalize(blk)
    return nc


def prep_D_inputs(inp):
    Wq = np.asarray(inp["L1_attn_Wq"])
    Wk = np.asarray(inp["L1_attn_Wk"])
    Wv = np.asarray(inp["L1_attn_Wv"])

    def wslice1(W, hbase):
        Ws = W.reshape(16, 64, NM, 128)[hbase:hbase + 8]
        out = np.zeros((4, 128, NM * 128), np.float32)
        for hp in range(4):
            pair = Ws[hp * 2:hp * 2 + 2].reshape(128, NM, 128)
            out[hp] = pair.transpose(2, 1, 0).reshape(128, NM * 128)
        return np.ascontiguousarray(out)

    pp = np.arange(128)[:, None]
    ff = np.arange(128)[None, :]
    tri = (ff >= pp).astype(np.float32)
    zr = (np.arange(T) + 1 + 2e-6).astype(np.float32)[None, :]
    ident = np.eye(128, dtype=np.float32)

    in_maps = []
    for c in range(8):
        par = c % 2
        hbase = par * 8
        in_maps.append({
            "Wq1L": wslice1(Wq, hbase), "Wk1L": wslice1(Wk, hbase),
            "Wv1L": wslice1(Wv, hbase),
            "zr_in": zr, "tri_in": tri, "ident_in": ident,
        })
    return in_maps



NI = 2816
NIT = NI // 128


def emit_quant(P, src_ap, dst_ap, W, scr, deps=(), eng="dve"):
    NB = W // 16
    y, f1, f2 = scr["y"][:, 0:W], scr["f1"][:, 0:W], scr["f2"][:, 0:W]
    bm, t1s, rr = scr["bm"][:, 0:NB], scr["t1"][:, 0:NB], scr["r"][:, 0:NB]
    P.waits("dve", list(deps) + [scr.get("_t_g")])
    P.step("dve", lambda e: e.tensor_reduce(
        out=bm.rearrange("p (b o) -> p b o", o=1),
        in_=src_ap.rearrange("p (b i) -> p b i", i=16),
        axis=AX.X, op=ALU.max, apply_absolute_value=True))
    P.step("dve", lambda e: e.tensor_scalar_max(out=t1s, in0=bm, scalar1=6e-12))
    P.step("dve", lambda e: e.reciprocal(out=rr, in_=t1s))
    P.step("dve", lambda e: e.tensor_scalar_mul(out=rr, in0=rr, scalar1=6.0))

    def bcast_mul(e, out, in0, in1_small):
        a3 = in0.rearrange("p (b i) -> p b i", i=16)
        b3 = in1_small.rearrange("p (b o) -> p b o", o=1)
        a3b, b3b = bass.broadcast_tensor_aps(a3, b3)
        return e.tensor_tensor(out=out.rearrange("p (b i) -> p b i", i=16),
                               in0=a3b, in1=b3b, op=ALU.mult)
    t_y = P.step("dve", lambda e: bcast_mul(e, y, src_ap, rr), inc=1)
    P.waits(eng, [t_y, scr.get("_t_dst")])
    IW = scr["ints"].shape[1] if hasattr(scr["ints"], "shape") else 512
    IW = int(IW)
    for o in range(0, W, IW):
        w2 = min(IW, W - o)
        P.step(eng, lambda e, o=o, w2=w2: e.tensor_scalar(
            out=scr["ints"][:, 0:w2], in0=y[:, o:o + w2], scalar1=2.0,
            scalar2=4.0, op0=ALU.mult, op1=ALU.min))
        P.step(eng, lambda e, o=o, w2=w2: e.tensor_scalar(
            out=f1[:, o:o + w2], in0=scr["ints"][:, 0:w2], scalar1=0.5,
            scalar2=-2.0, op0=ALU.mult, op1=ALU.max))
        P.step(eng, lambda e, o=o, w2=w2: e.tensor_scalar(
            out=scr["ints"][:, 0:w2], in0=y[:, o:o + w2], scalar1=4.0,
            scalar2=None, op0=ALU.min))
        P.step(eng, lambda e, o=o, w2=w2: e.tensor_scalar(
            out=f2[:, o:o + w2], in0=scr["ints"][:, 0:w2], scalar1=-4.0,
            scalar2=None, op0=ALU.max))
    P.step(eng, lambda e: e.tensor_add(out=f1, in0=f1, in1=f2))
    P.step(eng, lambda e: e.tensor_scalar(out=f2, in0=f2, scalar1=2.0,
                                          scalar2=-2.0, op0=ALU.min, op1=ALU.max))
    P.step(eng, lambda e: e.tensor_sub(out=f1, in0=f1, in1=f2))
    P.step(eng, lambda e: e.tensor_scalar(out=f2, in0=y, scalar1=5.0,
                                          scalar2=2.0, op0=ALU.is_gt, op1=ALU.mult))
    P.step(eng, lambda e: e.tensor_add(out=f1, in0=f1, in1=f2))
    P.step(eng, lambda e: e.tensor_scalar(out=f2, in0=y, scalar1=-5.0,
                                          scalar2=2.0, op0=ALU.is_lt, op1=ALU.mult))
    t_g = P.step(eng, lambda e: e.tensor_sub(out=f1, in0=f1, in1=f2), inc=1)
    scr["_t_g"] = t_g
    P.waits("dve", [t_g])
    P.step("dve", lambda e: e.tensor_scalar_mul(out=t1s, in0=t1s, scalar1=1.0 / 6.0))
    t = P.step("dve", lambda e: bcast_mul(e, dst_ap, f1, t1s), inc=1)
    scr["_t_dst"] = t
    return t


def emit_wtrans_rows(P, rows, S):
    for src_row, dst_row, wt in rows:
        nblk = wt // 128
        for base in range(0, nblk, 8):
            nf = min(8, nblk - base)
            t_ld = P.dma("wtld", "sync", S["buf"][:, 0:nf * 128],
                         src_row[:, base * 128:(base + nf) * 128],
                         deps=[S["t_tr"], S.get("extra_dep")])
            P.waits("pe", [t_ld, S["t_evac"]])
            for j in range(nf):
                P.step("pe", lambda e, j=j: e.transpose(
                    out=S["psB0"][:, j * 128:(j + 1) * 128],
                    in_=S["buf"][:, j * 128:(j + 1) * 128],
                    identity=S["ident"][:, :]), inc=(1 if j == nf - 1 else 0))
            t_tr = P.tok("pe")
            S["t_tr"] = t_tr
            P.waits("act", [t_tr, S["t_store"]])
            t_ev = P.step("act", lambda e, nf=nf: e.copy(
                out=S["evac"][:, 0:nf * 128], in_=S["psB0"][:, 0:nf * 128]),
                inc=1)
            S["t_evac"] = t_ev
            P.dma("wqst", "sync", dst_row[:, base * 128:(base + nf) * 128],
                  S["evac"][:, 0:nf * 128], deps=[t_ev])
            S["t_store"] = P.dtok("wqst")


def build_launch_E():
    nc = bass.Bass("TRN2", target_bir_lowering=False, debug=False)
    y1_in = nc.dram_tensor("y1_in", [NM, 128, T], BF16, kind="ExternalInput").ap()
    xn1_in = nc.dram_tensor("xn1_in", [D, TO], BF16, kind="ExternalInput").ap()
    x1_in = nc.dram_tensor("x1_in", [D, TO], F32, kind="ExternalInput").ap()
    Wo1L = nc.dram_tensor("Wo1L", [NM, 128, NM * 128], F32, kind="ExternalInput").ap()
    wg_in = nc.dram_tensor("wg_in", [NI, D], F32, kind="ExternalInput").ap()
    wu_in = nc.dram_tensor("wu_in", [NI, D], F32, kind="ExternalInput").ap()
    wd_in = nc.dram_tensor("wd_in", [D, NI], F32, kind="ExternalInput").ap()
    ident_in = nc.dram_tensor("ident_in", [128, 128], F32, kind="ExternalInput").ap()
    par_in = nc.dram_tensor("par_in", [1, 1], I32, kind="ExternalInput").ap()

    out_T = nc.dram_tensor("out_T", [D, TO], F32, kind="ExternalOutput").ap()
    import os
    FDBG = os.environ.get("FDBG", "")
    dbg = nc.dram_tensor("dbg", [D, T], F32, kind="ExternalOutput").ap() if FDBG else None

    wgq_d = nc.dram_tensor("wgq_d", [NIT, 128, D], BF16).ap()
    wuq_d = nc.dram_tensor("wuq_d", [NIT, 128, D], BF16).ap()
    wdq_d = nc.dram_tensor("wdq_d", [NM, 128, NI], BF16).ap()
    s1_d = nc.dram_tensor("s1_d", [D, TO], BF16).ap()

    ctx = ExitStack()
    with ctx:
        E = ctx.enter_context
        P = Prog(nc, ctx)
        poolA = E(nc.sbuf_tensor("poolA", [128, 16384], F32))
        poolB = E(nc.sbuf_tensor("poolB", [128, 32768], BF16))
        poolC = E(nc.sbuf_tensor("poolC", [128, 16384], BF16))
        poolW = E(nc.sbuf_tensor("poolW", [128, 10240], BF16))
        ints = E(nc.sbuf_tensor("ints", [128, NI], I32))
        ident_bf16 = E(nc.sbuf_tensor("ident_bf16", [128, 128], BF16))
        par_sb = E(nc.sbuf_tensor("par_sb", [1, 1], I32))
        psF = [E(nc.psum_tensor(f"psF{i}", [128, 512], F32)) for i in range(6)]
        psB = [E(nc.psum_tensor(f"psB{i}", [128, 1024], BF16)) for i in range(2)]
        blk = E(nc.Block())

        emit_load_parity(P, par_in, par_sb, engines=("gp",))
        t_id = P.dma("cnsg", "gp", ident_bf16[:, :], ident_in)
        for e in ("pe", "act", "dve"):
            P.wait(e, t_id)

        scr = dict(y=poolA[:, 0:2816], f1=poolA[:, 2816:5632],
                   f2=poolA[:, 5632:8448], ints=ints[:, :],
                   bm=poolA[0:128, 8448:8624], t1=poolA[:, 8624:8800],
                   r=poolA[:, 8800:8976])
        raw = poolA[:, 9216:12032]
        qnat = poolW[:, 0:2816]

        t_q = None
        t_tr_ev = None
        for wi, (w_ap, nrow, wq_d, wt) in enumerate([
                (wg_in, NIT, wgq_d, 1024), (wu_in, NIT, wuq_d, 1024),
                (wd_in, NM, wdq_d, 2816)]):
            for rt in range(nrow):
                t_ld = P.dma("wraw", "sync", raw[:, 0:wt],
                             w_ap[rt * 128:(rt + 1) * 128, :], deps=[t_q])
                t_q = emit_quant(P, raw[:, 0:wt], qnat[:, 0:wt], wt, scr,
                                 deps=[t_ld, t_tr_ev])
                for bb in range(wt // 128):
                    P.waits("pe", [t_q, t_tr_ev])
                    t_tr = P.step("pe", lambda e, bb=bb: e.transpose(
                        out=psB[0][:, 0:128], in_=qnat[:, bb * 128:(bb + 1) * 128],
                        identity=ident_bf16[:, :]), inc=1)
                    P.wait("act", t_tr)
                    t_tr_ev = P.step("act", lambda e: e.copy(
                        out=poolW[:, 4096:4224], in_=psB[0][:, 0:128]), inc=1)
                    P.dma("wqst", "sync", wq_d[rt, :, bb * 128:(bb + 1) * 128],
                          poolW[:, 4096:4224], deps=[t_tr_ev])
                    t_tr_ev = P.dtok("wqst")

        y1T = [poolB[:, ht * TO:(ht + 1) * TO] for ht in range(NM)]

        def go_y(e, sem):
            par = P.holder["par"]
            with e.If_eq(par, 0):
                for ht in range(NM):
                    e.dma_start(out=y1T[ht], in_=y1_in[ht, :, 0:TO]).then_inc(sem, 16)
            with e.Else():
                for ht in range(NM):
                    e.dma_start(out=y1T[ht], in_=y1_in[ht, :, TO:T]).then_inc(sem, 16)
        t_y = P.raw("yld", "gp", go_y, inc=16 * NM)
        woT = [poolB[:, 16384 + ht * 1024: 16384 + (ht + 1) * 1024]
               for ht in range(NM)]
        t_wo = [P.dma("wold", "gp", woT[ht], Wo1L[ht, :, :]) for ht in range(NM)]
        s1t = poolW[:, 4224:4736]
        t_s1 = None
        for c in range(TO // CHK):
            cs = slice(c * CHK, (c + 1) * CHK)
            for mt in range(NM):
                P.waits("pe", [t_y, t_wo[-1], t_s1])
                for ht in range(NM):
                    P.step("pe", lambda e, ht=ht, mt=mt, cs=cs: e.matmul(
                        psF[0][:, :], woT[ht][:, mt * 128:(mt + 1) * 128],
                        y1T[ht][:, cs], start=(ht == 0), stop=(ht == NM - 1)),
                        inc=(1 if ht == NM - 1 else 0))
                t_mm = P.tok("pe")
                P.waits("act", [t_mm])
                t_s1 = P.step("act", lambda e: e.copy(out=s1t, in_=psF[0][:, :]),
                              inc=1)
                P.dma("s1st", "sync", s1_d[mt * 128:(mt + 1) * 128, cs], s1t,
                      deps=[t_s1])
                t_s1 = P.dtok("s1st")
        P.barrier_all()

        xq = [poolC[:, m * TO:(m + 1) * TO] for m in range(NM)]
        t_x = [P.dma("xnld", "sync", xq[m], xn1_in[m * 128:(m + 1) * 128, :])
               for m in range(NM)]
        xnat = poolW[:, 4736:5760]
        t_nat_ev = None
        t_q2 = None
        for tt in range(TO // 128):
            for m in range(NM):
                P.waits("pe", [t_x[-1], t_nat_ev, t_q2])
                t_tr = P.step("pe", lambda e, m=m, tt=tt: e.transpose(
                    out=psB[0][:, 0:128], in_=xq[m][:, tt * 128:(tt + 1) * 128],
                    identity=ident_bf16[:, :]), inc=1)
                P.wait("act", t_tr)
                t_nat_ev = P.step("act", lambda e, m=m: e.copy(
                    out=xnat[:, m * 128:(m + 1) * 128], in_=psB[0][:, 0:128]), inc=1)
            t_q2 = emit_quant(P, xnat, qnat[:, 0:1024], 1024, scr, deps=[t_nat_ev])
            for m in range(NM):
                P.waits("pe", [t_q2])
                t_tr = P.step("pe", lambda e, m=m: e.transpose(
                    out=psB[0][:, 0:128], in_=qnat[:, m * 128:(m + 1) * 128],
                    identity=ident_bf16[:, :]), inc=1)
                P.wait("act", t_tr)
                t_q2 = P.step("act", lambda e, m=m, tt=tt: e.copy(
                    out=xq[m][:, tt * 128:(tt + 1) * 128], in_=psB[0][:, 0:128]),
                    inc=1)
        P.barrier_all()

        hT = [poolB[:, it * CHK:(it + 1) * CHK] for it in range(NIT)]
        wgb = [poolW[:, 5760 + p * 1024: 5760 + (p + 1) * 1024] for p in range(2)]
        wub = [poolW[:, 7808 + p * 1024: 7808 + (p + 1) * 1024] for p in range(2)]
        wdbuf = [poolB[:, 16384 + p * NI: 16384 + (p + 1) * NI] for p in range(2)]
        sg = poolA[:, 12032 + 0:12032 + 512]
        hnat = poolB[:, 22528:22528 + 2816]
        x1c = poolA[:, 12544:13056]
        g1c = poolA[:, 13056:13568]
        s1ct = poolW[:, 4224:4736]
        t_wg_cons = [None, None]
        t_wu_cons = [None, None]
        t_wd_cons = [None, None]
        t_hev = {}
        t_ps_free = [None, None]
        t_dn_free = [None, None]
        t_out_last = None
        t_out_prev_dve = None
        for c in range(TO // CHK):
            cs = slice(c * CHK, (c + 1) * CHK)
            for it in range(NIT):
                pb = it % 2
                t_lg = P.dma(f"wg{pb}", "gp", wgb[pb], wgq_d[it, :, :],
                             deps=[t_wg_cons[pb]])
                t_lu = P.dma(f"wu{pb}", "gp", wub[pb], wuq_d[it, :, :],
                             deps=[t_wu_cons[pb]])
                P.waits("pe", [t_lg, t_ps_free[0]])
                for m in range(NM):
                    P.step("pe", lambda e, m=m, it=it, pb=pb, cs=cs: e.matmul(
                        psF[0][:, :], wgb[pb][:, m * 128:(m + 1) * 128],
                        xq[m][:, cs], start=(m == 0), stop=(m == NM - 1)),
                        inc=(1 if m == NM - 1 else 0))
                t_g = P.tok("pe")
                t_wg_cons[pb] = t_g
                P.waits("pe", [t_lu, t_ps_free[1]])
                for m in range(NM):
                    P.step("pe", lambda e, m=m, it=it, pb=pb, cs=cs: e.matmul(
                        psF[1][:, :], wub[pb][:, m * 128:(m + 1) * 128],
                        xq[m][:, cs], start=(m == 0), stop=(m == NM - 1)),
                        inc=(1 if m == NM - 1 else 0))
                t_u = P.tok("pe")
                t_wu_cons[pb] = t_u
                P.wait("act", t_g)
                t_sg = P.step("act", lambda e: e.activation(
                    out=sg, in_=psF[0][:, :], func=AF.Silu), inc=1)
                t_ps_free[0] = t_sg
                P.waits("dve", [t_sg, t_u])
                t_h = P.step("dve", lambda e, it=it: e.tensor_tensor(
                    out=hT[it], in0=sg, in1=psF[1][:, :], op=ALU.mult), inc=1)
                t_ps_free[1] = t_h
                t_hev[it] = t_h
            t_qh = None
            t_natv = None
            for tt in range(4):
                for it in range(NIT):
                    P.waits("pe", [t_hev[it], t_natv, t_qh])
                    t_tr = P.step("pe", lambda e, it=it, tt=tt: e.transpose(
                        out=psB[0][:, 0:128], in_=hT[it][:, tt * 128:(tt + 1) * 128],
                        identity=ident_bf16[:, :]), inc=1)
                    P.wait("act", t_tr)
                    t_natv = P.step("act", lambda e, it=it: e.copy(
                        out=hnat[:, it * 128:(it + 1) * 128], in_=psB[0][:, 0:128]),
                        inc=1)
                t_qh = emit_quant(P, hnat, qnat[:, 0:2816], 2816, scr,
                                  deps=[t_natv])
                for it in range(NIT):
                    P.waits("pe", [t_qh])
                    t_tr = P.step("pe", lambda e, it=it: e.transpose(
                        out=psB[0][:, 0:128], in_=qnat[:, it * 128:(it + 1) * 128],
                        identity=ident_bf16[:, :]), inc=1)
                    P.wait("act", t_tr)
                    t_qh = P.step("act", lambda e, it=it, tt=tt: e.copy(
                        out=hT[it][:, tt * 128:(tt + 1) * 128],
                        in_=psB[0][:, 0:128]), inc=1)
            t_hq_all = t_qh
            for mt in range(NM):
                pb = mt % 2
                t_ldw = P.dma(f"wd{pb}", "gp", wdbuf[pb], wdq_d[mt, :, :],
                              deps=[t_wd_cons[pb]])
                P.waits("pe", [t_ldw, t_hq_all, t_dn_free[0]])
                for it in range(NIT):
                    P.step("pe", lambda e, it=it, pb=pb: e.matmul(
                        psF[2][:, :], wdbuf[pb][:, it * 128:(it + 1) * 128],
                        hT[it], start=(it == 0), stop=(it == NIT - 1)),
                        inc=(1 if it == NIT - 1 else 0))
                t_dn = P.tok("pe")
                t_wd_cons[pb] = t_dn
                P.waits("act", [t_dn, t_out_prev_dve])
                t_g1 = P.step("act", lambda e: e.activation(
                    out=g1c, in_=psF[2][:, :], func=AF.Silu), inc=1)
                t_dn_free[0] = t_g1

                def go_x1(e, sem, mt=mt, cs=cs):
                    return e.dma_start(out=x1c,
                                       in_=x1_in[mt * 128:(mt + 1) * 128, cs]
                                       ).then_inc(sem, 16)
                P.waits("sync", [t_out_last])
                t_x1 = P.raw("x1ld", "sync", go_x1, inc=16)
                t_s1r = P.dma("s1rd", "sync", s1ct,
                              s1_d[mt * 128:(mt + 1) * 128, cs],
                              deps=[t_out_last])
                P.waits("dve", [t_g1, t_x1, t_s1r])
                P.step("dve", lambda e: e.tensor_tensor(
                    out=g1c, in0=g1c, in1=s1ct, op=ALU.mult))
                t_x2 = P.step("dve", lambda e: e.tensor_tensor(
                    out=x1c, in0=x1c, in1=g1c, op=ALU.add), inc=1)
                t_out_prev_dve = t_x2
                t_out_last = P.dma("outst", "sync",
                                   out_T[mt * 128:(mt + 1) * 128, cs], x1c,
                                   deps=[t_x2])
        P.wait("sync", P.dtok("outst"))
        P.finalize(blk)
    return nc


def prep_E_inputs(inp):
    Wo = np.asarray(inp["L1_attn_Wo"])
    WoL = np.zeros((NM, 128, NM * 128), np.float32)
    WoT = Wo.T
    for hdt in range(NM):
        for mt in range(NM):
            WoL[hdt, :, mt * 128:(mt + 1) * 128] = WoT[hdt * 128:(hdt + 1) * 128,
                                                       mt * 128:(mt + 1) * 128]
    ident = np.eye(128, dtype=np.float32)
    base = {
        "Wo1L": WoL,
        "wg_in": np.asarray(inp["L1_ffn_w_gate"]),
        "wu_in": np.asarray(inp["L1_ffn_w_up"]),
        "wd_in": np.asarray(inp["L1_ffn_w_down"]),
        "ident_in": ident,
    }
    return base



POLL_LIM = 400000


def emit_exchange(P, nc, ex, go_write, n_write, sh_flag, nonce_sb, flag_rd):
    t_w = P.raw(f"exw{ex}", "gp", go_write, inc=16 * n_write)
    P.waits("gp", [t_w])

    def go_flag(e, sem, ex=ex):
        par = P.holder["par"]
        with e.If_eq(par, 0):
            e.dma_start(out=sh_flag[ex:ex + 1, 0:1], in_=nonce_sb[0:1, 0:1]
                        ).then_inc(sem, 16)
        with e.Else():
            e.dma_start(out=sh_flag[ex:ex + 1, 1:2], in_=nonce_sb[0:1, 0:1]
                        ).then_inc(sem, 16)
    P.raw(f"exf{ex}", "gp", go_flag, inc=16)
    psem = P.holder[f"pollsem{ex}"]

    def poll(e, ex=ex, psem=psem):
        cnt = P.holder["r_cnt"]
        go = P.holder["r_go"]
        f0 = P.holder["r_f0"]
        f1 = P.holder["r_f1"]
        nr = P.holder["r_nonce"]
        e.reg_mov(cnt, 0)
        e.reg_mov(go, 1)
        with e.While(go):
            e.reg_add(cnt, cnt, 16)
            e.dma_start(out=flag_rd[0:1, 0:2], in_=sh_flag[ex:ex + 1, 0:2]
                        ).then_inc(psem, 16)
            e.wait_ge(psem, cnt)
            v0 = e.value_load(flag_rd[0:1, 0:1])
            e.reg_mov(f0, v0)
            v1 = e.value_load(flag_rd[0:1, 1:2])
            e.reg_mov(f1, v1)
            e.reg_alu(f0, f0, nr, ALU.is_equal)
            e.reg_alu(f1, f1, nr, ALU.is_equal)
            e.reg_alu(f0, f0, f1, ALU.bitwise_and)
            e.reg_alu(go, f0, 1, ALU.is_lt)
            e.reg_alu(f1, cnt, 16 * POLL_LIM, ALU.is_lt)
            e.reg_alu(go, go, f1, ALU.bitwise_and)
        return e.engine_nop()
    t_done = P.step("gp", poll, inc=1)
    return t_done


def build_fused():
    nc = bass.Bass("TRN2", target_bir_lowering=False, debug=False)
    x0T = nc.dram_tensor("x0T", [D, T], F32, kind="ExternalInput").ap()
    Wr = nc.dram_tensor("Wr", [128, NM * NE], F32, kind="ExternalInput").ap()
    br_bc = nc.dram_tensor("br_bc", [128, NE], F32, kind="ExternalInput").ap()
    W1L = nc.dram_tensor("W1L", [NE, NF, 128, 1024], F32, kind="ExternalInput").ap()
    W2L = nc.dram_tensor("W2L", [NE, NM, 128, 4096], F32, kind="ExternalInput").ap()
    W2c_d = nc.dram_tensor("W2c_d", [NE, NM, 128, 4096], BF16).ap()
    b1r = nc.dram_tensor("b1r", [128, NE * NF], F32, kind="ExternalInput").ap()
    b2r = nc.dram_tensor("b2r", [128, NE * NM], F32, kind="ExternalInput").ap()
    selc = nc.dram_tensor("selc", [NE, NE * 128], F32, kind="ExternalInput").ap()
    ident_in = nc.dram_tensor("ident_in", [128, 128], F32, kind="ExternalInput").ap()
    par_in = nc.dram_tensor("par_in", [1, 1], I32, kind="ExternalInput").ap()
    nonce_in = nc.dram_tensor("nonce_in", [1, 1], I32, kind="ExternalInput").ap()
    WqL = nc.dram_tensor("WqL", [NH0, 128, 1024], F32, kind="ExternalInput").ap()
    WkL = nc.dram_tensor("WkL", [NH0, 128, 1024], F32, kind="ExternalInput").ap()
    WvL = nc.dram_tensor("WvL", [NH0, 128, 1024], F32, kind="ExternalInput").ap()
    WgT = nc.dram_tensor("WgT", [128, NM * 4], F32, kind="ExternalInput").ap()
    WdlT = nc.dram_tensor("WdlT", [128, NM * 48], F32, kind="ExternalInput").ap()
    bg_in = nc.dram_tensor("bg_in", [4, 1], F32, kind="ExternalInput").ap()
    lp_in = nc.dram_tensor("lp_in", [48, 1], F32, kind="ExternalInput").ap()
    mnat_in = nc.dram_tensor("mnat_in", [128, 2304], F32, kind="ExternalInput").ap()
    mst_in = nc.dram_tensor("mst_in", [128, 3 * 128], F32, kind="ExternalInput").ap()
    WoL = nc.dram_tensor("WoL", [NM, 128, NM * 128], F32, kind="ExternalInput").ap()
    lngb = nc.dram_tensor("lngb", [128, 2 * NM], F32, kind="ExternalInput").ap()
    Wq1L = nc.dram_tensor("Wq1L", [4, 128, 1024], F32, kind="ExternalInput").ap()
    Wk1L = nc.dram_tensor("Wk1L", [4, 128, 1024], F32, kind="ExternalInput").ap()
    Wv1L = nc.dram_tensor("Wv1L", [4, 128, 1024], F32, kind="ExternalInput").ap()
    zr_in = nc.dram_tensor("zr_in", [1, T], F32, kind="ExternalInput").ap()
    tri_in = nc.dram_tensor("tri_in", [128, 128], F32, kind="ExternalInput").ap()
    Wo1L = nc.dram_tensor("Wo1L", [NM, 128, NM * 128], F32, kind="ExternalInput").ap()
    wg_in = nc.dram_tensor("wg_in", [NI, D], F32, kind="ExternalInput").ap()
    wu_in = nc.dram_tensor("wu_in", [NI, D], F32, kind="ExternalInput").ap()
    wd_in = nc.dram_tensor("wd_in", [D, NI], F32, kind="ExternalInput").ap()

    out_T = nc.dram_tensor("out_T", [D, TO], F32, kind="ExternalOutput").ap()
    import os
    FDBG = os.environ.get("FDBG", "")
    dbg = nc.dram_tensor("dbg", [D, T], F32, kind="ExternalOutput").ap() if FDBG else None

    xn_d = nc.dram_tensor("xn_d", [D, T], BF16).ap()
    moe_d = nc.dram_tensor("moe_d", [D, TO], F32).ap()
    y0_d = nc.dram_tensor("y0_d", [NH0, 128, T], BF16).ap()
    x1_d = nc.dram_tensor("x1_d", [D, TO], F32).ap()
    xn1_d = nc.dram_tensor("xn1_d", [D, TO], BF16).ap()
    y1_d = nc.dram_tensor("y1_d", [NH1, 64, T], BF16).ap()
    wgq_d = nc.dram_tensor("wgq_d", [NIT, 128, D], BF16).ap()
    wuq_d = nc.dram_tensor("wuq_d", [NIT, 128, D], BF16).ap()
    wdq_d = nc.dram_tensor("wdq_d", [NM, 128, NI], BF16).ap()
    wgn_d = nc.dram_tensor("wgn_d", [NIT, 128, D], BF16).ap()
    wun_d = nc.dram_tensor("wun_d", [NIT, 128, D], BF16).ap()
    wdn_d = nc.dram_tensor("wdn_d", [NM, 128, NI], BF16).ap()
    s1_d = nc.dram_tensor("s1_d", [D, TO], BF16).ap()
    sh_y0 = nc.dram_tensor("sh_y0", [D, T], BF16, addr_space="Shared").ap()
    sh_xn1 = nc.dram_tensor("sh_xn1", [D, T], BF16, addr_space="Shared").ap()
    sh_y1 = nc.dram_tensor("sh_y1", [D, T], BF16, addr_space="Shared").ap()
    sh_flag = nc.dram_tensor("sh_flag", [4, 2], I32, addr_space="Shared").ap()

    ctx = ExitStack()
    with ctx:
        E = ctx.enter_context
        P = Prog(nc, ctx)
        poolA = E(nc.sbuf_tensor("poolA", [128, 16384], F32))
        poolB = E(nc.sbuf_tensor("poolB", [128, 32768], BF16))
        poolC = E(nc.sbuf_tensor("poolC", [128, 16384], BF16))
        poolS = E(nc.sbuf_tensor("poolS", [128, 4608], F32))
        poolW = E(nc.sbuf_tensor("poolW", [128, 11264], BF16))
        ints = E(nc.sbuf_tensor("ints", [128, 512], I32))
        ones_c = E(nc.sbuf_tensor("ones_c", [128, 1], F32))
        ones_r = E(nc.sbuf_tensor("ones_r", [1, 128], F32))
        ones_cb = E(nc.sbuf_tensor("ones_cb", [128, 1], BF16))
        ident_f32 = poolS[:, 1928:2056]
        ident_bf16 = E(nc.sbuf_tensor("ident_bf16", [128, 128], BF16))
        eps_b = E(nc.sbuf_tensor("eps_b", [1, 1], F32))
        par_sb = E(nc.sbuf_tensor("par_sb", [1, 1], I32))
        nonce_sb = E(nc.sbuf_tensor("nonce_sb", [1, 1], I32))
        flag_rd = E(nc.sbuf_tensor("flag_rd", [1, 2], I32))
        lngb_sb = E(nc.sbuf_tensor("lngb_sb", [128, 2 * NM], F32))
        bgs_sb = E(nc.sbuf_tensor("bgs_sb", [4, 1], F32))
        lps_sb = E(nc.sbuf_tensor("lps_sb", [48, 1], F32))
        wT_sb = E(nc.sbuf_tensor("wT_sb", [NE, TO], BF16))
        psF = [E(nc.psum_tensor(f"psF{i}", [128, 512], F32)) for i in range(6)]
        psB = [E(nc.psum_tensor(f"psB{i}", [128, 1024], BF16)) for i in range(2)]
        blk = E(nc.Block())

        P.step("dve", lambda e: e.memset(ones_c[:, :], 1.0))
        P.step("dve", lambda e: e.memset(eps_b[:, :], LN_EPS))
        P.step("dve", lambda e: e.memset(ones_cb[:, :], 1.0))
        t_init = P.step("dve", lambda e: e.memset(ones_r[:, :], 1.0), inc=1)
        for en in ("pe", "act", "gp"):
            P.wait(en, t_init)
        emit_load_parity(P, par_in, par_sb, engines=("gp", "dve"))
        t_nonce = P.dma("par", "gp", nonce_sb[0:1, 0:1], nonce_in)
        for i in range(3):
            P.holder[f"pollsem{i}"] = E(nc.semaphore(f"pollsem{i}"))

        def alloc_regs(e):
            for rn in ("r_cnt", "r_go", "r_f0", "r_f1", "r_nonce"):
                P.holder[rn] = e.alloc_register(rn)
            return e.reg_load(P.holder["r_nonce"], nonce_sb[0:1, 0:1])
        P.wait("gp", t_nonce)
        P.step("gp", alloc_regs)

        tc = [
            P.dma("cns", "sync", poolS[:, 1536:1600], Wr),
            P.dma("cns", "sync", poolS[0:128, 1600:1608], br_bc),
            P.dma("cns", "sync", poolS[:, 1608:1864], b1r),
            P.dma("cns", "sync", poolS[:, 1864:1928], b2r),
            P.dma("cnsg", "gp", poolW[0:NE, 10240:11264], selc),
            P.dma("cns", "sync", ident_f32, ident_in),
            P.dma("cnsg", "gp", ident_bf16[:, :], ident_in),
            P.dma("cns", "sync", lngb_sb[:, :], lngb),
            P.dma("cns", "sync", bgs_sb[:, :], bg_in),
            P.dma("cns", "sync", lps_sb[:, :], lp_in),
        ]
        cons_A = dict(ones_c=ones_c, ones_r=ones_r, ident_f32=ident_f32,
                      wr32_sb=poolS[:, 1536:1600], br_sb=poolS[0:128, 1600:1608],
                      b1_sb=poolS[:, 1608:1864], b2r_sb=poolS[:, 1864:1928],
                      sel_sb=poolW[0:NE, 10240:11264], psF=psF, psB=psB,
                      eps_b=eps_b)

        ln_bufs = dict(
            xbuf=[[poolA[:, (p * NM + m) * CHK:(p * NM + m + 1) * CHK]
                   for m in range(NM)] for p in range(2)],
            xsq=[poolS[:, 0:512], poolS[:, 512:1024]],
            rows=[poolA[0:1, 9216 + i * 512: 9216 + (i + 1) * 512] for i in range(5)],
            xnc=[poolB[:, m * CHK:(m + 1) * CHK] for m in range(NM)],
            tmp=[poolA[:, 8192 + (i % 2) * CHK: 8192 + (i % 2 + 1) * CHK]
                 for i in range(4)],
            rstd_all=poolA[0:1, 11776:13824],
            mur_all=poolA[0:1, 13824:15872],
        )
        ln_stores = emit_ln(P, x0T, xn_d, ln_bufs, psF, cons_A)
        P.barrier_all()

        xn_own = [poolC[:, m * TO:(m + 1) * TO] for m in range(NM)]
        t_own = emit_read_own(P, xn_d, xn_own)
        for en in ("pe", "dve", "act"):
            P.wait(en, t_own)
        rt_bufs = dict(
            logit=poolS[:, 0:8], m8=poolS[:, 8:16], p1=poolS[:, 16:17],
            p2=poolS[:, 17:18], eq1=poolS[:, 24:32], eq2=poolS[:, 32:40],
            wt=poolS[:, 40:48],
            xf=[poolA[:, m * CHK:(m + 1) * CHK] for m in range(NM)],
            xnf=[poolA[:, 4096 + m * CHK: 4096 + (m + 1) * CHK] for m in range(NM)],
        )
        for f16 in range(16):
            t_w1pre = P.dma("w1r", "gp",
                            poolB[:, f16 * 1024:(f16 + 1) * 1024],
                            W1L[0, f16, :, :])
        P.waits("pe", tc)
        emit_router_f32(P, x0T, wT_sb, ln_bufs["rstd_all"], ln_bufs["mur_all"],
                        rt_bufs, psF, cons_A)
        P.barrier_all()

        y_acc = [poolA[:, m * TO:(m + 1) * TO] for m in range(NM)]
        moe_bufs = dict(
            w1r=poolB[:, 0:16384],
            w2b=[poolW[:, p * 2048:(p + 1) * 2048] for p in range(2)],
            hT=[poolB[:, 16384 + f * 1024: 16384 + (f + 1) * 1024]
                for f in range(16)],
            wb_sb=poolS[:, 0:1024],
            ytmp=poolS[:, 1024:1536],
            y_acc_flat=poolA[:, :],
            W2c_d=W2c_d,
        )
        wq_jobs = []
        for rt in range(NIT):
            for s2 in range(2):
                wq_jobs.append((wg_in[rt * 128:(rt + 1) * 128,
                                      s2 * 512:(s2 + 1) * 512],
                                wgn_d[rt, :, s2 * 512:(s2 + 1) * 512], 512))
        for rt in range(NIT):
            for s2 in range(2):
                wq_jobs.append((wu_in[rt * 128:(rt + 1) * 128,
                                      s2 * 512:(s2 + 1) * 512],
                                wun_d[rt, :, s2 * 512:(s2 + 1) * 512], 512))
        wd_jobs = []
        for rt in range(NM):
            for o in range(0, NI, 512):
                w2 = min(512, NI - o)
                wd_jobs.append((wd_in[rt * 128:(rt + 1) * 128, o:o + w2],
                                wdn_d[rt, :, o:o + w2], w2))
        S3 = dict(flip=0, t_st=[None, None], pending=None,
                  y2=[poolS[:, 2056:2568], poolS[:, 2568:3080]],
                  scr=dict(f1=poolS[:, 3080:3592], f2=poolS[:, 3592:4104],
                           bm=poolS[:, 4104:4136], t1=poolS[:, 4136:4168],
                           r=poolS[:, 4168:4200], ints=ints[:, :]))

        def emit_wq_job(src_ap, dst_ap, w2):
            p = S3["flip"]
            S3["flip"] ^= 1
            t_ld = P.dma("wraw", "sync", S3["y2"][p][:, 0:w2], src_ap,
                         deps=[S3["t_st"][p]])
            scr = dict(S3["scr"])
            scr["y"] = S3["y2"][p]
            t_q = emit_quant(P, S3["y2"][p][:, 0:w2], S3["y2"][p][:, 0:w2],
                             w2, scr, deps=[t_ld])
            if S3["pending"] is not None:
                ps_, pd_, pw_, pt_, pp_ = S3["pending"]
                P.dma("wqn", "gp", pd_, ps_[:, 0:pw_], deps=[pt_])
                S3["t_st"][pp_] = P.dtok("wqn")
            S3["pending"] = (S3["y2"][p], dst_ap, w2, t_q, p)

        alljobs = wq_jobs + wd_jobs

        def moe_cb(i):
            n0 = (len(alljobs) * i) // 32
            n1 = (len(alljobs) * (i + 1)) // 32
            for j in range(n0, n1):
                emit_wq_job(*alljobs[j])

        moe_bufs["w1_pre_tok"] = t_w1pre
        t_moe = emit_moe(P, xn_own, wT_sb, y_acc, W1L, W2L, moe_bufs, psF, cons_A,
                         idle_cb=moe_cb)
        if S3["pending"] is not None:
            ps_, pd_, pw_, pt_, pp_ = S3["pending"]
            P.dma("wqn", "gp", pd_, ps_[:, 0:pw_], deps=[pt_])
        st_moe = [P.dma("moest", "sync", moe_d[m * 128:(m + 1) * 128, :], y_acc[m],
                        deps=[t_moe]) for m in range(NM)]
        P.barrier_all()

        cons_B = dict(ones_r=ones_r, ident_f32=ident_f32, ident_bf16=ident_bf16,
                      wg_sb=poolW[:, 5632:5664], wdl_sb=poolW[:, 5664:6048],
                      bgs_sb=bgs_sb, lps_sb=lps_sb,
                      mnat_sb=poolC[:, 12288:14592], mst_sb=poolS[:, 256:640],
                      WqL_ap=[WqL[i, :, :] for i in range(NH0)],
                      WkL_ap=[WkL[i, :, :] for i in range(NH0)],
                      WvL_ap=[WvL[i, :, :] for i in range(NH0)])
        tcb = [
            P.dma("cnsg", "gp", cons_B["wg_sb"], WgT),
            P.dma("cnsg", "gp", cons_B["wdl_sb"], WdlT),
            P.dma("cnsg2", "gp", cons_B["mnat_sb"], mnat_in),
            P.dma("cns2", "sync", cons_B["mst_sb"], mst_in),
        ]
        xnB = [poolB[:, m * T:(m + 1) * T] for m in range(NM)]
        t_xnB = [P.dma("xnf", "sync", xnB[m], xn_d[m * 128:(m + 1) * 128, :])
                 for m in range(NM)]
        for en in ("pe", "act", "dve", "gp"):
            P.waits(en, [t_xnB[-1], tcb[1], P.dtok("cns2"), P.dtok("cnsg2")])
        bufs_B = dict(
            attn_rows=poolA[:, 0:4096],
            lam_nat=poolA[:, 4096:5632],
            rowpool=poolA[64:65, 11648:15360],
            Rsb=poolA[:, 5632:8704],
            exin=poolA[:, 8704:9216],
            expt=poolA[:, 9216:9728],
            exin_b=poolA[:, 9728:10240],
            expt_b=poolA[:, 10240:10752],
            exin_c=poolA[:, 15360:15872],
            expt_c=poolA[:, 15872:16384],
            natx=poolA[:, 10752:11008],
            nate=poolA[:, 11008:11264],
            natl=poolA[:, 11264:11520],
            dmx=poolA[:, 11520:11648],
            sp_scr=[poolA[:, 11648:12160][0:48, :], poolA[:, 12160:12672][0:48, :]],
            spg_scr=poolA[96:100, 12672:13184],
            SPcol=poolS[:, 0:128],
            negSPcol=poolS[:, 128:256],
            QT=poolC[:, 0:4096],
            KT=poolC[:, 4096:8192],
            Vnat=poolC[:, 8192:12288],
            vtmp=poolW[:, 2048:2560],
            A_sb=[poolW[:, 0:512], poolW[:, 512:1024], poolW[:, 2048:2560]],
            ytile=poolW[:, 1024:1536],
            natA=poolW[:, 1536:1792],
            wq=poolW[:, 2560:3584],
            wk=poolW[:, 3584:4608],
            wv=poolW[:, 4608:5632],
            stateT32=poolS[:, 4200:4328],
            gcol=poolS[:, 4328:4332],
            mccol=poolS[:, 4332:4333],
            dmcol=poolS[:, 4333:4334],
            frow=poolS[0:1, 640:1152],
            spbnd=poolS[64:65, 1152:1153],
            tmp11=poolS[64:65, 1153:1154],
            ones96=poolS[64:65, 4334:4462],
            stateT=poolW[:, 8096:8224],
            knat=poolW[:, 8224:8736],
            vsc=poolW[:, 8736:9248],
            qsc=poolW[:, 9248:9760],
            natAT=[poolW[:, 9760:10016], poolW[:, 1792:2048]],
        )
        emit_l0_prep(P, xnB, bufs_B, psF, cons_B)
        P.barrier_all()
        ST4 = dict(t_tr=None, t_evac=None, t_store=None,
                   buf=poolW[:, 6048:7072], evac=poolW[:, 7072:8096],
                   psB0=psB[0][:, :], ident=ident_bf16)
        rows_t4 = [(wgn_d[rt, :, :], wgq_d[rt, :, :], 1024)
                   for rt in range(NIT)] + \
                  [(wun_d[rt, :, :], wuq_d[rt, :, :], 1024)
                   for rt in range(NIT)]
        def go_noop(e, sem):
            return None

        for h in range(NH0):
            tk = emit_l0_head(P, h, xnB, y0_d, bufs_B, psF, psB, cons_B)
            bufs_B["head_dep"] = tk

            def go_y0h(e, sem, h=h):
                par = P.holder["par"]
                with e.If_eq(par, 0):
                    e.dma_start(out=sh_y0[h * 128:(h + 1) * 128, :],
                                in_=y0_d[h, :, :]).then_inc(sem, 16)
                with e.Else():
                    e.dma_start(out=sh_y0[(NH0 + h) * 128:(NH0 + h + 1) * 128, :],
                                in_=y0_d[h, :, :]).then_inc(sem, 16)
            P.waits("gp", [P.dtok("yst")])
            P.raw("y0cp", "gp", go_y0h, inc=16)
            emit_wtrans_rows(P, rows_t4[h * 11:(h + 1) * 11], ST4)
        P.barrier_all()

        P.wait("gp", P.dtok("y0cp"))
        t_ex0 = emit_exchange(P, nc, 0, go_noop, 0, sh_flag, nonce_sb, flag_rd)
        for en in ENGINES:
            P.wait(en, t_ex0)

        yT = [poolB[:, ht * TO:(ht + 1) * TO] for ht in range(NM)]

        def go_yrd(e, sem):
            par = P.holder["par"]
            with e.If_eq(par, 0):
                for ht in range(NM):
                    e.dma_start(out=yT[ht], in_=sh_y0[ht * 128:(ht + 1) * 128, 0:TO]
                                ).then_inc(sem, 16)
            with e.Else():
                for ht in range(NM):
                    e.dma_start(out=yT[ht], in_=sh_y0[ht * 128:(ht + 1) * 128, TO:T]
                                ).then_inc(sem, 16)
        t_y = P.raw("yld", "gp", go_yrd, inc=16 * NM)
        xn_ownC = [poolC[:, m * TO:(m + 1) * TO] for m in range(NM)]
        t_xnC = emit_read_own(P, xn_d, xn_ownC)
        s_pre = [poolA[:, m * TO:(m + 1) * TO] for m in range(NM)]
        s_sb = [poolB[:, 16384 + m * TO: 16384 + (m + 1) * TO] for m in range(NM)]
        rows_off = [1024, 1536, 2048, 2560, 1536]
        lnC_bufs = dict(
            xsq=[poolS[:, 0:512], poolS[:, 512:1024]],
            rows=[poolS[0:1, o:o + 512] for o in rows_off],
        )
        g_sbC = poolS[:, 4096:4608]
        woT = [poolB[:, 24576 + ht * 1024: 24576 + (ht + 1) * 1024]
               for ht in range(NM)]
        t_wo = [P.dma("wold", "gp", woT[ht], WoL[ht, :, :]) for ht in range(NM)]
        cons_C = dict(ones_c=ones_c, ones_r=ones_r, eps_b=eps_b)
        NCH = TO // CHK
        t_spre = None
        for c in range(NCH):
            cs = slice(c * CHK, (c + 1) * CHK)
            for mt in range(NM):
                P.waits("pe", [t_y, t_wo[-1], t_spre])
                for ht in range(NM):
                    P.step("pe", lambda e, ht=ht, mt=mt, cs=cs: e.matmul(
                        psF[0][:, :], woT[ht][:, mt * 128:(mt + 1) * 128],
                        yT[ht][:, cs], start=(ht == 0), stop=(ht == NM - 1)),
                        inc=(1 if ht == NM - 1 else 0))
                t_mm = P.tok("pe")
                P.waits("dve", [t_mm, t_xnC])
                t_spre = P.step("dve", lambda e, mt=mt, cs=cs: e.tensor_tensor(
                    out=s_pre[mt][:, cs], in0=psF[0][:, :], in1=xn_ownC[mt][:, cs],
                    op=ALU.add), inc=1)
        P.barrier_all()
        for c in range(NCH):
            cs = slice(c * CHK, (c + 1) * CHK)

            def write_s(m, ps_b1, ps_b2, cs=cs):
                P.step("dve", lambda e, m=m: e.tensor_mul(
                    out=lnC_bufs["xsq"][m % 2], in0=s_pre[m][:, cs], in1=ps_b1))
                P.step("dve", lambda e, m=m: e.tensor_sub(
                    out=lnC_bufs["xsq"][m % 2], in0=lnC_bufs["xsq"][m % 2],
                    in1=ps_b2))
                return P.step("dve", lambda e, m=m: e.tensor_scalar(
                    out=s_sb[m][:, cs], in0=lnC_bufs["xsq"][m % 2],
                    scalar1=lngb_sb[:, m:m + 1],
                    scalar2=lngb_sb[:, NM + m:NM + m + 1],
                    op0=ALU.mult, op1=ALU.add), inc=1)
            src = [s_pre[m][:, cs] for m in range(NM)]
            t_ln2 = emit_ln_stats_apply(P, src, CHK, lnC_bufs, psF, cons_C, write_s)
            P.barrier_all()
            x1c = [poolA[:, m * TO + c * CHK: m * TO + (c + 1) * CHK]
                   for m in range(NM)]
            t_g = None
            for m in range(NM):
                t_moe2 = P.dma("moeld", "sync", poolS[:, 3584:4096],
                               moe_d[m * 128:(m + 1) * 128, cs], deps=[t_g])

                def go_x0(e, sem, m=m, cs=cs, c=c):
                    par = P.holder["par"]
                    with e.If_eq(par, 0):
                        e.dma_start(out=poolS[:, 3072:3584],
                                    in_=x0T[m * 128:(m + 1) * 128, cs]
                                    ).then_inc(sem, 16)
                    with e.Else():
                        e.dma_start(out=poolS[:, 3072:3584],
                                    in_=x0T[m * 128:(m + 1) * 128,
                                            TO + c * CHK:TO + (c + 1) * CHK]
                                    ).then_inc(sem, 16)
                t_x0 = P.raw("x0ld", "gp", go_x0, inc=16)
                P.waits("act", [t_moe2])
                t_g = P.step("act", lambda e: e.activation(
                    out=g_sbC, in_=poolS[:, 3584:4096], func=AF.Silu), inc=1)
                P.waits("dve", [t_g, t_ln2, t_x0])
                P.step("dve", lambda e, m=m, cs=cs: e.tensor_mul(
                    out=g_sbC, in0=g_sbC, in1=s_sb[m][:, cs]))
                t_x1 = P.step("dve", lambda e, m=m, x1c=x1c: e.tensor_tensor(
                    out=x1c[m], in0=poolS[:, 3072:3584], in1=g_sbC, op=ALU.add),
                    inc=1)
                P.dma("x1st", "sync", x1_d[m * 128:(m + 1) * 128, cs], x1c[m],
                      deps=[t_x1])
                P.waits("gp", [t_x1])
                P.waits("sync", [t_x1])
            P.barrier_all()
            xn1c = [poolC[:, m * TO + c * CHK: m * TO + (c + 1) * CHK]
                    for m in range(NM)]

            def write_xn1(m, ps_b1, ps_b2, cs=cs, x1c=x1c, xn1c=xn1c):
                P.step("dve", lambda e, m=m, x1c=x1c: e.tensor_mul(
                    out=lnC_bufs["xsq"][m % 2], in0=x1c[m], in1=ps_b1))
                t = P.step("dve", lambda e, m=m, xn1c=xn1c: e.tensor_sub(
                    out=xn1c[m], in0=lnC_bufs["xsq"][m % 2], in1=ps_b2), inc=1)
                P.dma("xn1st", "sync", xn1_d[m * 128:(m + 1) * 128, cs],
                      xn1c[m], deps=[t])
                return t
            emit_ln_stats_apply(P, x1c, CHK, lnC_bufs, psF, cons_C, write_xn1)

            def go_xn1c(e, sem, cs=cs, c=c):
                par = P.holder["par"]
                with e.If_eq(par, 0):
                    e.dma_start(out=sh_xn1[:, c * CHK:(c + 1) * CHK],
                                in_=xn1_d[:, cs]).then_inc(sem, 16)
                with e.Else():
                    e.dma_start(out=sh_xn1[:, TO + c * CHK:TO + (c + 1) * CHK],
                                in_=xn1_d[:, cs]).then_inc(sem, 16)
            P.waits("gp", [P.dtok("xn1st")])
            P.raw("xn1cp", "gp", go_xn1c, inc=16)
            P.barrier_all()
        P.wait("gp", P.dtok("x1st"))
        P.wait("gp", P.dtok("xn1cp"))
        P.wait("gp", st_moe[-1])

        t_ex1 = emit_exchange(P, nc, 1, go_noop, 0, sh_flag, nonce_sb, flag_rd)
        for en in ENGINES:
            P.wait(en, t_ex1)

        cons_D = dict(ones_r=ones_r, ones_cb=ones_cb, ident_bf16=ident_bf16,
                      tri_sb=poolW[:, 8864:8992], zr_sb=poolA[0:1, 5120:9216],
                      Wq1L_ap=[Wq1L[i, :, :] for i in range(4)],
                      Wk1L_ap=[Wk1L[i, :, :] for i in range(4)],
                      Wv1L_ap=[Wv1L[i, :, :] for i in range(4)])
        tcd = [
            P.dma("cns3", "sync", cons_D["zr_sb"], zr_in),
            P.dma("cnsg2", "gp", cons_D["tri_sb"], tri_in),
        ]
        xnD = [poolB[:, m * T:(m + 1) * T] for m in range(NM)]
        t_xnD = [P.dma("xn1f", "sync", xnD[m], sh_xn1[m * 128:(m + 1) * 128, :])
                 for m in range(NM)]
        for en in ("pe", "act", "dve", "gp"):
            P.waits(en, [t_xnD[-1], tcd[0], P.dtok("cnsg2")])
        bufs_D = dict(
            QT=poolC[:, 0:4096], KT=poolC[:, 4096:8192], VT=poolC[:, 8192:12288],
            Vn=poolC[:, 12288:14368],
            pc=poolA[:, 0:4096],
            of32=poolA[0:64, 4096:4608],
            zf=poolA[0:1, 4608:5120],
            vh=poolW[:, 0:4096],
            A_sb=[poolW[:, 4096:4608], poolW[:, 4608:5120],
                  poolW[:, 8992:9504], poolW[:, 9504:10016]],
            y1t=poolW[0:64, 5120:5632],
            wq=poolW[:, 5792:6816],
            wk=poolW[:, 6816:7840],
            wv=poolW[:, 7840:8864],
            wslots=[[poolW[:, 5792:6816], poolW[:, 6816:7840],
                     poolW[:, 7840:8864]],
                    [poolW[:, 0:1024], poolW[:, 1024:2048],
                     poolW[:, 2048:3072]]],
        )
        ST8 = dict(t_tr=None, t_evac=None, t_store=None,
                   buf=poolW[:, 10016:11040], evac=poolC[:, 14368:15392],
                   psB0=psB[0][:, :], ident=ident_bf16)
        rows_t8 = [(wdn_d[rt, :, :], wdq_d[rt, :, :], NI) for rt in range(NM)]
        for hp in range(4):
            tk = emit_l1_pair(P, hp, xnD, y1_d, bufs_D, psF, psB, cons_D)
            bufs_D["head_dep"] = tk

            def go_y1p(e, sem, hp=hp):
                par = P.holder["par"]
                src = y1_d[hp * 2:(hp + 1) * 2, :, :].rearrange(
                    "a b t -> (a b) t")
                with e.If_eq(par, 0):
                    e.dma_start(out=sh_y1[hp * 128:(hp + 1) * 128, :],
                                in_=src).then_inc(sem, 16)
                with e.Else():
                    e.dma_start(out=sh_y1[512 + hp * 128:512 + (hp + 1) * 128, :],
                                in_=src).then_inc(sem, 16)
            P.waits("gp", [P.dtok("yst")])
            P.raw("y1cp", "gp", go_y1p, inc=16)
            emit_wtrans_rows(P, rows_t8[hp * 2:(hp + 1) * 2], ST8)
        P.barrier_all()

        P.wait("gp", P.dtok("y1cp"))
        t_ex2 = emit_exchange(P, nc, 2, go_noop, 0, sh_flag, nonce_sb, flag_rd)
        for en in ENGINES:
            P.wait(en, t_ex2)

        scr = dict(y=poolA[:, 0:2816], f1=poolA[:, 2816:5632],
                   f2=poolA[:, 5632:8448], ints=ints[:, :],
                   bm=poolA[0:128, 8448:8624], t1=poolA[:, 8624:8800],
                   r=poolA[:, 8800:8976])
        qnat = poolW[:, 0:2816]
        y1T = [poolB[:, ht * TO:(ht + 1) * TO] for ht in range(NM)]

        def go_y1rd(e, sem):
            par = P.holder["par"]
            with e.If_eq(par, 0):
                for ht in range(NM):
                    e.dma_start(out=y1T[ht], in_=sh_y1[ht * 128:(ht + 1) * 128, 0:TO]
                                ).then_inc(sem, 16)
            with e.Else():
                for ht in range(NM):
                    e.dma_start(out=y1T[ht], in_=sh_y1[ht * 128:(ht + 1) * 128, TO:T]
                                ).then_inc(sem, 16)
        t_y1rd = P.raw("y1ld", "gp", go_y1rd, inc=16 * NM)
        wo1T = [poolB[:, 16384 + ht * 1024: 16384 + (ht + 1) * 1024]
                for ht in range(NM)]
        t_wo1 = [P.dma("wo1ld", "gp", wo1T[ht], Wo1L[ht, :, :]) for ht in range(NM)]
        s1t = poolW[:, 4224:4736]
        t_s1 = None
        for c in range(TO // CHK):
            cs = slice(c * CHK, (c + 1) * CHK)
            for mt in range(NM):
                P.waits("pe", [t_y1rd, t_wo1[-1], t_s1])
                for ht in range(NM):
                    P.step("pe", lambda e, ht=ht, mt=mt, cs=cs: e.matmul(
                        psF[0][:, :], wo1T[ht][:, mt * 128:(mt + 1) * 128],
                        y1T[ht][:, cs], start=(ht == 0), stop=(ht == NM - 1)),
                        inc=(1 if ht == NM - 1 else 0))
                t_mm = P.tok("pe")
                P.waits("act", [t_mm])
                t_s1 = P.step("act", lambda e: e.copy(out=s1t, in_=psF[0][:, :]),
                              inc=1)
                P.dma("s1st", "sync", s1_d[mt * 128:(mt + 1) * 128, cs], s1t,
                      deps=[t_s1])
                t_s1 = P.dtok("s1st")
        P.barrier_all()
        xq = [poolC[:, m * TO:(m + 1) * TO] for m in range(NM)]
        t_x = [P.dma("xnld", "sync", xq[m], xn1_d[m * 128:(m + 1) * 128, :])
               for m in range(NM)]
        xnat = poolW[:, 2816:3840]
        PS0 = dict(r=[None, None], n=0)
        xst = dict(qfree=None, natfree=None)
        xq_ready = [None] * 16

        def ps_bank():
            bk = PS0["n"] % 2
            PS0["n"] += 1
            return bk

        def xquant_tile(tt):
            bk = ps_bank()
            P.waits("pe", [t_x[-1], PS0["r"][bk]])
            for m in range(NM):
                P.step("pe", lambda e, m=m, tt=tt, bk=bk: e.transpose(
                    out=psB[bk][:, m * 128:(m + 1) * 128],
                    in_=xq[m][:, tt * 128:(tt + 1) * 128],
                    identity=ident_bf16[:, :]), inc=(1 if m == NM - 1 else 0))
            t_tr = P.tok("pe")
            P.waits("act", [t_tr, xst["natfree"]])
            t_natv = P.step("act", lambda e, bk=bk: e.copy(
                out=xnat[:, 0:1024], in_=psB[bk][:, :]), inc=1)
            PS0["r"][bk] = t_natv
            t_q2 = emit_quant(P, xnat[:, 0:1024], qnat[:, 0:1024], 1024, scr,
                              deps=[t_natv, xst["qfree"]])
            xst["natfree"] = t_q2
            bk = ps_bank()
            P.waits("pe", [t_q2, PS0["r"][bk]])
            for m in range(NM):
                P.step("pe", lambda e, m=m, bk=bk: e.transpose(
                    out=psB[bk][:, m * 128:(m + 1) * 128],
                    in_=qnat[:, m * 128:(m + 1) * 128],
                    identity=ident_bf16[:, :]), inc=(1 if m == NM - 1 else 0))
            t_tr2 = P.tok("pe")
            xst["qfree"] = t_tr2
            P.waits("act", [t_tr2])
            for m in range(NM):
                P.step("act", lambda e, m=m, tt=tt, bk=bk: e.copy(
                    out=xq[m][:, tt * 128:(tt + 1) * 128],
                    in_=psB[bk][:, m * 128:(m + 1) * 128]),
                    inc=(1 if m == NM - 1 else 0))
            t_bk = P.tok("act")
            PS0["r"][bk] = t_bk
            xq_ready[tt] = t_bk

        for tt in range(4):
            xquant_tile(tt)
        P.barrier_all()
        hT2 = [[poolB[:, hb * 11264 + it * CHK: hb * 11264 + (it + 1) * CHK]
                for it in range(NIT)] for hb in range(2)]
        hreg = [poolB[:, 0:11264], poolB[:, 11264:22528]]
        hnat = poolB[:, 22528:25344]
        wdbuf = [poolB[:, 25344 + p * NI: 25344 + (p + 1) * NI] for p in range(2)]
        sg2 = [poolA[:, 12032:12544], poolA[:, 13568:14080]]
        x1cE = poolA[:, 12544:13056]
        g1c = poolA[:, 13056:13568]
        s1ct = poolW[:, 4224:4736]
        ps_g = [psF[0][:, :], psF[2][:, :]]
        ps_u = [psF[1][:, :], psF[3][:, :]]
        ps_dn = psF[4][:, :]
        t_wg_cons = [None, None]
        t_wu_cons = [None, None]
        t_wd_cons = [None, None]
        t_psg_free = [None, None]
        t_psu_free = [None, None]
        t_sg_cons = [None, None]
        t_dn_free = None
        t_out_last = None
        t_out_prev_dve = None
        t_h_cons = [None, None]
        t_hq_done = [None, None]

        def emit_gateup(c):
            hb = c % 2
            H = hT2[hb]
            cs = slice(c * CHK, (c + 1) * CHK)
            nxt = [4 + c * 4 + j for j in range(4)] if c < 3 else []
            for it in range(NIT):
                pb = it % 2
                t_lg = P.dma(f"wg{pb}", "gp", wgb[pb], wgq_d[it, :, :],
                             deps=[t_wg_cons[pb]])
                t_lu = P.dma(f"wu{pb}", "gp", wub[pb], wuq_d[it, :, :],
                             deps=[t_wu_cons[pb]])
                P.waits("pe", [t_lg, t_psg_free[pb], t_h_cons[hb] if it == 0 else None,
                               xq_ready[c * 4 + 3] if it == 0 else None])
                for m in range(NM):
                    P.step("pe", lambda e, m=m, pb=pb, cs=cs: e.matmul(
                        ps_g[pb], wgb[pb][:, m * 128:(m + 1) * 128],
                        xq[m][:, cs], start=(m == 0), stop=(m == NM - 1)),
                        inc=(1 if m == NM - 1 else 0))
                t_g2 = P.tok("pe")
                t_wg_cons[pb] = t_g2
                P.waits("pe", [t_lu, t_psu_free[pb]])
                for m in range(NM):
                    P.step("pe", lambda e, m=m, pb=pb, cs=cs: e.matmul(
                        ps_u[pb], wub[pb][:, m * 128:(m + 1) * 128],
                        xq[m][:, cs], start=(m == 0), stop=(m == NM - 1)),
                        inc=(1 if m == NM - 1 else 0))
                t_u = P.tok("pe")
                t_wu_cons[pb] = t_u
                P.waits("act", [t_g2, t_sg_cons[pb]])
                t_sg = P.step("act", lambda e, pb=pb: e.activation(
                    out=sg2[pb], in_=ps_g[pb], func=AF.Silu), inc=1)
                t_psg_free[pb] = t_sg
                P.waits("dve", [t_sg, t_u, t_h_cons[hb] if it == 0 else None])
                t_h = P.step("dve", lambda e, it=it, pb=pb, H=H: e.tensor_tensor(
                    out=H[it], in0=sg2[pb], in1=ps_u[pb], op=ALU.mult), inc=1)
                t_sg_cons[pb] = t_h
                t_psu_free[pb] = t_h
                if nxt and it in (4, 9, 14, 19):
                    xquant_tile(nxt.pop(0))
            return P.tok("dve")

        def emit_hquant(c, t_hmults):
            hb = c % 2
            H = hT2[hb]
            hr3 = hreg[hb].rearrange("p (i c) -> p i c", c=CHK)
            t_qh = None
            for tt in range(4):
                for base in range(0, NIT, 8):
                    nf = min(8, NIT - base)
                    bk = ps_bank()
                    P.waits("pe", [t_hmults, PS0["r"][bk]])
                    for j in range(nf):
                        P.step("pe", lambda e, j=j, base=base, tt=tt, H=H, bk=bk:
                               e.transpose(
                            out=psB[bk][:, j * 128:(j + 1) * 128],
                            in_=H[base + j][:, tt * 128:(tt + 1) * 128],
                            identity=ident_bf16[:, :]),
                            inc=(1 if j == nf - 1 else 0))
                    t_tr = P.tok("pe")
                    P.waits("act", [t_tr, xst["natfree"] if base == 0 else None])
                    t_hn = P.step("act", lambda e, base=base, nf=nf, bk=bk:
                                  e.copy(
                        out=hnat[:, base * 128:(base + nf) * 128],
                        in_=psB[bk][:, 0:nf * 128]), inc=1)
                    PS0["r"][bk] = t_hn
                t_qh = emit_quant(P, hnat[:, :], qnat[:, 0:2816], 2816, scr,
                                  deps=[t_hn, xst["qfree"]])
                xst["natfree"] = t_qh
                for base in range(0, NIT, 8):
                    nf = min(8, NIT - base)
                    bk = ps_bank()
                    P.waits("pe", [t_qh, PS0["r"][bk]])
                    for j in range(nf):
                        P.step("pe", lambda e, j=j, base=base, bk=bk: e.transpose(
                            out=psB[bk][:, j * 128:(j + 1) * 128],
                            in_=qnat[:, (base + j) * 128:(base + j + 1) * 128],
                            identity=ident_bf16[:, :]),
                            inc=(1 if j == nf - 1 else 0))
                    t_tr2 = P.tok("pe")
                    if base + nf == NIT:
                        xst["qfree"] = t_tr2
                    P.waits("dve", [t_tr2])
                    t_bk = P.step("dve", lambda e, base=base, nf=nf, tt=tt,
                                  hr3=hr3, bk=bk: e.tensor_copy(
                        out=hr3[:, base:base + nf, tt * 128:(tt + 1) * 128],
                        in_=psB[bk][:, 0:nf * 128].rearrange(
                            "p (i c) -> p i c", c=128)),
                        inc=1)
                    PS0["r"][bk] = t_bk
            t_hq_done[hb] = P.tok("dve")

        def emit_down(c):
            nonlocal t_dn_free, t_out_last, t_out_prev_dve
            hb = c % 2
            H = hT2[hb]
            cs = slice(c * CHK, (c + 1) * CHK)
            t_last_mm = None
            for mt in range(NM):
                pb = mt % 2
                t_ldw = P.dma(f"wd{pb}", "gp", wdbuf[pb], wdq_d[mt, :, :],
                              deps=[t_wd_cons[pb]])
                P.waits("pe", [t_ldw, t_hq_done[hb] if mt == 0 else None,
                               t_dn_free])
                for it in range(NIT):
                    P.step("pe", lambda e, it=it, pb=pb, H=H: e.matmul(
                        ps_dn, wdbuf[pb][:, it * 128:(it + 1) * 128],
                        H[it], start=(it == 0), stop=(it == NIT - 1)),
                        inc=(1 if it == NIT - 1 else 0))
                t_dn = P.tok("pe")
                t_last_mm = t_dn
                t_wd_cons[pb] = t_dn
                P.waits("act", [t_dn, t_out_prev_dve])
                t_g1 = P.step("act", lambda e: e.activation(
                    out=g1c, in_=ps_dn, func=AF.Silu), inc=1)
                t_dn_free = t_g1

                def go_x1(e, sem, mt=mt, cs=cs):
                    return e.dma_start(out=x1cE,
                                       in_=x1_d[mt * 128:(mt + 1) * 128, cs]
                                       ).then_inc(sem, 16)
                P.waits("sync", [t_out_last])
                t_x1r = P.raw("x1ldE", "sync", go_x1, inc=16)
                t_s1r = P.dma("s1rd", "sync", s1ct,
                              s1_d[mt * 128:(mt + 1) * 128, cs],
                              deps=[t_out_last])
                P.waits("dve", [t_g1, t_x1r, t_s1r])
                P.step("dve", lambda e: e.tensor_tensor(
                    out=g1c, in0=g1c, in1=s1ct, op=ALU.mult))
                t_x2 = P.step("dve", lambda e: e.tensor_tensor(
                    out=x1cE, in0=x1cE, in1=g1c, op=ALU.add), inc=1)
                t_out_prev_dve = t_x2
                t_out_last = P.dma("outst", "sync",
                                   out_T[mt * 128:(mt + 1) * 128, cs], x1cE,
                                   deps=[t_x2])
            t_h_cons[hb] = t_last_mm

        wgb = [poolW[:, 4736 + p * 1024: 4736 + (p + 1) * 1024] for p in range(2)]
        wub = [poolW[:, 6784 + p * 1024: 6784 + (p + 1) * 1024] for p in range(2)]
        for c in range(5):
            t_hm = None
            if c < 4:
                t_hm = emit_gateup(c)
            if c >= 1:
                emit_down(c - 1)
            if c < 4:
                emit_hquant(c, t_hm)

        P.wait("sync", P.dtok("outst"))
        if FDBG:
            P.barrier_all()
            srcs = {"xn": (xn_d, T, BF16), "moe": (moe_d, TO, F32),
                    "x1": (x1_d, TO, F32), "xn1": (xn1_d, TO, BF16),
                    "shy0": (sh_y0, T, BF16), "shxn1": (sh_xn1, T, BF16),
                    "shy1": (sh_y1, T, BF16), "s1": (s1_d, TO, BF16)}
            sap, w, dt = srcs[FDBG]
            for m in range(NM):
                tdl = P.dma("dbgl", "gp", poolA[:, 0:w], sap[m * 128:(m + 1) * 128, 0:w])
                P.dma("dbgs", "sync", dbg[m * 128:(m + 1) * 128, 0:w], poolA[:, 0:w],
                      deps=[tdl])
                P.wait("gp", P.dtok("dbgs"))
            P.wait("sync", P.dtok("dbgs"))
        P.finalize(blk)
    return nc


def prep_fused_inputs(inp):
    maps_A = prep_A_inputs(inp)
    maps_B = prep_B_inputs(inp)
    base_C = prep_C_inputs(inp)
    maps_D = prep_D_inputs(inp)
    base_E = prep_E_inputs(inp)
    nonce = np.array([[np.random.randint(1, 2**31 - 1)]], np.int32)
    in_maps = []
    for c in range(8):
        m = {}
        m.update(maps_A[c])
        m.update(maps_B[c])
        m.update({k: v for k, v in base_C.items() if k != "ident_in"})
        m.update(maps_D[c])
        m.update({k: v for k, v in base_E.items() if k != "ident_in"})
        m["nonce_in"] = nonce
        in_maps.append(m)
    return in_maps


def kernel_fused(**inputs):
    inp = {k: np.asarray(v) for k, v in inputs.items()}
    in_maps = prep_fused_inputs(inp)
    res = run_bass_kernel_spmd(build_fused(), in_maps, core_ids=list(range(8)))
    out = np.empty((B, T, D), np.float32)
    for c in range(8):
        b, par = c // 2, c % 2
        out[b, par * TO:(par + 1) * TO, :] = np.asarray(res.results[c]["out_T"]).T
    return out
